# version 06
# baseline (speedup 1.0000x reference)
import numpy as np
import ml_dtypes
import concourse.bass as bass
import concourse.mybir as mybir
from concourse.bass_utils import run_bass_kernel_spmd
from concourse.expressions import RuntimeValue

F32 = mybir.dt.float32
BF16 = mybir.dt.bfloat16
I32 = mybir.dt.int32
AF = mybir.ActivationFunctionType
ALU = mybir.AluOpType
ds = bass.ds

D = 1024
T = 8192
IC = 8
QT = 512
NQT = T // QT
NCH = T // 128
DH = 4096
OCH = DH // 128
NLN8 = -2.0794415416798357

STAGE = 4


class Ctx:

    def __init__(self, nc):
        self.nc = nc
        self.eng = {"T": nc.tensor, "A": nc.scalar, "V": nc.vector,
                    "G": nc.gpsimd, "Y": nc.sync}
        self.sem = {k: nc.alloc_semaphore("sm" + k) for k in self.eng}
        self.pool = [nc.alloc_semaphore(f"sp{i}") for i in range(56)]
        self.cnt = {k: 0 for k in self.eng}
        self.water = {}
        nums = [s.num for s in list(self.sem.values()) + self.pool]
        assert max(nums) - min(nums) + 1 == len(nums), "sem range not contiguous"
        nc.gpsimd.sem_clear(range(min(nums), max(nums) + 1))
        nc.all_engine_barrier()

    def dsem(self, name):
        if name not in self.sem:
            self.sem[name] = self.pool.pop(0)
            self.cnt[name] = 0
        return self.sem[name]

    def wait(self, who, tok):
        if tok is None:
            return
        name, val = tok
        if name == who or val <= 0:
            return
        if self.water.get((who, name), 0) >= val:
            return
        self.water[(who, name)] = val
        self.eng[who].wait_ge(self.sem[name], val)

    def mm2(self, nc, *args, **kw):
        nc.tensor.matmul(*args, **kw)
        return self.inc("T", nc.tensor.matmul(*args, **kw))

    def v2(self, emit):
        emit()
        return self.inc("V", emit())

    def inc(self, who, ins):
        self.cnt[who] += 1
        ins.then_inc(self.sem[who], 1)
        return (who, self.cnt[who])

    def dma(self, who, name, dst, src, deps=()):
        for d in deps:
            self.wait(who, d)
        s = self.dsem(name)
        self.eng[who].dma_start(dst, src).then_inc(s, 16)
        self.cnt[name] = self.cnt.get(name, 0) + 16
        return (name, self.cnt[name])

    def join(self, engines=("T", "A", "V", "G", "Y")):
        toks = [(k, self.cnt[k]) for k in engines]
        for who in engines:
            for t in toks:
                self.wait(who, t)


class Arena:
    def __init__(self, nc, nbytes):
        self.ap = nc.alloc_sbuf_tensor("arena", [128, nbytes], mybir.dt.int8).ap()
        self.off = 0
        self.nbytes = nbytes

    def alloc(self, cols, dtype):
        esz = mybir.dt.size(dtype)
        n = cols * esz
        off = (self.off + 31) // 32 * 32
        assert off + n <= self.nbytes, f"arena overflow: {off + n} > {self.nbytes}"
        self.off = off + n
        return self.ap[:, off:off + n].bitcast(dtype)

    def mark(self):
        return self.off

    def reset(self, mark):
        self.off = mark

    def at(self, off, cols, dtype):
        n = cols * mybir.dt.size(dtype)
        assert off % 32 == 0 and off + n <= self.nbytes
        return self.ap[:, off:off + n].bitcast(dtype)


def build_nc(debug=()):
    nc = bass.Bass("TRN2", target_bir_lowering=False)
    c = Ctx(nc)

    def din(name, shape, dt=F32):
        return nc.dram_tensor(name, shape, dt, kind="ExternalInput").ap()

    d_xt = din("xtp", [128, IC * T])
    d_xo = din("xtop", [128, IC * 2048])
    d_wq = din("wqp", [128, IC * 256])
    d_wk = din("wkp", [128, IC * 256])
    d_wv = din("wvp", [128, IC * 256])
    d_bqk = din("bqkp", [128, 4])
    d_bv = din("bvp", [1, 256])
    d_al = din("alp", [1, 4])
    d_wo = din("wop", [128, IC * D])
    d_bo = din("bop", [128, 8])
    d_w1 = din("w1p", [OCH, 128, IC * 128])
    d_w2 = din("w2p", [OCH, 128, IC * 128])
    d_h1 = din("h1p", [OCH, 128, OCH * 128])
    d_h2 = din("h2p", [OCH, 128, OCH * 128])
    d_w3 = din("w3p", [8, 128, OCH * 128])
    d_s1 = din("s1p", [128, OCH])
    d_s2 = din("s2p", [128, OCH])
    d_s3 = din("s3p", [128, OCH])
    d_s4 = din("s4p", [128, OCH])
    d_s5 = din("s5p", [128, 8])
    d_tq = din("tqp", [128, NQT * NCH])
    d_d0 = din("d0p", [128, 4 * QT])
    d_dq = din("dqp", [1, QT])
    d_dqb = din("dqbp", [128, QT])
    d_on = din("onp", [1, 128])
    d_ey = din("eyp", [2, 512])
    d_hg = din("hgp", [1, 1], I32)

    d_cat = nc.dram_tensor("attn_cat", [256, T], BF16, kind="Internal")
    d_all1 = nc.dram_tensor("attn_all1", [512, T], BF16, kind="Internal")
    d_all2 = nc.dram_tensor("attn_all2", [512, T], BF16, kind="Internal")
    d_out = nc.dram_tensor("outp", [128, 8 * 2048], F32, kind="ExternalOutput").ap()

    dbg_outs = {}
    for name, shape, dt in debug:
        dbg_outs[name] = nc.dram_tensor(name, shape, dt, kind="ExternalOutput").ap()

    def sb(name, shape, dt):
        return nc.alloc_sbuf_tensor(name, shape, dt).ap()

    s_tq = sb("s_tq", [128, NQT * NCH], F32)
    s_d0 = sb("s_d0", [128, 4 * QT], F32)
    s_dq = sb("s_dq", [1, QT], F32)
    s_dqb = sb("s_dqb", [128, QT], F32)
    s_onf = sb("s_onf", [1, 128], F32)
    s_onb = sb("s_onb", [1, 128], BF16)
    s_ey = sb("s_ey", [2, 512], F32)
    s_bqk = sb("s_bqk", [128, 4], F32)
    s_bv = sb("s_bv", [1, 256], F32)
    s_al = sb("s_al", [1, 4], F32)
    s_hgi = sb("s_hgi", [1, 1], I32)
    s_bo = sb("s_bo", [128, 8], F32)
    s_s1 = sb("s_s1", [128, OCH], F32)
    s_s2 = sb("s_s2", [128, OCH], F32)
    s_s3 = sb("s_s3", [128, OCH], F32)
    s_s4 = sb("s_s4", [128, OCH], F32)
    s_s5 = sb("s_s5", [128, 8], F32)
    s_ar = sb("s_ar", [1, 512], F32)
    s_nar = sb("s_nar", [1, 4], F32)
    s_m511 = sb("s_m511", [1, 4], F32)
    s_ab = sb("s_ab", [128, 4], F32)
    s_n8ab = sb("s_n8ab", [128, 4], F32)
    s_m511b = sb("s_m511b", [128, 4], F32)
    s_ur = sb("s_ur", [1, 512], F32)
    s_nub = sb("s_nub", [128, 4], F32)
    s_k2r = sb("s_k2r", [1, 4], F32)
    s_q2r = sb("s_q2r", [1, 4], F32)
    s_k2mP = [sb("s_k2m0", [2, 1], F32), sb("s_k2m1", [2, 1], F32)]
    s_q2mP = [sb("s_q2m0", [2, 1], F32), sb("s_q2m1", [2, 1], F32)]
    s_ind2 = sb("s_ind2", [128, 2], BF16)
    s_onc = sb("s_onc", [128, 1], F32)
    s_eps = sb("s_eps", [1, 1], F32)
    s_one = sb("s_one", [1, 1], F32)
    s_nl8 = sb("s_nl8", [1, 1], F32)
    s_t14a = sb("s_t14a", [1, 4], F32)
    s_t14b = sb("s_t14b", [1, 4], F32)
    s_bsl = [sb(f"s_bsl{h}", [128, NCH], F32) for h in range(2)]

    pb = [nc.alloc_psum_tensor(f"pb{i}", [128, 512], F32).ap() for i in range(8)]

    A = Arena(nc, 182 * 1024)
    last = {}

    for dst, src in [(s_tq, d_tq), (s_d0, d_d0), (s_dq, d_dq), (s_dqb, d_dqb), (s_onf, d_on),
                     (s_ey, d_ey), (s_bqk, d_bqk), (s_bv, d_bv), (s_al, d_al),
                     (s_hgi, d_hg), (s_bo, d_bo), (s_s1, d_s1), (s_s2, d_s2),
                     (s_s3, d_s3), (s_s4, d_s4), (s_s5, d_s5)]:
        c.dma("Y", "cst", dst, src)
    tok_cst = ("cst", c.cnt["cst"])
    tok_cstg = c.dma("G", "cstg", s_onb, d_on)

    c.wait("V", tok_cst)
    c.wait("V", tok_cstg)
    nc.vector.memset(s_onc, 1.0)
    nc.vector.memset(s_eps, 1e-5)
    nc.vector.memset(s_one, 1.0)
    nc.vector.memset(s_nl8, NLN8)
    nc.vector.memset(s_ind2, 0.0)
    nc.vector.memset(s_ind2[0:64, 0:1], 1.0)
    nc.vector.memset(s_ind2[64:128, 1:2], 1.0)
    nc.vector.memset(s_ur, 0.0)
    nc.vector.memset(s_ar, 0.0)
    nc.vector.memset(s_k2mP[0], 0.0)
    nc.vector.memset(s_k2mP[1], 0.0)
    nc.vector.memset(s_q2mP[0], 0.0)
    tv0 = c.inc("V", nc.vector.memset(s_q2mP[1], 0.0))

    c.wait("A", tok_cst)
    c.wait("A", tv0)
    nc.scalar.activation(s_t14a, s_al, AF.Abs)
    nc.scalar.activation(s_t14a, s_t14a, AF.Exp, scale=-1.0)
    nc.scalar.activation(s_t14a, s_t14a, AF.Ln, bias=s_one)
    ta0 = c.inc("A", nc.scalar.activation(s_t14b, s_al, AF.Relu))
    c.wait("V", ta0)
    nc.vector.tensor_add(s_ar[0:1, 0:4], s_t14a, s_t14b)
    nc.vector.tensor_scalar_mul(s_nar, s_ar[0:1, 0:4], -1.0)
    tv1 = c.inc("V", nc.vector.tensor_scalar_mul(s_m511, s_ar[0:1, 0:4], -511.0))
    c.wait("T", tv1)
    tmm0 = c.mm2(nc, pb[0], s_onf, s_ar[0:1, :], start=True, stop=True)
    c.wait("V", tmm0)
    nc.vector.tensor_copy(s_ab, pb[0][:, 0:4])
    nc.vector.tensor_copy(s_ab, pb[0][:, 0:4])
    nc.vector.tensor_scalar_mul(s_n8ab, s_ab, -8.0)
    nc.vector.tensor_scalar_mul(s_n8ab, s_ab, -8.0)
    nc.vector.tensor_scalar_mul(s_m511b, s_ab, -511.0)
    tv2 = c.inc("V", nc.vector.tensor_scalar_mul(s_m511b, s_ab, -511.0))
    last[("pb", 0)] = tv2

    c.wait("G", tok_cst)
    rhg = nc.gpsimd.alloc_register("rhg")
    nc.gpsimd.reg_load(rhg, s_hgi)
    vhg = RuntimeValue(rhg, min_val=0, max_val=3)

    pairst = {}

    def proj_all():
        c.join()
        s_KT = [A.alloc(T, BF16) for _ in range(2)]
        s_QT = [A.alloc(T, BF16) for _ in range(2)]
        s_V = [[A.alloc(NCH * 65, BF16) for _ in range(2)] for _ in range(2)]
        pairst["KT"] = s_KT
        pairst["QT"] = s_QT
        pairst["V"] = s_V
        m1 = A.mark()
        s_xt = [A.alloc(IC * 512, BF16) for _ in range(2)]
        s_wqb = A.alloc(IC * 256, BF16)
        s_wkb = A.alloc(IC * 256, BF16)
        s_wvb = A.alloc(IC * 256, BF16)
        s_wv260 = A.alloc(IC * 260, BF16)
        s_bv260 = A.alloc(260, BF16)
        s_sq = A.alloc(512, BF16)
        s_sq2 = A.alloc(512, BF16)
        s_r21 = A.alloc(1, F32)
        s_r22 = A.alloc(1, F32)

        twq = c.dma("G", "wq0", s_wqb.rearrange("p (a o) -> p a o", o=256),
                    d_wq.rearrange("p (a o) -> p a o", o=256))
        twk = c.dma("G", "wk0", s_wkb.rearrange("p (a o) -> p a o", o=256),
                    d_wk.rearrange("p (a o) -> p a o", o=256))
        twv = c.dma("G", "wv0", s_wvb.rearrange("p (a o) -> p a o", o=256),
                    d_wv.rearrange("p (a o) -> p a o", o=256))
        c.wait("V", twv)
        c.wait("V", tok_cst)
        nc.vector.memset(s_wv260, 0.0)
        for ic in range(IC):
            for hl in range(4):
                nc.vector.tensor_copy(
                    s_wv260[:, 260 * ic + 65 * hl:260 * ic + 65 * hl + 64],
                    s_wvb[:, 256 * ic + 64 * hl:256 * ic + 64 * hl + 64])
        nc.vector.memset(s_bv260[0:1, 0:260], 0.0)
        for hl in range(4):
            nc.vector.tensor_copy(s_bv260[0:1, 65 * hl:65 * hl + 64],
                                  s_bv[0:1, 64 * hl:64 * hl + 64])
            nc.vector.memset(s_bv260[0:1, 65 * hl + 64:65 * hl + 65], 1.0)
        nc.vector.memset(s_bv260[0:1, 259:260], 1.0)
        tv260 = c.inc("V", nc.vector.memset(s_bv260[0:1, 259:260], 1.0))

        tev = {}
        tvv = None
        for tc in range(16):
            sl = tc % 2
            dep = last.get(("xt", sl))
            tx = c.dma("G", f"x{sl}", s_xt[sl].rearrange("p (a t) -> p a t", t=512),
                       d_xt.rearrange("p (a t) -> p a t", t=T)
                       [:, :, 512 * tc:512 * tc + 512],
                       deps=[dep] if dep else [])
            c.wait("T", tx)
            c.wait("T", twk)
            c.wait("T", twq)
            c.wait("T", tv260)
            for p2 in range(2):
                bK = pb[p2]
                bQ = pb[2 + p2]
                c.wait("T", tev.get(("K", p2)))
                for ic in range(IC):
                    ins = nc.tensor.matmul(
                        bK, s_wkb[:, 256 * ic + 128 * p2:256 * ic + 128 * p2 + 128],
                        s_xt[sl][:, 512 * ic:512 * ic + 512],
                        start=(ic == 0), stop=(ic == IC - 1))
                tk = c.inc("T", ins)
                c.wait("T", tev.get(("Q", p2)))
                for ic in range(IC):
                    ins = nc.tensor.matmul(
                        bQ, s_wqb[:, 256 * ic + 128 * p2:256 * ic + 128 * p2 + 128],
                        s_xt[sl][:, 512 * ic:512 * ic + 512],
                        start=(ic == 0), stop=(ic == IC - 1))
                tq_ = c.inc("T", ins)
                c.wait("V", tk)
                tev[("K", p2)] = c.inc("V", nc.vector.tensor_scalar(
                    s_KT[p2][:, 512 * tc:512 * tc + 512], bK,
                    s_bqk[:, 2 + p2:3 + p2], None, ALU.add))
                c.wait("V", tq_)
                tev[("Q", p2)] = c.inc("V", nc.vector.tensor_scalar(
                    s_QT[p2][:, 512 * tc:512 * tc + 512], bQ,
                    s_bqk[:, p2:p2 + 1], None, ALU.add))
            for ts in range(4):
                bV = pb[4 + ts % 2]
                kc = 4 * tc + ts
                c.wait("T", tev.get(("V", ts % 2)))
                for ic in range(IC):
                    nc.tensor.matmul(
                        bV[:, 0:260],
                        s_xt[sl][:, 512 * ic + 128 * ts:512 * ic + 128 * ts + 128],
                        s_wv260[:, 260 * ic:260 * ic + 260],
                        start=(ic == 0), stop=False)
                tvv = c.inc("T", nc.tensor.matmul(
                    bV[:, 0:260], s_onb, s_bv260[0:1, 0:260],
                    start=False, stop=True))
                c.wait("V", tvv)
                for p2 in range(2):
                    for h in range(2):
                        ins = nc.vector.tensor_copy(
                            s_V[p2][h][:, 65 * kc:65 * kc + 65],
                            bV[:, 130 * p2 + 65 * h:130 * p2 + 65 * h + 65])
                tev[("V", ts % 2)] = c.inc("V", ins)
            last[("xt", sl)] = tvv

        for key in (("K", 0), ("K", 1), ("Q", 0), ("Q", 1)):
            c.wait("A", tev[key])
        stat_src = []
        for p2 in range(2):
            stat_src += [(s_KT[p2], s_k2mP[p2], kc2) for kc2 in range(16)]
            stat_src += [(s_QT[p2], s_q2mP[p2], kc2) for kc2 in range(16)]
        sqs = [s_sq, s_sq2]
        NST = len(stat_src)
        tvm = None
        tsq_l = {}

        def stat_sq(i):
            srcb, _, kc2 = stat_src[i]
            dep = last.get(("stsq", i % 2))
            if dep:
                c.wait("A", dep)
            return c.inc("A", nc.scalar.activation(
                sqs[i % 2], srcb[:, 512 * kc2:512 * kc2 + 512], AF.Square))

        tsq_l[0] = stat_sq(0)
        for i in range(NST):
            bnk = pb[6] if i % 2 == 0 else pb[7]
            c.wait("T", tsq_l[i])
            c.wait("T", tv0)
            dep = last.get(("pbst6", i % 2))
            if dep:
                c.wait("T", dep)
            tm = c.mm2(nc, bnk[0:2, :], s_ind2, sqs[i % 2],
                       start=True, stop=True)
            last[("stsq", i % 2)] = tm
            if i + 1 < NST:
                tsq_l[i + 1] = stat_sq(i + 1)
            dstm = stat_src[i][1]
            c.wait("V", tm)
            if tvm is not None:
                c.wait("V", tvm)
            nc.vector.reduce_max(s_r21[0:2, 0:1], bnk[0:2, :],
                                 axis=mybir.AxisListType.X)
            nc.vector.reduce_max(s_r21[0:2, 0:1], bnk[0:2, :],
                                 axis=mybir.AxisListType.X)
            nc.vector.tensor_max(s_r22[0:2, 0:1], dstm, s_r21[0:2, 0:1])
            nc.vector.tensor_max(s_r22[0:2, 0:1], dstm, s_r21[0:2, 0:1])
            nc.vector.tensor_copy(dstm, s_r22[0:2, 0:1])
            tvm = c.inc("V", nc.vector.tensor_copy(dstm, s_r22[0:2, 0:1]))
            last[("pbst6", i % 2)] = tvm
        last[("pb", 6)] = tvm
        c.wait("T", tvm)

        for p2 in range(2):
            dep = last.get(("pb", 6))
            if dep:
                c.wait("T", dep)
            tk2 = c.mm2(nc, pb[6][0:1, 0:512], s_k2mP[p2], s_ey,
                        start=True, stop=True)
            c.wait("V", tk2)
            tvk = c.v2(lambda p2=p2: nc.vector.tensor_copy(
                s_k2r[0:1, 2 * p2:2 * p2 + 2], pb[6][0:1, 0:2]))
            c.wait("T", tvk)
            tq2 = c.mm2(nc, pb[6][0:1, 0:512], s_q2mP[p2], s_ey,
                        start=True, stop=True)
            c.wait("V", tq2)
            nc.vector.tensor_copy(s_q2r[0:1, 2 * p2:2 * p2 + 2], pb[6][0:1, 0:2])
            nc.vector.tensor_copy(s_q2r[0:1, 2 * p2:2 * p2 + 2], pb[6][0:1, 0:2])
            nc.vector.tensor_mul(
                s_t14a[0:1, 2 * p2:2 * p2 + 2], s_k2r[0:1, 2 * p2:2 * p2 + 2],
                s_q2r[0:1, 2 * p2:2 * p2 + 2])
            tvq = c.inc("V", nc.vector.tensor_mul(
                s_t14a[0:1, 2 * p2:2 * p2 + 2], s_k2r[0:1, 2 * p2:2 * p2 + 2],
                s_q2r[0:1, 2 * p2:2 * p2 + 2]))
            last[("pb", 6)] = tvq
            c.wait("A", tvq)
            nc.scalar.activation(s_t14b[0:1, 2 * p2:2 * p2 + 2],
                                 s_t14a[0:1, 2 * p2:2 * p2 + 2], AF.Ln)
            tua = c.inc("A", nc.scalar.activation(
                s_ur[0:1, 2 * p2:2 * p2 + 2], s_t14b[0:1, 2 * p2:2 * p2 + 2],
                AF.Exp, scale=0.5, bias=s_nl8))
            c.wait("T", tua)
            c.wait("T", tvq)
            tub = c.mm2(nc, pb[6], s_onf, s_ur[0:1, :], start=True, stop=True)
            c.wait("V", tub)
            tnu = c.v2(lambda p2=p2: nc.vector.tensor_scalar(
                s_nub[:, 2 * p2:2 * p2 + 2], pb[6][:, 2 * p2:2 * p2 + 2],
                -1.0, None, ALU.mult))
            last[("pb", 6)] = tnu
            pairst[("tnu", p2)] = tnu
        A.reset(m1)

    def attention_pair(p):
        c.join()
        mark = A.mark()
        s_KT = pairst["KT"][p]
        s_QT = pairst["QT"][p]
        s_V = pairst["V"][p]
        tnu = pairst[("tnu", p)]
        s_ds = [[A.alloc(512, F32) for _ in range(2)] for _ in range(2)]
        s_P = [[A.alloc(512, BF16) for _ in range(14)] for _ in range(2)]
        s_dh = [A.alloc(512, F32) for _ in range(2)]
        s_xc = [A.alloc(512, F32) for _ in range(2)]
        s_x2 = [A.alloc(512, F32) for _ in range(2)]
        s_ao = [A.alloc(512, BF16) for _ in range(2)]
        s_G = [A.alloc(QT, F32) for _ in range(2)]
        s_H = [A.alloc(QT, F32) for _ in range(2)]
        s_lin = [A.alloc(QT, F32) for _ in range(2)]

        tgh = None
        for h in range(2):
            hl = 2 * p + h
            c.wait("V", tv2)
            tgt = c.inc("V", nc.vector.tensor_scalar(
                s_x2[h], s_dqb, s_ab[:, hl:hl + 1], None, ALU.mult))
            c.wait("A", tgt)
            nc.scalar.activation(s_G[h], s_x2[h], AF.Exp, scale=-1.0)
            tgh = c.inc("A", nc.scalar.activation(
                s_H[h], s_x2[h], AF.Exp, scale=1.0,
                bias=s_m511b[:, hl:hl + 1]))

        if STAGE < 2:
            A.reset(mark)
            return

        bS = [[pb[0], pb[1]], [pb[2], pb[3]]]
        bL = [pb[4], pb[5]]
        bR = [pb[6], pb[7]]
        c.join()

        NP = 14
        AVLAG = 8
        EPIPOS = 7
        RUNWAY = 6
        pend = None

        def emit_epi_V1(st):
            qt0 = st["qt"]
            hasL, hasR = qt0 > 0, qt0 < NQT - 1
            tvcs = []
            for h in range(2):
                c.wait("V", st["tAV"])
                c.wait("V", tgh)
                if hasL:
                    nc.vector.tensor_mul(s_xc[h][0:65, :], bL[h][0:65, :],
                                         s_G[h][0:65, :])
                    if hasR:
                        nc.vector.tensor_mul(s_x2[h][0:65, :], bR[h][0:65, :],
                                             s_H[h][0:65, :])
                        nc.vector.tensor_add(s_xc[h][0:65, :], s_xc[h][0:65, :],
                                             s_x2[h][0:65, :])
                else:
                    nc.vector.tensor_mul(s_xc[h][0:65, :], bR[h][0:65, :],
                                         s_H[h][0:65, :])
                nc.vector.tensor_add(s_xc[h][0:65, :], s_xc[h][0:65, :],
                                     s_dh[h][0:65, :])
                nc.vector.reciprocal(s_lin[h][0:1, :], s_xc[h][64:65, :])
                tvcs.append(c.inc("V", nc.vector.reciprocal(
                    s_lin[h][0:1, :], s_xc[h][64:65, :])))
            st["tvcs"] = tvcs
            last[("accR", 0)] = tvcs[-1]
            last[("accR", 1)] = tvcs[-1]

        def emit_epi_T(st):
            tbcs = []
            for h in range(2):
                c.wait("T", st["tvcs"][h])
                tbcs.append(c.mm2(nc, bL[h][0:64, :], s_onf[0:1, 0:64],
                                  s_lin[h][0:1, :], start=True, stop=True))
            st["tbcs"] = tbcs

        def emit_epi_V2(st):
            qt0 = st["qt"]
            aos = s_ao[qt0 % 2]
            tEp = None
            for h in range(2):
                c.wait("V", st["tbcs"][h])
                dep = last.get(("aost", qt0 % 2))
                if dep:
                    c.wait("V", dep)
                tEp = c.v2(lambda h=h, aos=aos: nc.vector.tensor_mul(
                    aos[64 * h:64 * h + 64, :], s_xc[h][0:64, :],
                    bL[h][0:64, :]))
                last[("accL", h)] = tEp
            tdma = c.dma("Y", f"cat{qt0 % 2}",
                         d_cat.ap()[128 * p:128 * p + 128,
                                    512 * qt0:512 * qt0 + 512],
                         aos, deps=[tEp])
            last[("aost", qt0 % 2)] = tdma

        for qt in range(NQT):
            tb = []
            for h in range(2):
                hl = 2 * p + h
                dep = last.get(("bsl", h))
                if dep:
                    c.wait("V", dep)
                c.wait("V", tnu)
                c.wait("V", tv2)
                tb.append(c.inc("V", nc.vector.tensor_scalar(
                    s_bsl[h], s_tq[:, 64 * qt:64 * qt + 64],
                    s_ab[:, hl:hl + 1], s_nub[:, hl:hl + 1], ALU.mult, ALU.add)))
            if pend is not None:
                emit_epi_V1(pend)

            ndiag0 = 4 * qt
            Lc = [("L", cc, None) for cc in range(0, ndiag0)]
            Rc = [("R", cc, None) for cc in range(ndiag0 + 4, NCH)]
            Dc = [("D", ndiag0 + j, j) for j in range(4)]
            if Lc:
                borrow = "R"
                chunks = Lc[:RUNWAY] + Dc + Lc[RUNWAY:] + Rc
            else:
                borrow = "L"
                chunks = Rc[:RUNWAY] + Dc + Rc[RUNWAY:]
            nch = len(chunks)
            bDA = bR if borrow == "R" else bL
            lastof = {}
            firstof = {}
            for i, (kind, cc, j) in enumerate(chunks):
                lastof[kind] = i
                firstof.setdefault(kind, i)

            def emit_S(i):
                kind, cc, j = chunks[i]
                buf = i % 2
                for h in range(2):
                    dep = last.get(("expS", h, buf))
                    if dep:
                        c.wait("T", dep)
                    ins = nc.tensor.matmul(
                        bS[h][buf],
                        s_KT[64 * h:64 * h + 64, 128 * cc:128 * cc + 128],
                        s_QT[64 * h:64 * h + 64, 512 * qt:512 * qt + 512],
                        start=True, stop=True, tile_position=(64 * h, 0))
                return c.inc("T", ins)

            def emit_mid(i, tS):
                kind, cc, j = chunks[i]
                buf = i % 2
                pslot = i % NP
                if kind == "D":
                    c.wait("V", tS)
                    for h in range(2):
                        hl = 2 * p + h
                        dep = last.get(("expD", h, j % 2))
                        if dep:
                            c.wait("V", dep)
                        ins = nc.vector.scalar_tensor_tensor(
                            s_ds[h][j % 2], s_d0[:, 512 * j:512 * j + 512],
                            s_n8ab[:, hl:hl + 1], bS[h][buf], ALU.mult, ALU.add)
                    tD = c.inc("V", ins)
                    last[("expS", 0, buf)] = tD
                    last[("expS", 1, buf)] = tD
                    c.wait("A", tD)
                    for h in range(2):
                        hl = 2 * p + h
                        dep = last.get(("avP", h, pslot))
                        if dep:
                            c.wait("A", dep)
                        ins = nc.scalar.activation(
                            s_P[h][pslot], s_ds[h][j % 2], AF.Exp, scale=0.125,
                            bias=s_nub[:, hl:hl + 1])
                    tE = c.inc("A", ins)
                    last[("expD", 0, j % 2)] = tE
                    last[("expD", 1, j % 2)] = tE
                else:
                    c.wait("A", tS)
                    c.wait("A", tb[0])
                    c.wait("A", tb[1])
                    for h in range(2):
                        dep = last.get(("avP", h, pslot))
                        if dep:
                            c.wait("A", dep)
                        ins = nc.scalar.activation(
                            s_P[h][pslot], bS[h][buf], AF.Exp, scale=0.125,
                            bias=s_bsl[h][:, cc:cc + 1])
                    tE = c.inc("A", ins)
                    last[("expS", 0, buf)] = tE
                    last[("expS", 1, buf)] = tE
                return tE

            def emit_AV(i, tE):
                kind, cc, j = chunks[i]
                pslot = i % NP
                first = (i == firstof[kind])
                lastk = (i == lastof[kind])
                bACC = bDA if kind == "D" else (bL if kind == "L" else bR)
                c.wait("T", tE)
                for h in range(2):
                    if first:
                        key = ("acc" + borrow, h) if kind == "D" else ("acc" + kind, h)
                        dep = last.get(key)
                        if dep:
                            c.wait("T", dep)
                        if kind == "R" and borrow == "R":
                            dep = last.get(("accDevac", h))
                            if dep:
                                c.wait("T", dep)
                        if kind == "L" and borrow == "L":
                            dep = last.get(("accDevac", h))
                            if dep:
                                c.wait("T", dep)
                    ins = nc.tensor.matmul(
                        bACC[h][0:65, :], s_V[h][:, 65 * cc:65 * cc + 65],
                        s_P[h][pslot], start=first, stop=lastk)
                tAV = c.inc("T", ins)
                last[("avP", 0, pslot)] = tAV
                last[("avP", 1, pslot)] = tAV
                if kind == "D" and lastk:
                    c.wait("V", tAV)
                    for h in range(2):
                        nc.vector.tensor_copy(s_dh[h][0:65, :], bDA[h][0:65, :])
                        ins = nc.vector.tensor_copy(s_dh[h][0:65, :],
                                                    bDA[h][0:65, :])
                    tDh = c.inc("V", ins)
                    last[("accDevac", 0)] = tDh
                    last[("accDevac", 1)] = tDh
                return tAV

            tS_l = {0: emit_S(0)}
            tE_l = {}
            tAV = None
            avdone = 0
            epi_T_done = pend is None
            for i in range(nch):
                tE_l[i] = emit_mid(i, tS_l[i])
                if i + 1 < nch:
                    tS_l[i + 1] = emit_S(i + 1)
                if not epi_T_done and i >= EPIPOS:
                    emit_epi_T(pend)
                    emit_epi_V2(pend)
                    epi_T_done = True
                navs = 0 if i < AVLAG else (2 if i >= nch - AVLAG else 1)
                for _ in range(navs):
                    if avdone <= i:
                        tAV = emit_AV(avdone, tE_l[avdone])
                        avdone += 1
            if not epi_T_done:
                emit_epi_T(pend)
                emit_epi_V2(pend)
            while avdone < nch:
                tAV = emit_AV(avdone, tE_l[avdone])
                avdone += 1
            last[("bsl", 0)] = ("A", c.cnt["A"])
            last[("bsl", 1)] = ("A", c.cnt["A"])
            pend = {"qt": qt, "tAV": tAV}
        emit_epi_V1(pend)
        emit_epi_T(pend)
        emit_epi_V2(pend)
        pend = None
        A.reset(mark)

    proj_all()
    attention_pair(0)
    tok_cc1 = None
    if STAGE >= 3:
        c.wait("G", ("cat0", c.cnt.get("cat0", 0)))
        c.wait("G", ("cat1", c.cnt.get("cat1", 0)))
        ccsem = c.dsem("cc")
        nc.gpsimd.collective_compute(
            "AllGather", ALU.bypass,
            replica_groups=[[0, 1, 2, 3], [4, 5, 6, 7]],
            ins=[d_cat.ap()[0:128, :].opt()], outs=[d_all1.ap().opt()],
        ).then_inc(ccsem)
        c.cnt["cc"] = c.cnt.get("cc", 0) + 1
        tok_cc1 = ("cc", c.cnt["cc"])
    attention_pair(1)

    tok_cc = None
    if STAGE >= 3:
        c.wait("G", ("cat0", c.cnt.get("cat0", 0)))
        c.wait("G", ("cat1", c.cnt.get("cat1", 0)))
        ccsem = c.dsem("cc")
        nc.gpsimd.collective_compute(
            "AllGather", ALU.bypass,
            replica_groups=[[0, 1, 2, 3], [4, 5, 6, 7]],
            ins=[d_cat.ap()[128:256, :].opt()], outs=[d_all2.ap().opt()],
        ).then_inc(ccsem)
        c.cnt["cc"] = c.cnt.get("cc", 0) + 1
        tok_cc = ("cc", c.cnt["cc"])
    if STAGE >= 4:
        c.join()
        A.reset(0)
        markB = A.mark()
        s_wo = A.alloc(IC * D, BF16)
        s_aoq = [A.alloc(IC * 512, BF16) for _ in range(2)]
        s_xoq = [A.alloc(512, F32) for _ in range(2)]
        s_u = A.alloc(IC * 512, F32)
        s_h = A.alloc(OCH * 512, BF16)
        s_h2 = A.alloc(OCH * 512, BF16)
        s_hws = [A.alloc(OCH * 128, BF16) for _ in range(3)]
        s_sqB = [A.alloc(512, F32) for _ in range(2)]
        s_mr = A.alloc(QT, F32)[0:1, :]
        s_er = A.alloc(QT, F32)[0:1, :]
        s_vr = A.alloc(QT, F32)[0:1, :]
        s_vr2 = A.alloc(QT, F32)[0:1, :]
        s_vr3 = A.alloc(QT, F32)[0:1, :]
        s_arw = A.alloc(QT, F32)[0:1, :]
        s_brw = A.alloc(QT, F32)[0:1, :]
        s_brw2 = A.alloc(QT, F32)[0:1, :]
        u2_off = (A.mark() + 31) // 32 * 32
        s_x1b = A.alloc(IC * 512, BF16)
        s_w1s = [A.alloc(IC * 128, BF16) for _ in range(3)]
        s_w2s = [A.alloc(IC * 128, BF16) for _ in range(3)]
        s_u2 = A.at(u2_off, IC * 512, F32)

        two = c.dma("G", "wo", s_wo, d_wo)
        pf = {}
        for oc in range(3):
            pf[("w1s", oc)] = c.dma("G", f"w1s{oc % 3}", s_w1s[oc % 3], d_w1[oc])
            pf[("w2s", oc)] = c.dma("G", f"w2s{oc % 3}", s_w2s[oc % 3], d_w2[oc])
            pf[("hws", oc)] = c.dma("G", f"hws{oc % 3}", s_hws[oc % 3], d_h1[oc])

        def ln_rows(bM1, bM2, tstats):
            c.wait("V", tstats)
            nc.vector.tensor_scalar(s_mr, bM1[0:1, :], 1.0 / D, None, ALU.mult)
            nc.vector.tensor_scalar(s_mr, bM1[0:1, :], 1.0 / D, None, ALU.mult)
            nc.vector.tensor_scalar(s_er, bM2[0:1, :], 1.0 / D, None, ALU.mult)
            nc.vector.tensor_scalar(s_er, bM2[0:1, :], 1.0 / D, None, ALU.mult)
            nc.vector.tensor_mul(s_vr2, s_mr, s_mr)
            nc.vector.tensor_mul(s_vr2, s_mr, s_mr)
            nc.vector.tensor_sub(s_vr, s_er, s_vr2)
            tvv_ = c.inc("V", nc.vector.tensor_sub(s_vr, s_er, s_vr2))
            c.wait("A", tvv_)
            nc.scalar.activation(s_vr3, s_vr, AF.Ln, bias=s_eps)
            tra = c.inc("A", nc.scalar.activation(s_arw, s_vr3, AF.Exp, scale=-0.5))
            c.wait("V", tra)
            nc.vector.tensor_mul(s_brw, s_mr, s_arw)
            nc.vector.tensor_scalar_mul(s_brw2, s_brw, -1.0)
            return c.inc("V", nc.vector.tensor_scalar_mul(s_brw2, s_brw, -1.0))

        for g in range(4):
            asl = g % 2
            depao = last.get(("aoq", asl))
            aodeps = [depao] if depao else []
            if tok_cc1 is not None and g == 0:
                aodeps += [tok_cc1, tok_cc]
            c.dma("G", f"aoq{asl}",
                  s_aoq[asl].rearrange("p (a t) -> p a t", t=512)[:, 0::2, :],
                  d_all1.ap().rearrange("(a p) t -> p a t", p=128)
                  [:, :, ds((vhg + 4 * g) * 512, 512)],
                  deps=aodeps)
            tao = c.dma("G", f"aoq{asl}",
                        s_aoq[asl].rearrange("p (a t) -> p a t", t=512)[:, 1::2, :],
                        d_all2.ap().rearrange("(a p) t -> p a t", p=128)
                        [:, :, ds((vhg + 4 * g) * 512, 512)],
                        deps=aodeps)
            tu = None
            for oc in range(IC):
                sl = oc % 2
                dep = last.get(("xoq", sl))
                txo = c.dma("Y", f"xoq{sl}", s_xoq[sl],
                            d_xo.rearrange("p (a t) -> p a t", t=2048)
                            [:, oc, 512 * g:512 * g + 512],
                            deps=[dep] if dep else [])
                bU = pb[oc % 2]
                c.wait("T", tao)
                c.wait("T", two)
                dep = last.get(("bU", oc % 2))
                if dep:
                    c.wait("T", dep)
                for ic in range(IC):
                    ins = nc.tensor.matmul(
                        bU, s_wo[:, 1024 * ic + 128 * oc:1024 * ic + 128 * oc + 128],
                        s_aoq[asl][:, 512 * ic:512 * ic + 512],
                        start=(ic == 0), stop=(ic == IC - 1))
                tm = c.inc("T", ins)
                c.wait("V", tm)
                c.wait("V", txo)
                tu = c.inc("V", nc.vector.scalar_tensor_tensor(
                    s_u[:, 512 * oc:512 * oc + 512], bU, s_bo[:, oc:oc + 1],
                    s_xoq[sl], ALU.add, ALU.add))
                last[("bU", oc % 2)] = tu
                last[("xoq", sl)] = tu
                dep = last.get(("sqB", oc % 2))
                if dep:
                    c.wait("A", dep)
                c.wait("A", tu)
                tsq = c.inc("A", nc.scalar.activation(
                    s_sqB[oc % 2], s_u[:, 512 * oc:512 * oc + 512], AF.Square))
                c.wait("T", tu)
                c.wait("T", tsq)
                if oc == 0:
                    dep = last.get(("pbst",))
                    if dep:
                        c.wait("T", dep)
                nc.tensor.matmul(pb[2][0:1, :], s_onc,
                                 s_u[:, 512 * oc:512 * oc + 512],
                                 start=(oc == 0), stop=(oc == IC - 1))
                tst = c.inc("T", nc.tensor.matmul(
                    pb[3][0:1, :], s_onc, s_sqB[oc % 2],
                    start=(oc == 0), stop=(oc == IC - 1)))
                last[("sqB", oc % 2)] = tst
            last[("aoq", asl)] = tm
            trows = ln_rows(pb[2], pb[3], tst)
            c.wait("T", trows)
            nc.tensor.matmul(pb[2], s_onf, s_arw, start=True, stop=True)
            nc.tensor.matmul(pb[2], s_onf, s_arw, start=True, stop=True)
            tbc = c.mm2(nc, pb[3], s_onf, s_brw2, start=True, stop=True)
            tx1 = None
            for oc in range(IC):
                c.wait("V", tbc)
                nc.vector.tensor_mul(s_sqB[oc % 2],
                                     s_u[:, 512 * oc:512 * oc + 512], pb[2])
                nc.vector.tensor_add(s_u[:, 512 * oc:512 * oc + 512],
                                     s_sqB[oc % 2], pb[3])
                nc.vector.tensor_add(s_u[:, 512 * oc:512 * oc + 512],
                                     s_sqB[oc % 2], pb[3])
                tx1 = c.inc("V", nc.vector.tensor_copy(
                    s_x1b[:, 512 * oc:512 * oc + 512],
                    s_u[:, 512 * oc:512 * oc + 512]))
            last[("pbst",)] = tx1

            if "dbg_x1" in dbg_outs and g == 0:
                c.dma("Y", "dbg", dbg_outs["dbg_x1"], s_u, deps=[tx1])
            if "dbg_ao" in dbg_outs and g == 0:
                c.dma("Y", "dbg", dbg_outs["dbg_ao"], s_aoq, deps=[tao])

            th = None
            for oc in range(OCH):
                sl = oc % 3
                if ("w1s", oc) in pf and g == 0:
                    t1 = pf[("w1s", oc)]
                    t2 = pf[("w2s", oc)]
                else:
                    dep = last.get(("w1s", sl))
                    t1 = c.dma("G", f"w1s{sl}", s_w1s[sl], d_w1[oc],
                               deps=[dep] if dep else [])
                    dep = last.get(("w2s", sl))
                    t2 = c.dma("G", f"w2s{sl}", s_w2s[sl], d_w2[oc],
                               deps=[dep] if dep else [])
                b1 = pb[4 + (oc % 2) * 2]
                b2 = pb[5 + (oc % 2) * 2]
                c.wait("T", t1)
                c.wait("T", tx1)
                dep = last.get(("b1", oc % 2))
                if dep:
                    c.wait("T", dep)
                for ic in range(IC):
                    ins = nc.tensor.matmul(
                        b1, s_w1s[sl][:, 128 * ic:128 * ic + 128],
                        s_x1b[:, 512 * ic:512 * ic + 512],
                        start=(ic == 0), stop=(ic == IC - 1))
                tm1 = c.inc("T", ins)
                c.wait("T", t2)
                for ic in range(IC):
                    ins = nc.tensor.matmul(
                        b2, s_w2s[sl][:, 128 * ic:128 * ic + 128],
                        s_x1b[:, 512 * ic:512 * ic + 512],
                        start=(ic == 0), stop=(ic == IC - 1))
                tm2 = c.inc("T", ins)
                c.wait("A", tm1)
                tsi = c.inc("A", nc.scalar.activation(
                    s_h[:, 512 * oc:512 * oc + 512], b1, AF.Silu,
                    scale=s_s1[:, oc:oc + 1]))
                c.wait("V", tm2)
                c.wait("V", tsi)
                th = c.inc("V", nc.vector.scalar_tensor_tensor(
                    s_h[:, 512 * oc:512 * oc + 512], b2, s_s2[:, oc:oc + 1],
                    s_h[:, 512 * oc:512 * oc + 512], ALU.mult, ALU.mult))
                last[("b1", oc % 2)] = th
                last[("w1s", sl)] = tm1
                last[("w2s", sl)] = tm2

            def hstage(dsrc, src_ap, dst_ap, scol, noc, wait_tok, resid=False, wtag_pf=False):
                tg_ = None
                for oc in range(noc):
                    sl = oc % 3
                    if ("hws", oc) in pf and wtag_pf:
                        tw = pf.pop(("hws", oc))
                    else:
                        dep = last.get(("hws", sl))
                        tw = c.dma("G", f"hws{sl}", s_hws[sl], dsrc[oc],
                                   deps=[dep] if dep else [])
                    bk = pb[4 + oc % 2]
                    c.wait("T", tw)
                    c.wait("T", wait_tok)
                    dep = last.get(("bh", oc % 2))
                    if dep:
                        c.wait("T", dep)
                    for icp in range(OCH):
                        ins = nc.tensor.matmul(
                            bk, s_hws[sl][:, 128 * icp:128 * icp + 128],
                            src_ap[:, 512 * icp:512 * icp + 512],
                            start=(icp == 0), stop=(icp == OCH - 1))
                    tm = c.inc("T", ins)
                    if not resid:
                        c.wait("A", tm)
                        tg_ = c.inc("A", nc.scalar.activation(
                            dst_ap[:, 512 * oc:512 * oc + 512], bk, AF.Gelu,
                            scale=scol[:, oc:oc + 1]))
                    else:
                        c.wait("V", tm)
                        tg_ = c.inc("V", nc.vector.scalar_tensor_tensor(
                            dst_ap[:, 512 * oc:512 * oc + 512], bk,
                            scol[:, oc:oc + 1],
                            s_u[:, 512 * oc:512 * oc + 512],
                            ALU.mult, ALU.add))
                        dep = last.get(("sqB", oc % 2))
                        if dep:
                            c.wait("A", dep)
                        c.wait("A", tg_)
                        tsq = c.inc("A", nc.scalar.activation(
                            s_sqB[oc % 2],
                            dst_ap[:, 512 * oc:512 * oc + 512], AF.Square))
                        c.wait("T", tg_)
                        c.wait("T", tsq)
                        if oc == 0:
                            dep2 = last.get(("pbst",))
                            if dep2:
                                c.wait("T", dep2)
                        nc.tensor.matmul(
                            pb[2][0:1, :], s_onc,
                            dst_ap[:, 512 * oc:512 * oc + 512],
                            start=(oc == 0), stop=(oc == noc - 1))
                        tst2 = c.inc("T", nc.tensor.matmul(
                            pb[3][0:1, :], s_onc, s_sqB[oc % 2],
                            start=(oc == 0), stop=(oc == noc - 1)))
                        last[("sqB", oc % 2)] = tst2
                        last[("lnst",)] = tst2
                    last[("bh", oc % 2)] = tg_
                    last[("hws", sl)] = tm
                return tg_

            tgel1 = hstage(d_h1, s_h, s_h2, s_s3, OCH, th, wtag_pf=(g == 0))
            tgel2 = hstage(d_h2, s_h2, s_h, s_s4, OCH, tgel1)
            tu2 = hstage(d_w3, s_h, s_u2, s_s5, IC, tgel2, resid=True)

            trows = ln_rows(pb[2], pb[3], last[("lnst",)])
            c.wait("T", trows)
            nc.tensor.matmul(pb[2], s_onf, s_arw, start=True, stop=True)
            nc.tensor.matmul(pb[2], s_onf, s_arw, start=True, stop=True)
            tbc = c.mm2(nc, pb[3], s_onf, s_brw2, start=True, stop=True)
            tfin = None
            for oc in range(IC):
                c.wait("V", tbc)
                nc.vector.tensor_mul(s_sqB[oc % 2],
                                     s_u2[:, 512 * oc:512 * oc + 512], pb[2])
                nc.vector.tensor_add(s_u2[:, 512 * oc:512 * oc + 512],
                                     s_sqB[oc % 2], pb[3])
                tfin = c.inc("V", nc.vector.tensor_add(
                    s_u2[:, 512 * oc:512 * oc + 512],
                    s_sqB[oc % 2], pb[3]))
            last[("pbst",)] = tfin
            tout = c.dma("Y", "out",
                         d_out.rearrange("p (a t) -> p a t", t=2048)
                         [:, :, 512 * g:512 * g + 512],
                         s_u2.rearrange("p (a t) -> p a t", t=512),
                         deps=[tfin])
            c.wait("V", tout)
            for _sl in range(3):
                last[("w1s", _sl)] = tout
                last[("w2s", _sl)] = tout
        A.reset(markB)
    else:
        c.wait("Y", tv0)
        c.dma("Y", "out", d_out[:, 0:1024], s_tq[:, 0:1024], deps=[tok_cst])

    c.wait("Y", ("out", c.cnt.get("out", 0)))
    if 2 <= STAGE < 3:
        c.wait("Y", ("cat0", c.cnt.get("cat0", 0)))
        c.wait("Y", ("cat1", c.cnt.get("cat1", 0)))
    if dbg_outs:
        c.wait("Y", ("dbg", c.cnt.get("dbg", 0)))
        c.wait("G", ("dbg", c.cnt.get("dbg", 0)))
    return nc


_CACHE = {}


def _host_tables():
    tq = np.zeros((NQT, 128, NCH), np.float32)
    dj = np.arange(128).reshape(128, 1)
    cc = np.arange(NCH).reshape(1, NCH)
    j = 128 * cc + dj
    for qt in range(NQT):
        i0 = 512 * qt
        i1 = i0 + 511
        tq[qt] = np.where(cc < 4 * qt, j - i0,
                          np.where(cc >= 4 * qt + 4, i1 - j, 0.0))
    tqp = np.ascontiguousarray(tq.transpose(1, 0, 2).reshape(128, NQT * NCH))
    d0 = np.zeros((4, 128, QT), np.float32)
    di = np.arange(QT).reshape(1, QT)
    for cp in range(4):
        d0[cp] = np.abs(128 * cp + np.arange(128).reshape(128, 1) - di)
    d0p = np.ascontiguousarray(d0.transpose(1, 0, 2).reshape(128, 4 * QT))
    return tqp, d0p


def _pcol(v, p=128):
    v = np.asarray(v, np.float32).reshape(-1)
    return np.ascontiguousarray(v.reshape(-1, p).T)


def _wblocks(wt, och):
    I, O = wt.shape
    icn = I // 128
    r = wt.reshape(icn, 128, och, 128).transpose(2, 1, 0, 3)
    return np.ascontiguousarray(r.reshape(och, 128, icn * 128))


def _prows(wt):
    I, O = wt.shape
    icn = I // 128
    return np.ascontiguousarray(
        wt.reshape(icn, 128, O).transpose(1, 0, 2).reshape(128, icn * O))


def kernel(L0_attn_x, L0_attn_qkv_w, L0_attn_qkv_b, L0_attn_out_w, L0_attn_out_b,
           L0_attn_decay_alpha, L0_ffn_w1_w, L0_ffn_w1_s, L0_ffn_w2_w, L0_ffn_w2_s,
           L0_ffn_h1_w, L0_ffn_h1_s, L0_ffn_h2_w, L0_ffn_h2_s, L0_ffn_w3_w,
           L0_ffn_w3_s, _debug=(), _stage=None):
    global STAGE
    if _stage is not None:
        STAGE = _stage
    x = np.asarray(L0_attn_x, np.float32)
    qkv_w = np.asarray(L0_attn_qkv_w, np.float32)
    qkv_b = np.asarray(L0_attn_qkv_b, np.float32)
    out_w = np.asarray(L0_attn_out_w, np.float32)
    out_b = np.asarray(L0_attn_out_b, np.float32)
    alpha = np.asarray(L0_attn_decay_alpha, np.float32)

    key = ("nc", STAGE, tuple(sorted(d[0] for d in _debug)))
    if key not in _CACHE:
        _CACHE[key] = build_nc(debug=_debug)
    nc = _CACHE[key]

    tqp, d0p = _host_tables()
    shared = {
        "wop": _prows(np.ascontiguousarray(out_w.T)),
        "bop": _pcol(out_b),
        "w1p": _wblocks(np.ascontiguousarray(np.asarray(L0_ffn_w1_w, np.float32).T), OCH),
        "w2p": _wblocks(np.ascontiguousarray(np.asarray(L0_ffn_w2_w, np.float32).T), OCH),
        "h1p": _wblocks(np.ascontiguousarray(np.asarray(L0_ffn_h1_w, np.float32).T), OCH),
        "h2p": _wblocks(np.ascontiguousarray(np.asarray(L0_ffn_h2_w, np.float32).T), OCH),
        "w3p": _wblocks(np.ascontiguousarray(np.asarray(L0_ffn_w3_w, np.float32).T), 8),
        "s1p": _pcol(L0_ffn_w1_s), "s2p": _pcol(L0_ffn_w2_s),
        "s3p": _pcol(L0_ffn_h1_s), "s4p": _pcol(L0_ffn_h2_s),
        "s5p": _pcol(L0_ffn_w3_s),
        "tqp": tqp, "d0p": d0p,
        "dqp": np.arange(QT, dtype=np.float32).reshape(1, QT),
        "onp": np.ones((1, 128), np.float32),
        "dqbp": np.tile(np.arange(QT, dtype=np.float32), (128, 1)),
        "eyp": np.concatenate([np.eye(2, dtype=np.float32), np.zeros((2, 510), np.float32)], axis=1),
    }
    xtp = [_prows(np.ascontiguousarray(x[b].T)) for b in range(2)]

    in_maps = []
    for core in range(8):
        b, hg = core // 4, core % 4
        xo = np.concatenate([x[b, 2048 * g + 512 * hg:2048 * g + 512 * hg + 512]
                             for g in range(4)], axis=0)
        m = dict(shared)
        m["xtp"] = xtp[b]
        m["xtop"] = _prows(np.ascontiguousarray(xo.T))
        m["wqp"] = _prows(np.ascontiguousarray(
            qkv_w[256 * hg:256 * hg + 256, :].T))
        m["wkp"] = _prows(np.ascontiguousarray(
            qkv_w[1024 + 256 * hg:1024 + 256 * hg + 256, :].T))
        m["wvp"] = _prows(np.ascontiguousarray(
            qkv_w[2048 + 256 * hg:2048 + 256 * hg + 256, :].T))
        bq = qkv_b[256 * hg:256 * hg + 256].reshape(2, 128)
        bk = qkv_b[1024 + 256 * hg:1024 + 256 * hg + 256].reshape(2, 128)
        m["bqkp"] = np.ascontiguousarray(np.stack(
            [bq[0], bq[1], bk[0], bk[1]], axis=1))
        m["bvp"] = np.ascontiguousarray(
            qkv_b[2048 + 256 * hg:2048 + 256 * hg + 256].reshape(1, 256))
        m["alp"] = np.ascontiguousarray(alpha[4 * hg:4 * hg + 4].reshape(1, 4))
        m["hgp"] = np.array([[hg]], np.int32)
        in_maps.append(m)

    res = run_bass_kernel_spmd(nc, in_maps, core_ids=list(range(8)))

    out = np.empty((2, 8192, 1024), np.float32)
    for core in range(8):
        b, hg = core // 4, core % 4
        o = np.asarray(res.results[core]["outp"])
        o = o.reshape(128, 8, 2048).transpose(1, 0, 2).reshape(1024, 2048)
        for g in range(4):
            out[b, 2048 * g + 512 * hg:2048 * g + 512 * hg + 512, :] = \
                o[:, 512 * g:512 * g + 512].T
    if _debug:
        return out, [dict(r) for r in res.results]
    return out


# version 07
# speedup vs baseline: 1.0126x; 1.0126x over previous
import numpy as np
import ml_dtypes
import concourse.bass as bass
import concourse.mybir as mybir
from concourse.bass_utils import run_bass_kernel_spmd
from concourse.expressions import RuntimeValue

F32 = mybir.dt.float32
BF16 = mybir.dt.bfloat16
I32 = mybir.dt.int32
AF = mybir.ActivationFunctionType
ALU = mybir.AluOpType
ds = bass.ds

D = 1024
T = 8192
IC = 8
QT = 512
NQT = T // QT
NCH = T // 128
DH = 4096
OCH = DH // 128
NLN8 = -2.0794415416798357

STAGE = 4


class Ctx:

    def __init__(self, nc):
        self.nc = nc
        self.eng = {"T": nc.tensor, "A": nc.scalar, "V": nc.vector,
                    "G": nc.gpsimd, "Y": nc.sync}
        self.sem = {k: nc.alloc_semaphore("sm" + k) for k in self.eng}
        self.pool = [nc.alloc_semaphore(f"sp{i}") for i in range(56)]
        self.cnt = {k: 0 for k in self.eng}
        self.water = {}
        nums = [s.num for s in list(self.sem.values()) + self.pool]
        assert max(nums) - min(nums) + 1 == len(nums), "sem range not contiguous"
        nc.gpsimd.sem_clear(range(min(nums), max(nums) + 1))
        nc.all_engine_barrier()

    def dsem(self, name):
        if name not in self.sem:
            self.sem[name] = self.pool.pop(0)
            self.cnt[name] = 0
        return self.sem[name]

    def wait(self, who, tok):
        if tok is None:
            return
        name, val = tok
        if name == who or val <= 0:
            return
        if self.water.get((who, name), 0) >= val:
            return
        self.water[(who, name)] = val
        self.eng[who].wait_ge(self.sem[name], val)

    def mm2(self, nc, *args, **kw):
        nc.tensor.matmul(*args, **kw)
        return self.inc("T", nc.tensor.matmul(*args, **kw))

    def v2(self, emit):
        emit()
        return self.inc("V", emit())

    def inc(self, who, ins):
        self.cnt[who] += 1
        ins.then_inc(self.sem[who], 1)
        return (who, self.cnt[who])

    def dma(self, who, name, dst, src, deps=()):
        for d in deps:
            self.wait(who, d)
        s = self.dsem(name)
        self.eng[who].dma_start(dst, src).then_inc(s, 16)
        self.cnt[name] = self.cnt.get(name, 0) + 16
        return (name, self.cnt[name])

    def join(self, engines=("T", "A", "V", "G", "Y")):
        toks = [(k, self.cnt[k]) for k in engines]
        for who in engines:
            for t in toks:
                self.wait(who, t)


class Arena:
    def __init__(self, nc, nbytes):
        self.ap = nc.alloc_sbuf_tensor("arena", [128, nbytes], mybir.dt.int8).ap()
        self.off = 0
        self.nbytes = nbytes

    def alloc(self, cols, dtype):
        esz = mybir.dt.size(dtype)
        n = cols * esz
        off = (self.off + 31) // 32 * 32
        assert off + n <= self.nbytes, f"arena overflow: {off + n} > {self.nbytes}"
        self.off = off + n
        return self.ap[:, off:off + n].bitcast(dtype)

    def mark(self):
        return self.off

    def reset(self, mark):
        self.off = mark

    def at(self, off, cols, dtype):
        n = cols * mybir.dt.size(dtype)
        assert off % 32 == 0 and off + n <= self.nbytes
        return self.ap[:, off:off + n].bitcast(dtype)


def build_nc(debug=()):
    nc = bass.Bass("TRN2", target_bir_lowering=False)
    c = Ctx(nc)

    def din(name, shape, dt=F32):
        return nc.dram_tensor(name, shape, dt, kind="ExternalInput").ap()

    d_xt = din("xtp", [128, IC * T])
    d_xo = din("xtop", [128, IC * 2048])
    d_wq = din("wqp", [128, IC * 256])
    d_wk = din("wkp", [128, IC * 256])
    d_wv = din("wvp", [128, IC * 256])
    d_bqk = din("bqkp", [128, 4])
    d_bv = din("bvp", [1, 256])
    d_al = din("alp", [1, 4])
    d_wo = din("wop", [128, IC * D])
    d_bo = din("bop", [128, 8])
    d_w1 = din("w1p", [OCH, 128, IC * 128])
    d_w2 = din("w2p", [OCH, 128, IC * 128])
    d_h1 = din("h1p", [OCH, 128, OCH * 128])
    d_h2 = din("h2p", [OCH, 128, OCH * 128])
    d_w3 = din("w3p", [8, 128, OCH * 128])
    d_s1 = din("s1p", [128, OCH])
    d_s2 = din("s2p", [128, OCH])
    d_s3 = din("s3p", [128, OCH])
    d_s4 = din("s4p", [128, OCH])
    d_s5 = din("s5p", [128, 8])
    d_tq = din("tqp", [128, NQT * NCH])
    d_d0 = din("d0p", [128, 4 * QT])
    d_dq = din("dqp", [1, QT])
    d_dqb = din("dqbp", [128, QT])
    d_on = din("onp", [1, 128])
    d_ey = din("eyp", [2, 512])
    d_hg = din("hgp", [1, 1], I32)

    d_cat = nc.dram_tensor("attn_cat", [256, T], BF16, kind="Internal")
    d_all1 = nc.dram_tensor("attn_all1", [512, T], BF16, kind="Internal")
    d_all2 = nc.dram_tensor("attn_all2", [512, T], BF16, kind="Internal")
    d_out = nc.dram_tensor("outp", [128, 8 * 2048], F32, kind="ExternalOutput").ap()

    dbg_outs = {}
    for name, shape, dt in debug:
        dbg_outs[name] = nc.dram_tensor(name, shape, dt, kind="ExternalOutput").ap()

    def sb(name, shape, dt):
        return nc.alloc_sbuf_tensor(name, shape, dt).ap()

    s_tq = sb("s_tq", [128, NQT * NCH], F32)
    s_d0 = sb("s_d0", [128, 4 * QT], F32)
    s_dq = sb("s_dq", [1, QT], F32)
    s_dqb = sb("s_dqb", [128, QT], F32)
    s_onf = sb("s_onf", [1, 128], F32)
    s_onb = sb("s_onb", [1, 128], BF16)
    s_ey = sb("s_ey", [2, 512], F32)
    s_bqk = sb("s_bqk", [128, 4], F32)
    s_bv = sb("s_bv", [1, 256], F32)
    s_al = sb("s_al", [1, 4], F32)
    s_hgi = sb("s_hgi", [1, 1], I32)
    s_bo = sb("s_bo", [128, 8], F32)
    s_s1 = sb("s_s1", [128, OCH], F32)
    s_s2 = sb("s_s2", [128, OCH], F32)
    s_s3 = sb("s_s3", [128, OCH], F32)
    s_s4 = sb("s_s4", [128, OCH], F32)
    s_s5 = sb("s_s5", [128, 8], F32)
    s_ar = sb("s_ar", [1, 512], F32)
    s_nar = sb("s_nar", [1, 4], F32)
    s_m511 = sb("s_m511", [1, 4], F32)
    s_ab = sb("s_ab", [128, 4], F32)
    s_n8ab = sb("s_n8ab", [128, 4], F32)
    s_m511b = sb("s_m511b", [128, 4], F32)
    s_ur = sb("s_ur", [1, 512], F32)
    s_nub = sb("s_nub", [128, 4], F32)
    s_k2r = sb("s_k2r", [1, 4], F32)
    s_q2r = sb("s_q2r", [1, 4], F32)
    s_k2mP = [sb("s_k2m0", [2, 1], F32), sb("s_k2m1", [2, 1], F32)]
    s_q2mP = [sb("s_q2m0", [2, 1], F32), sb("s_q2m1", [2, 1], F32)]
    s_ind2 = sb("s_ind2", [128, 2], BF16)
    s_onc = sb("s_onc", [128, 1], F32)
    s_eps = sb("s_eps", [1, 1], F32)
    s_one = sb("s_one", [1, 1], F32)
    s_nl8 = sb("s_nl8", [1, 1], F32)
    s_t14a = sb("s_t14a", [1, 4], F32)
    s_t14b = sb("s_t14b", [1, 4], F32)
    s_bsl = [sb(f"s_bsl{h}", [128, NCH], F32) for h in range(2)]

    pb = [nc.alloc_psum_tensor(f"pb{i}", [128, 512], F32).ap() for i in range(8)]

    A = Arena(nc, 182 * 1024)
    last = {}

    for dst, src in [(s_tq, d_tq), (s_d0, d_d0), (s_dq, d_dq), (s_dqb, d_dqb), (s_onf, d_on),
                     (s_ey, d_ey), (s_bqk, d_bqk), (s_bv, d_bv), (s_al, d_al),
                     (s_hgi, d_hg), (s_bo, d_bo), (s_s1, d_s1), (s_s2, d_s2),
                     (s_s3, d_s3), (s_s4, d_s4), (s_s5, d_s5)]:
        c.dma("Y", "cst", dst, src)
    tok_cst = ("cst", c.cnt["cst"])
    tok_cstg = c.dma("G", "cstg", s_onb, d_on)

    c.wait("V", tok_cst)
    c.wait("V", tok_cstg)
    nc.vector.memset(s_onc, 1.0)
    nc.vector.memset(s_eps, 1e-5)
    nc.vector.memset(s_one, 1.0)
    nc.vector.memset(s_nl8, NLN8)
    nc.vector.memset(s_ind2, 0.0)
    nc.vector.memset(s_ind2[0:64, 0:1], 1.0)
    nc.vector.memset(s_ind2[64:128, 1:2], 1.0)
    nc.vector.memset(s_ur, 0.0)
    nc.vector.memset(s_ar, 0.0)
    nc.vector.memset(s_k2mP[0], 0.0)
    nc.vector.memset(s_k2mP[1], 0.0)
    nc.vector.memset(s_q2mP[0], 0.0)
    tv0 = c.inc("V", nc.vector.memset(s_q2mP[1], 0.0))

    c.wait("A", tok_cst)
    c.wait("A", tv0)
    nc.scalar.activation(s_t14a, s_al, AF.Abs)
    nc.scalar.activation(s_t14a, s_t14a, AF.Exp, scale=-1.0)
    nc.scalar.activation(s_t14a, s_t14a, AF.Ln, bias=s_one)
    ta0 = c.inc("A", nc.scalar.activation(s_t14b, s_al, AF.Relu))
    c.wait("V", ta0)
    nc.vector.tensor_add(s_ar[0:1, 0:4], s_t14a, s_t14b)
    nc.vector.tensor_scalar_mul(s_nar, s_ar[0:1, 0:4], -1.0)
    tv1 = c.inc("V", nc.vector.tensor_scalar_mul(s_m511, s_ar[0:1, 0:4], -511.0))
    c.wait("T", tv1)
    tmm0 = c.mm2(nc, pb[0], s_onf, s_ar[0:1, :], start=True, stop=True)
    c.wait("V", tmm0)
    nc.vector.tensor_copy(s_ab, pb[0][:, 0:4])
    nc.vector.tensor_copy(s_ab, pb[0][:, 0:4])
    nc.vector.tensor_scalar_mul(s_n8ab, s_ab, -8.0)
    nc.vector.tensor_scalar_mul(s_n8ab, s_ab, -8.0)
    nc.vector.tensor_scalar_mul(s_m511b, s_ab, -511.0)
    tv2 = c.inc("V", nc.vector.tensor_scalar_mul(s_m511b, s_ab, -511.0))
    last[("pb", 0)] = tv2

    c.wait("G", tok_cst)
    rhg = nc.gpsimd.alloc_register("rhg")
    nc.gpsimd.reg_load(rhg, s_hgi)
    vhg = RuntimeValue(rhg, min_val=0, max_val=3)

    pairst = {}

    def proj_all():
        c.join()
        s_KT = [A.alloc(T, BF16) for _ in range(2)]
        s_QT = [A.alloc(T, BF16) for _ in range(2)]
        s_V = [[A.alloc(NCH * 65, BF16) for _ in range(2)] for _ in range(2)]
        pairst["KT"] = s_KT
        pairst["QT"] = s_QT
        pairst["V"] = s_V
        m1 = A.mark()
        s_xt = [A.alloc(IC * 512, BF16) for _ in range(2)]
        s_wqb = A.alloc(IC * 256, BF16)
        s_wkb = A.alloc(IC * 256, BF16)
        s_wvb = A.alloc(IC * 256, BF16)
        s_wv260 = A.alloc(IC * 260, BF16)
        s_bv260 = A.alloc(260, BF16)
        s_sq = A.alloc(512, BF16)
        s_sq2 = A.alloc(512, BF16)
        s_r21 = A.alloc(1, F32)
        s_r22 = A.alloc(1, F32)

        twq = c.dma("G", "wq0", s_wqb.rearrange("p (a o) -> p a o", o=256),
                    d_wq.rearrange("p (a o) -> p a o", o=256))
        twk = c.dma("G", "wk0", s_wkb.rearrange("p (a o) -> p a o", o=256),
                    d_wk.rearrange("p (a o) -> p a o", o=256))
        twv = c.dma("G", "wv0", s_wvb.rearrange("p (a o) -> p a o", o=256),
                    d_wv.rearrange("p (a o) -> p a o", o=256))
        c.wait("V", twv)
        c.wait("V", tok_cst)
        nc.vector.memset(s_wv260, 0.0)
        for ic in range(IC):
            for hl in range(4):
                nc.vector.tensor_copy(
                    s_wv260[:, 260 * ic + 65 * hl:260 * ic + 65 * hl + 64],
                    s_wvb[:, 256 * ic + 64 * hl:256 * ic + 64 * hl + 64])
        nc.vector.memset(s_bv260[0:1, 0:260], 0.0)
        for hl in range(4):
            nc.vector.tensor_copy(s_bv260[0:1, 65 * hl:65 * hl + 64],
                                  s_bv[0:1, 64 * hl:64 * hl + 64])
            nc.vector.memset(s_bv260[0:1, 65 * hl + 64:65 * hl + 65], 1.0)
        nc.vector.memset(s_bv260[0:1, 259:260], 1.0)
        tv260 = c.inc("V", nc.vector.memset(s_bv260[0:1, 259:260], 1.0))

        tev = {}
        tvv = None
        for tc in range(16):
            sl = tc % 2
            dep = last.get(("xt", sl))
            tx = c.dma("G", f"x{sl}", s_xt[sl].rearrange("p (a t) -> p a t", t=512),
                       d_xt.rearrange("p (a t) -> p a t", t=T)
                       [:, :, 512 * tc:512 * tc + 512],
                       deps=[dep] if dep else [])
            c.wait("T", tx)
            c.wait("T", twk)
            c.wait("T", twq)
            c.wait("T", tv260)
            for p2 in range(2):
                bK = pb[p2]
                bQ = pb[2 + p2]
                c.wait("T", tev.get(("K", p2)))
                for ic in range(IC):
                    ins = nc.tensor.matmul(
                        bK, s_wkb[:, 256 * ic + 128 * p2:256 * ic + 128 * p2 + 128],
                        s_xt[sl][:, 512 * ic:512 * ic + 512],
                        start=(ic == 0), stop=(ic == IC - 1))
                tk = c.inc("T", ins)
                c.wait("T", tev.get(("Q", p2)))
                for ic in range(IC):
                    ins = nc.tensor.matmul(
                        bQ, s_wqb[:, 256 * ic + 128 * p2:256 * ic + 128 * p2 + 128],
                        s_xt[sl][:, 512 * ic:512 * ic + 512],
                        start=(ic == 0), stop=(ic == IC - 1))
                tq_ = c.inc("T", ins)
                c.wait("V", tk)
                tev[("K", p2)] = c.inc("V", nc.vector.tensor_scalar(
                    s_KT[p2][:, 512 * tc:512 * tc + 512], bK,
                    s_bqk[:, 2 + p2:3 + p2], None, ALU.add))
                c.wait("V", tq_)
                tev[("Q", p2)] = c.inc("V", nc.vector.tensor_scalar(
                    s_QT[p2][:, 512 * tc:512 * tc + 512], bQ,
                    s_bqk[:, p2:p2 + 1], None, ALU.add))
            for ts in range(4):
                bV = pb[4 + ts % 2]
                kc = 4 * tc + ts
                c.wait("T", tev.get(("V", ts % 2)))
                for ic in range(IC):
                    nc.tensor.matmul(
                        bV[:, 0:260],
                        s_xt[sl][:, 512 * ic + 128 * ts:512 * ic + 128 * ts + 128],
                        s_wv260[:, 260 * ic:260 * ic + 260],
                        start=(ic == 0), stop=False)
                tvv = c.inc("T", nc.tensor.matmul(
                    bV[:, 0:260], s_onb, s_bv260[0:1, 0:260],
                    start=False, stop=True))
                c.wait("V", tvv)
                for p2 in range(2):
                    for h in range(2):
                        ins = nc.vector.tensor_copy(
                            s_V[p2][h][:, 65 * kc:65 * kc + 65],
                            bV[:, 130 * p2 + 65 * h:130 * p2 + 65 * h + 65])
                tev[("V", ts % 2)] = c.inc("V", ins)
            last[("xt", sl)] = tvv

        for key in (("K", 0), ("K", 1), ("Q", 0), ("Q", 1)):
            c.wait("A", tev[key])
        stat_src = []
        for p2 in range(2):
            stat_src += [(s_KT[p2], s_k2mP[p2], kc2) for kc2 in range(16)]
            stat_src += [(s_QT[p2], s_q2mP[p2], kc2) for kc2 in range(16)]
        sqs = [s_sq, s_sq2]
        NST = len(stat_src)
        tvm = None
        tsq_l = {}

        def stat_sq(i):
            srcb, _, kc2 = stat_src[i]
            dep = last.get(("stsq", i % 2))
            if dep:
                c.wait("A", dep)
            return c.inc("A", nc.scalar.activation(
                sqs[i % 2], srcb[:, 512 * kc2:512 * kc2 + 512], AF.Square))

        tsq_l[0] = stat_sq(0)
        for i in range(NST):
            bnk = pb[6] if i % 2 == 0 else pb[7]
            c.wait("T", tsq_l[i])
            c.wait("T", tv0)
            dep = last.get(("pbst6", i % 2))
            if dep:
                c.wait("T", dep)
            tm = c.mm2(nc, bnk[0:2, :], s_ind2, sqs[i % 2],
                       start=True, stop=True)
            last[("stsq", i % 2)] = tm
            if i + 1 < NST:
                tsq_l[i + 1] = stat_sq(i + 1)
            dstm = stat_src[i][1]
            c.wait("V", tm)
            if tvm is not None:
                c.wait("V", tvm)
            nc.vector.reduce_max(s_r21[0:2, 0:1], bnk[0:2, :],
                                 axis=mybir.AxisListType.X)
            nc.vector.reduce_max(s_r21[0:2, 0:1], bnk[0:2, :],
                                 axis=mybir.AxisListType.X)
            nc.vector.tensor_max(s_r22[0:2, 0:1], dstm, s_r21[0:2, 0:1])
            nc.vector.tensor_max(s_r22[0:2, 0:1], dstm, s_r21[0:2, 0:1])
            nc.vector.tensor_copy(dstm, s_r22[0:2, 0:1])
            tvm = c.inc("V", nc.vector.tensor_copy(dstm, s_r22[0:2, 0:1]))
            last[("pbst6", i % 2)] = tvm
        last[("pb", 6)] = tvm
        c.wait("T", tvm)

        for p2 in range(2):
            dep = last.get(("pb", 6))
            if dep:
                c.wait("T", dep)
            tk2 = c.mm2(nc, pb[6][0:1, 0:512], s_k2mP[p2], s_ey,
                        start=True, stop=True)
            c.wait("V", tk2)
            tvk = c.v2(lambda p2=p2: nc.vector.tensor_copy(
                s_k2r[0:1, 2 * p2:2 * p2 + 2], pb[6][0:1, 0:2]))
            c.wait("T", tvk)
            tq2 = c.mm2(nc, pb[6][0:1, 0:512], s_q2mP[p2], s_ey,
                        start=True, stop=True)
            c.wait("V", tq2)
            nc.vector.tensor_copy(s_q2r[0:1, 2 * p2:2 * p2 + 2], pb[6][0:1, 0:2])
            nc.vector.tensor_copy(s_q2r[0:1, 2 * p2:2 * p2 + 2], pb[6][0:1, 0:2])
            nc.vector.tensor_mul(
                s_t14a[0:1, 2 * p2:2 * p2 + 2], s_k2r[0:1, 2 * p2:2 * p2 + 2],
                s_q2r[0:1, 2 * p2:2 * p2 + 2])
            tvq = c.inc("V", nc.vector.tensor_mul(
                s_t14a[0:1, 2 * p2:2 * p2 + 2], s_k2r[0:1, 2 * p2:2 * p2 + 2],
                s_q2r[0:1, 2 * p2:2 * p2 + 2]))
            last[("pb", 6)] = tvq
            c.wait("A", tvq)
            nc.scalar.activation(s_t14b[0:1, 2 * p2:2 * p2 + 2],
                                 s_t14a[0:1, 2 * p2:2 * p2 + 2], AF.Ln)
            tua = c.inc("A", nc.scalar.activation(
                s_ur[0:1, 2 * p2:2 * p2 + 2], s_t14b[0:1, 2 * p2:2 * p2 + 2],
                AF.Exp, scale=0.5, bias=s_nl8))
            c.wait("T", tua)
            c.wait("T", tvq)
            tub = c.mm2(nc, pb[6], s_onf, s_ur[0:1, :], start=True, stop=True)
            c.wait("V", tub)
            tnu = c.v2(lambda p2=p2: nc.vector.tensor_scalar(
                s_nub[:, 2 * p2:2 * p2 + 2], pb[6][:, 2 * p2:2 * p2 + 2],
                -1.0, None, ALU.mult))
            last[("pb", 6)] = tnu
            pairst[("tnu", p2)] = tnu
        A.reset(m1)

    def attention_pair(p):
        c.join()
        mark = A.mark()
        s_KT = pairst["KT"][p]
        s_QT = pairst["QT"][p]
        s_V = pairst["V"][p]
        tnu = pairst[("tnu", p)]
        s_ds = [[A.alloc(512, F32) for _ in range(2)] for _ in range(2)]
        s_P = [[A.alloc(512, BF16) for _ in range(14)] for _ in range(2)]
        s_dh = [A.alloc(512, F32) for _ in range(2)]
        s_xc = [A.alloc(512, F32) for _ in range(2)]
        s_x2 = [A.alloc(512, F32) for _ in range(2)]
        s_ao = [A.alloc(512, BF16) for _ in range(2)]
        s_G = [A.alloc(QT, F32) for _ in range(2)]
        s_H = [A.alloc(QT, F32) for _ in range(2)]
        s_lin = [A.alloc(QT, F32) for _ in range(2)]

        tgh = None
        for h in range(2):
            hl = 2 * p + h
            c.wait("V", tv2)
            tgt = c.inc("V", nc.vector.tensor_scalar(
                s_x2[h], s_dqb, s_ab[:, hl:hl + 1], None, ALU.mult))
            c.wait("A", tgt)
            nc.scalar.activation(s_G[h], s_x2[h], AF.Exp, scale=-1.0)
            tgh = c.inc("A", nc.scalar.activation(
                s_H[h], s_x2[h], AF.Exp, scale=1.0,
                bias=s_m511b[:, hl:hl + 1]))

        if STAGE < 2:
            A.reset(mark)
            return

        bS = [[pb[0], pb[1]], [pb[2], pb[3]]]
        bL = [pb[4], pb[5]]
        bR = [pb[6], pb[7]]
        c.join()

        NP = 14
        AVLAG = 12
        EPIPOS = 10
        RUNWAY = 6
        pend = None

        def emit_epi_V1(st):
            qt0 = st["qt"]
            hasL, hasR = qt0 > 0, qt0 < NQT - 1
            tvcs = []
            for h in range(2):
                c.wait("V", st["tAV"])
                c.wait("V", tgh)
                if hasL:
                    nc.vector.tensor_mul(s_xc[h][0:65, :], bL[h][0:65, :],
                                         s_G[h][0:65, :])
                    if hasR:
                        nc.vector.tensor_mul(s_x2[h][0:65, :], bR[h][0:65, :],
                                             s_H[h][0:65, :])
                        nc.vector.tensor_add(s_xc[h][0:65, :], s_xc[h][0:65, :],
                                             s_x2[h][0:65, :])
                else:
                    nc.vector.tensor_mul(s_xc[h][0:65, :], bR[h][0:65, :],
                                         s_H[h][0:65, :])
                nc.vector.tensor_add(s_xc[h][0:65, :], s_xc[h][0:65, :],
                                     s_dh[h][0:65, :])
                nc.vector.reciprocal(s_lin[h][0:1, :], s_xc[h][64:65, :])
                tvcs.append(c.inc("V", nc.vector.reciprocal(
                    s_lin[h][0:1, :], s_xc[h][64:65, :])))
            st["tvcs"] = tvcs
            last[("accR", 0)] = tvcs[-1]
            last[("accR", 1)] = tvcs[-1]

        def emit_epi_T(st):
            tbcs = []
            for h in range(2):
                c.wait("T", st["tvcs"][h])
                tbcs.append(c.mm2(nc, bL[h][0:64, :], s_onf[0:1, 0:64],
                                  s_lin[h][0:1, :], start=True, stop=True))
            st["tbcs"] = tbcs

        def emit_epi_V2(st):
            qt0 = st["qt"]
            aos = s_ao[qt0 % 2]
            tEp = None
            for h in range(2):
                c.wait("V", st["tbcs"][h])
                dep = last.get(("aost", qt0 % 2))
                if dep:
                    c.wait("V", dep)
                tEp = c.v2(lambda h=h, aos=aos: nc.vector.tensor_mul(
                    aos[64 * h:64 * h + 64, :], s_xc[h][0:64, :],
                    bL[h][0:64, :]))
                last[("accL", h)] = tEp
            tdma = c.dma("Y", f"cat{qt0 % 2}",
                         d_cat.ap()[128 * p:128 * p + 128,
                                    512 * qt0:512 * qt0 + 512],
                         aos, deps=[tEp])
            last[("aost", qt0 % 2)] = tdma

        for qt in range(NQT):
            tb = []
            for h in range(2):
                hl = 2 * p + h
                dep = last.get(("bsl", h))
                if dep:
                    c.wait("V", dep)
                c.wait("V", tnu)
                c.wait("V", tv2)
                tb.append(c.inc("V", nc.vector.tensor_scalar(
                    s_bsl[h], s_tq[:, 64 * qt:64 * qt + 64],
                    s_ab[:, hl:hl + 1], s_nub[:, hl:hl + 1], ALU.mult, ALU.add)))
            if pend is not None:
                emit_epi_V1(pend)

            ndiag0 = 4 * qt
            Lc = [("L", cc, None) for cc in range(0, ndiag0)]
            Rc = [("R", cc, None) for cc in range(ndiag0 + 4, NCH)]
            Dc = [("D", ndiag0 + j, j) for j in range(4)]
            if Lc:
                borrow = "R"
                chunks = Lc[:RUNWAY] + Dc + Lc[RUNWAY:] + Rc
            else:
                borrow = "L"
                chunks = Rc[:RUNWAY] + Dc + Rc[RUNWAY:]
            nch = len(chunks)
            bDA = bR if borrow == "R" else bL
            lastof = {}
            firstof = {}
            for i, (kind, cc, j) in enumerate(chunks):
                lastof[kind] = i
                firstof.setdefault(kind, i)

            def emit_S(i):
                kind, cc, j = chunks[i]
                buf = i % 2
                for h in range(2):
                    dep = last.get(("expS", h, buf))
                    if dep:
                        c.wait("T", dep)
                    ins = nc.tensor.matmul(
                        bS[h][buf],
                        s_KT[64 * h:64 * h + 64, 128 * cc:128 * cc + 128],
                        s_QT[64 * h:64 * h + 64, 512 * qt:512 * qt + 512],
                        start=True, stop=True, tile_position=(64 * h, 0))
                return c.inc("T", ins)

            def emit_mid(i, tS):
                kind, cc, j = chunks[i]
                buf = i % 2
                pslot = i % NP
                if kind == "D":
                    c.wait("V", tS)
                    for h in range(2):
                        hl = 2 * p + h
                        dep = last.get(("expD", h, j % 2))
                        if dep:
                            c.wait("V", dep)
                        ins = nc.vector.scalar_tensor_tensor(
                            s_ds[h][j % 2], s_d0[:, 512 * j:512 * j + 512],
                            s_n8ab[:, hl:hl + 1], bS[h][buf], ALU.mult, ALU.add)
                    tD = c.inc("V", ins)
                    last[("expS", 0, buf)] = tD
                    last[("expS", 1, buf)] = tD
                    c.wait("A", tD)
                    for h in range(2):
                        hl = 2 * p + h
                        dep = last.get(("avP", h, pslot))
                        if dep:
                            c.wait("A", dep)
                        ins = nc.scalar.activation(
                            s_P[h][pslot], s_ds[h][j % 2], AF.Exp, scale=0.125,
                            bias=s_nub[:, hl:hl + 1])
                    tE = c.inc("A", ins)
                    last[("expD", 0, j % 2)] = tE
                    last[("expD", 1, j % 2)] = tE
                else:
                    c.wait("A", tS)
                    c.wait("A", tb[0])
                    c.wait("A", tb[1])
                    for h in range(2):
                        dep = last.get(("avP", h, pslot))
                        if dep:
                            c.wait("A", dep)
                        ins = nc.scalar.activation(
                            s_P[h][pslot], bS[h][buf], AF.Exp, scale=0.125,
                            bias=s_bsl[h][:, cc:cc + 1])
                    tE = c.inc("A", ins)
                    last[("expS", 0, buf)] = tE
                    last[("expS", 1, buf)] = tE
                return tE

            def emit_AV(i, tE):
                kind, cc, j = chunks[i]
                pslot = i % NP
                first = (i == firstof[kind])
                lastk = (i == lastof[kind])
                bACC = bDA if kind == "D" else (bL if kind == "L" else bR)
                c.wait("T", tE)
                for h in range(2):
                    if first:
                        key = ("acc" + borrow, h) if kind == "D" else ("acc" + kind, h)
                        dep = last.get(key)
                        if dep:
                            c.wait("T", dep)
                        if kind == "R" and borrow == "R":
                            dep = last.get(("accDevac", h))
                            if dep:
                                c.wait("T", dep)
                        if kind == "L" and borrow == "L":
                            dep = last.get(("accDevac", h))
                            if dep:
                                c.wait("T", dep)
                    ins = nc.tensor.matmul(
                        bACC[h][0:65, :], s_V[h][:, 65 * cc:65 * cc + 65],
                        s_P[h][pslot], start=first, stop=lastk)
                tAV = c.inc("T", ins)
                last[("avP", 0, pslot)] = tAV
                last[("avP", 1, pslot)] = tAV
                if kind == "D" and lastk:
                    c.wait("V", tAV)
                    for h in range(2):
                        nc.vector.tensor_copy(s_dh[h][0:65, :], bDA[h][0:65, :])
                        ins = nc.vector.tensor_copy(s_dh[h][0:65, :],
                                                    bDA[h][0:65, :])
                    tDh = c.inc("V", ins)
                    last[("accDevac", 0)] = tDh
                    last[("accDevac", 1)] = tDh
                return tAV

            tS_l = {0: emit_S(0)}
            tE_l = {}
            tAV = None
            avdone = 0
            epi_T_done = pend is None
            for i in range(nch):
                tE_l[i] = emit_mid(i, tS_l[i])
                if i + 1 < nch:
                    tS_l[i + 1] = emit_S(i + 1)
                if not epi_T_done and i >= EPIPOS:
                    emit_epi_T(pend)
                    emit_epi_V2(pend)
                    epi_T_done = True
                navs = 0 if i < AVLAG else (2 if i >= nch - AVLAG else 1)
                for _ in range(navs):
                    if avdone <= i:
                        tAV = emit_AV(avdone, tE_l[avdone])
                        avdone += 1
            if not epi_T_done:
                emit_epi_T(pend)
                emit_epi_V2(pend)
            while avdone < nch:
                tAV = emit_AV(avdone, tE_l[avdone])
                avdone += 1
            last[("bsl", 0)] = ("A", c.cnt["A"])
            last[("bsl", 1)] = ("A", c.cnt["A"])
            pend = {"qt": qt, "tAV": tAV}
        emit_epi_V1(pend)
        emit_epi_T(pend)
        emit_epi_V2(pend)
        pend = None
        A.reset(mark)

    proj_all()
    attention_pair(0)
    tok_cc1 = None
    if STAGE >= 3:
        c.wait("G", ("cat0", c.cnt.get("cat0", 0)))
        c.wait("G", ("cat1", c.cnt.get("cat1", 0)))
        ccsem = c.dsem("cc")
        nc.gpsimd.collective_compute(
            "AllGather", ALU.bypass,
            replica_groups=[[0, 1, 2, 3], [4, 5, 6, 7]],
            ins=[d_cat.ap()[0:128, :].opt()], outs=[d_all1.ap().opt()],
        ).then_inc(ccsem)
        c.cnt["cc"] = c.cnt.get("cc", 0) + 1
        tok_cc1 = ("cc", c.cnt["cc"])
    attention_pair(1)

    tok_cc = None
    if STAGE >= 3:
        c.wait("G", ("cat0", c.cnt.get("cat0", 0)))
        c.wait("G", ("cat1", c.cnt.get("cat1", 0)))
        ccsem = c.dsem("cc")
        nc.gpsimd.collective_compute(
            "AllGather", ALU.bypass,
            replica_groups=[[0, 1, 2, 3], [4, 5, 6, 7]],
            ins=[d_cat.ap()[128:256, :].opt()], outs=[d_all2.ap().opt()],
        ).then_inc(ccsem)
        c.cnt["cc"] = c.cnt.get("cc", 0) + 1
        tok_cc = ("cc", c.cnt["cc"])
    if STAGE >= 4:
        c.join()
        A.reset(0)
        markB = A.mark()
        s_wo = A.alloc(IC * D, BF16)
        s_aoq = [A.alloc(IC * 512, BF16) for _ in range(2)]
        s_xoq = [A.alloc(512, F32) for _ in range(2)]
        s_u = A.alloc(IC * 512, F32)
        s_h = A.alloc(OCH * 512, BF16)
        s_h2 = A.alloc(OCH * 512, BF16)
        s_hws = [A.alloc(OCH * 128, BF16) for _ in range(3)]
        s_sqB = [A.alloc(512, F32) for _ in range(2)]
        s_mr = A.alloc(QT, F32)[0:1, :]
        s_er = A.alloc(QT, F32)[0:1, :]
        s_vr = A.alloc(QT, F32)[0:1, :]
        s_vr2 = A.alloc(QT, F32)[0:1, :]
        s_vr3 = A.alloc(QT, F32)[0:1, :]
        s_arw = A.alloc(QT, F32)[0:1, :]
        s_brw = A.alloc(QT, F32)[0:1, :]
        s_brw2 = A.alloc(QT, F32)[0:1, :]
        u2_off = (A.mark() + 31) // 32 * 32
        s_x1b = A.alloc(IC * 512, BF16)
        s_w1s = [A.alloc(IC * 128, BF16) for _ in range(3)]
        s_w2s = [A.alloc(IC * 128, BF16) for _ in range(3)]
        s_u2 = A.at(u2_off, IC * 512, F32)

        two = c.dma("G", "wo", s_wo, d_wo)
        pf = {}
        for oc in range(3):
            pf[("w1s", oc)] = c.dma("G", f"w1s{oc % 3}", s_w1s[oc % 3], d_w1[oc])
            pf[("w2s", oc)] = c.dma("G", f"w2s{oc % 3}", s_w2s[oc % 3], d_w2[oc])
            pf[("hws", oc)] = c.dma("G", f"hws{oc % 3}", s_hws[oc % 3], d_h1[oc])

        def ln_rows(bM1, bM2, tstats):
            c.wait("V", tstats)
            nc.vector.tensor_scalar(s_mr, bM1[0:1, :], 1.0 / D, None, ALU.mult)
            nc.vector.tensor_scalar(s_mr, bM1[0:1, :], 1.0 / D, None, ALU.mult)
            nc.vector.tensor_scalar(s_er, bM2[0:1, :], 1.0 / D, None, ALU.mult)
            nc.vector.tensor_scalar(s_er, bM2[0:1, :], 1.0 / D, None, ALU.mult)
            nc.vector.tensor_mul(s_vr2, s_mr, s_mr)
            nc.vector.tensor_mul(s_vr2, s_mr, s_mr)
            nc.vector.tensor_sub(s_vr, s_er, s_vr2)
            tvv_ = c.inc("V", nc.vector.tensor_sub(s_vr, s_er, s_vr2))
            c.wait("A", tvv_)
            nc.scalar.activation(s_vr3, s_vr, AF.Ln, bias=s_eps)
            tra = c.inc("A", nc.scalar.activation(s_arw, s_vr3, AF.Exp, scale=-0.5))
            c.wait("V", tra)
            nc.vector.tensor_mul(s_brw, s_mr, s_arw)
            nc.vector.tensor_scalar_mul(s_brw2, s_brw, -1.0)
            return c.inc("V", nc.vector.tensor_scalar_mul(s_brw2, s_brw, -1.0))

        for g in range(4):
            asl = g % 2
            depao = last.get(("aoq", asl))
            aodeps = [depao] if depao else []
            if tok_cc1 is not None and g == 0:
                aodeps += [tok_cc1, tok_cc]
            c.dma("G", f"aoq{asl}",
                  s_aoq[asl].rearrange("p (a t) -> p a t", t=512)[:, 0::2, :],
                  d_all1.ap().rearrange("(a p) t -> p a t", p=128)
                  [:, :, ds((vhg + 4 * g) * 512, 512)],
                  deps=aodeps)
            tao = c.dma("G", f"aoq{asl}",
                        s_aoq[asl].rearrange("p (a t) -> p a t", t=512)[:, 1::2, :],
                        d_all2.ap().rearrange("(a p) t -> p a t", p=128)
                        [:, :, ds((vhg + 4 * g) * 512, 512)],
                        deps=aodeps)
            tu = None
            for oc in range(IC):
                sl = oc % 2
                dep = last.get(("xoq", sl))
                txo = c.dma("Y", f"xoq{sl}", s_xoq[sl],
                            d_xo.rearrange("p (a t) -> p a t", t=2048)
                            [:, oc, 512 * g:512 * g + 512],
                            deps=[dep] if dep else [])
                bU = pb[oc % 2]
                c.wait("T", tao)
                c.wait("T", two)
                dep = last.get(("bU", oc % 2))
                if dep:
                    c.wait("T", dep)
                for ic in range(IC):
                    ins = nc.tensor.matmul(
                        bU, s_wo[:, 1024 * ic + 128 * oc:1024 * ic + 128 * oc + 128],
                        s_aoq[asl][:, 512 * ic:512 * ic + 512],
                        start=(ic == 0), stop=(ic == IC - 1))
                tm = c.inc("T", ins)
                c.wait("V", tm)
                c.wait("V", txo)
                tu = c.inc("V", nc.vector.scalar_tensor_tensor(
                    s_u[:, 512 * oc:512 * oc + 512], bU, s_bo[:, oc:oc + 1],
                    s_xoq[sl], ALU.add, ALU.add))
                last[("bU", oc % 2)] = tu
                last[("xoq", sl)] = tu
                dep = last.get(("sqB", oc % 2))
                if dep:
                    c.wait("A", dep)
                c.wait("A", tu)
                tsq = c.inc("A", nc.scalar.activation(
                    s_sqB[oc % 2], s_u[:, 512 * oc:512 * oc + 512], AF.Square))
                c.wait("T", tu)
                c.wait("T", tsq)
                if oc == 0:
                    dep = last.get(("pbst",))
                    if dep:
                        c.wait("T", dep)
                nc.tensor.matmul(pb[2][0:1, :], s_onc,
                                 s_u[:, 512 * oc:512 * oc + 512],
                                 start=(oc == 0), stop=(oc == IC - 1))
                tst = c.inc("T", nc.tensor.matmul(
                    pb[3][0:1, :], s_onc, s_sqB[oc % 2],
                    start=(oc == 0), stop=(oc == IC - 1)))
                last[("sqB", oc % 2)] = tst
            last[("aoq", asl)] = tm
            trows = ln_rows(pb[2], pb[3], tst)
            c.wait("T", trows)
            nc.tensor.matmul(pb[2], s_onf, s_arw, start=True, stop=True)
            nc.tensor.matmul(pb[2], s_onf, s_arw, start=True, stop=True)
            tbc = c.mm2(nc, pb[3], s_onf, s_brw2, start=True, stop=True)
            tx1 = None
            for oc in range(IC):
                c.wait("V", tbc)
                nc.vector.tensor_mul(s_sqB[oc % 2],
                                     s_u[:, 512 * oc:512 * oc + 512], pb[2])
                nc.vector.tensor_add(s_u[:, 512 * oc:512 * oc + 512],
                                     s_sqB[oc % 2], pb[3])
                nc.vector.tensor_add(s_u[:, 512 * oc:512 * oc + 512],
                                     s_sqB[oc % 2], pb[3])
                tx1 = c.inc("V", nc.vector.tensor_copy(
                    s_x1b[:, 512 * oc:512 * oc + 512],
                    s_u[:, 512 * oc:512 * oc + 512]))
            last[("pbst",)] = tx1

            if "dbg_x1" in dbg_outs and g == 0:
                c.dma("Y", "dbg", dbg_outs["dbg_x1"], s_u, deps=[tx1])
            if "dbg_ao" in dbg_outs and g == 0:
                c.dma("Y", "dbg", dbg_outs["dbg_ao"], s_aoq, deps=[tao])

            th = None
            for oc in range(OCH):
                sl = oc % 3
                if ("w1s", oc) in pf and g == 0:
                    t1 = pf[("w1s", oc)]
                    t2 = pf[("w2s", oc)]
                else:
                    dep = last.get(("w1s", sl))
                    t1 = c.dma("G", f"w1s{sl}", s_w1s[sl], d_w1[oc],
                               deps=[dep] if dep else [])
                    dep = last.get(("w2s", sl))
                    t2 = c.dma("G", f"w2s{sl}", s_w2s[sl], d_w2[oc],
                               deps=[dep] if dep else [])
                b1 = pb[4 + (oc % 2) * 2]
                b2 = pb[5 + (oc % 2) * 2]
                c.wait("T", t1)
                c.wait("T", tx1)
                dep = last.get(("b1", oc % 2))
                if dep:
                    c.wait("T", dep)
                for ic in range(IC):
                    ins = nc.tensor.matmul(
                        b1, s_w1s[sl][:, 128 * ic:128 * ic + 128],
                        s_x1b[:, 512 * ic:512 * ic + 512],
                        start=(ic == 0), stop=(ic == IC - 1))
                tm1 = c.inc("T", ins)
                c.wait("T", t2)
                for ic in range(IC):
                    ins = nc.tensor.matmul(
                        b2, s_w2s[sl][:, 128 * ic:128 * ic + 128],
                        s_x1b[:, 512 * ic:512 * ic + 512],
                        start=(ic == 0), stop=(ic == IC - 1))
                tm2 = c.inc("T", ins)
                c.wait("A", tm1)
                tsi = c.inc("A", nc.scalar.activation(
                    s_h[:, 512 * oc:512 * oc + 512], b1, AF.Silu,
                    scale=s_s1[:, oc:oc + 1]))
                c.wait("V", tm2)
                c.wait("V", tsi)
                th = c.inc("V", nc.vector.scalar_tensor_tensor(
                    s_h[:, 512 * oc:512 * oc + 512], b2, s_s2[:, oc:oc + 1],
                    s_h[:, 512 * oc:512 * oc + 512], ALU.mult, ALU.mult))
                last[("b1", oc % 2)] = th
                last[("w1s", sl)] = tm1
                last[("w2s", sl)] = tm2

            def hstage(dsrc, src_ap, dst_ap, scol, noc, wait_tok, resid=False, wtag_pf=False):
                tg_ = None
                for oc in range(noc):
                    sl = oc % 3
                    if ("hws", oc) in pf and wtag_pf:
                        tw = pf.pop(("hws", oc))
                    else:
                        dep = last.get(("hws", sl))
                        tw = c.dma("G", f"hws{sl}", s_hws[sl], dsrc[oc],
                                   deps=[dep] if dep else [])
                    bk = pb[4 + oc % 2]
                    c.wait("T", tw)
                    c.wait("T", wait_tok)
                    dep = last.get(("bh", oc % 2))
                    if dep:
                        c.wait("T", dep)
                    for icp in range(OCH):
                        ins = nc.tensor.matmul(
                            bk, s_hws[sl][:, 128 * icp:128 * icp + 128],
                            src_ap[:, 512 * icp:512 * icp + 512],
                            start=(icp == 0), stop=(icp == OCH - 1))
                    tm = c.inc("T", ins)
                    if not resid:
                        c.wait("A", tm)
                        tg_ = c.inc("A", nc.scalar.activation(
                            dst_ap[:, 512 * oc:512 * oc + 512], bk, AF.Gelu,
                            scale=scol[:, oc:oc + 1]))
                    else:
                        c.wait("V", tm)
                        tg_ = c.inc("V", nc.vector.scalar_tensor_tensor(
                            dst_ap[:, 512 * oc:512 * oc + 512], bk,
                            scol[:, oc:oc + 1],
                            s_u[:, 512 * oc:512 * oc + 512],
                            ALU.mult, ALU.add))
                        dep = last.get(("sqB", oc % 2))
                        if dep:
                            c.wait("A", dep)
                        c.wait("A", tg_)
                        tsq = c.inc("A", nc.scalar.activation(
                            s_sqB[oc % 2],
                            dst_ap[:, 512 * oc:512 * oc + 512], AF.Square))
                        c.wait("T", tg_)
                        c.wait("T", tsq)
                        if oc == 0:
                            dep2 = last.get(("pbst",))
                            if dep2:
                                c.wait("T", dep2)
                        nc.tensor.matmul(
                            pb[2][0:1, :], s_onc,
                            dst_ap[:, 512 * oc:512 * oc + 512],
                            start=(oc == 0), stop=(oc == noc - 1))
                        tst2 = c.inc("T", nc.tensor.matmul(
                            pb[3][0:1, :], s_onc, s_sqB[oc % 2],
                            start=(oc == 0), stop=(oc == noc - 1)))
                        last[("sqB", oc % 2)] = tst2
                        last[("lnst",)] = tst2
                    last[("bh", oc % 2)] = tg_
                    last[("hws", sl)] = tm
                return tg_

            tgel1 = hstage(d_h1, s_h, s_h2, s_s3, OCH, th, wtag_pf=(g == 0))
            tgel2 = hstage(d_h2, s_h2, s_h, s_s4, OCH, tgel1)
            tu2 = hstage(d_w3, s_h, s_u2, s_s5, IC, tgel2, resid=True)

            trows = ln_rows(pb[2], pb[3], last[("lnst",)])
            c.wait("T", trows)
            nc.tensor.matmul(pb[2], s_onf, s_arw, start=True, stop=True)
            nc.tensor.matmul(pb[2], s_onf, s_arw, start=True, stop=True)
            tbc = c.mm2(nc, pb[3], s_onf, s_brw2, start=True, stop=True)
            tfin = None
            for oc in range(IC):
                c.wait("V", tbc)
                nc.vector.tensor_mul(s_sqB[oc % 2],
                                     s_u2[:, 512 * oc:512 * oc + 512], pb[2])
                nc.vector.tensor_add(s_u2[:, 512 * oc:512 * oc + 512],
                                     s_sqB[oc % 2], pb[3])
                tfin = c.inc("V", nc.vector.tensor_add(
                    s_u2[:, 512 * oc:512 * oc + 512],
                    s_sqB[oc % 2], pb[3]))
            last[("pbst",)] = tfin
            tout = c.dma("Y", "out",
                         d_out.rearrange("p (a t) -> p a t", t=2048)
                         [:, :, 512 * g:512 * g + 512],
                         s_u2.rearrange("p (a t) -> p a t", t=512),
                         deps=[tfin])
            c.wait("V", tout)
            for _sl in range(3):
                last[("w1s", _sl)] = tout
                last[("w2s", _sl)] = tout
        A.reset(markB)
    else:
        c.wait("Y", tv0)
        c.dma("Y", "out", d_out[:, 0:1024], s_tq[:, 0:1024], deps=[tok_cst])

    c.wait("Y", ("out", c.cnt.get("out", 0)))
    if 2 <= STAGE < 3:
        c.wait("Y", ("cat0", c.cnt.get("cat0", 0)))
        c.wait("Y", ("cat1", c.cnt.get("cat1", 0)))
    if dbg_outs:
        c.wait("Y", ("dbg", c.cnt.get("dbg", 0)))
        c.wait("G", ("dbg", c.cnt.get("dbg", 0)))
    return nc


_CACHE = {}


def _host_tables():
    tq = np.zeros((NQT, 128, NCH), np.float32)
    dj = np.arange(128).reshape(128, 1)
    cc = np.arange(NCH).reshape(1, NCH)
    j = 128 * cc + dj
    for qt in range(NQT):
        i0 = 512 * qt
        i1 = i0 + 511
        tq[qt] = np.where(cc < 4 * qt, j - i0,
                          np.where(cc >= 4 * qt + 4, i1 - j, 0.0))
    tqp = np.ascontiguousarray(tq.transpose(1, 0, 2).reshape(128, NQT * NCH))
    d0 = np.zeros((4, 128, QT), np.float32)
    di = np.arange(QT).reshape(1, QT)
    for cp in range(4):
        d0[cp] = np.abs(128 * cp + np.arange(128).reshape(128, 1) - di)
    d0p = np.ascontiguousarray(d0.transpose(1, 0, 2).reshape(128, 4 * QT))
    return tqp, d0p


def _pcol(v, p=128):
    v = np.asarray(v, np.float32).reshape(-1)
    return np.ascontiguousarray(v.reshape(-1, p).T)


def _wblocks(wt, och):
    I, O = wt.shape
    icn = I // 128
    r = wt.reshape(icn, 128, och, 128).transpose(2, 1, 0, 3)
    return np.ascontiguousarray(r.reshape(och, 128, icn * 128))


def _prows(wt):
    I, O = wt.shape
    icn = I // 128
    return np.ascontiguousarray(
        wt.reshape(icn, 128, O).transpose(1, 0, 2).reshape(128, icn * O))


def kernel(L0_attn_x, L0_attn_qkv_w, L0_attn_qkv_b, L0_attn_out_w, L0_attn_out_b,
           L0_attn_decay_alpha, L0_ffn_w1_w, L0_ffn_w1_s, L0_ffn_w2_w, L0_ffn_w2_s,
           L0_ffn_h1_w, L0_ffn_h1_s, L0_ffn_h2_w, L0_ffn_h2_s, L0_ffn_w3_w,
           L0_ffn_w3_s, _debug=(), _stage=None):
    global STAGE
    if _stage is not None:
        STAGE = _stage
    x = np.asarray(L0_attn_x, np.float32)
    qkv_w = np.asarray(L0_attn_qkv_w, np.float32)
    qkv_b = np.asarray(L0_attn_qkv_b, np.float32)
    out_w = np.asarray(L0_attn_out_w, np.float32)
    out_b = np.asarray(L0_attn_out_b, np.float32)
    alpha = np.asarray(L0_attn_decay_alpha, np.float32)

    key = ("nc", STAGE, tuple(sorted(d[0] for d in _debug)))
    if key not in _CACHE:
        _CACHE[key] = build_nc(debug=_debug)
    nc = _CACHE[key]

    tqp, d0p = _host_tables()
    shared = {
        "wop": _prows(np.ascontiguousarray(out_w.T)),
        "bop": _pcol(out_b),
        "w1p": _wblocks(np.ascontiguousarray(np.asarray(L0_ffn_w1_w, np.float32).T), OCH),
        "w2p": _wblocks(np.ascontiguousarray(np.asarray(L0_ffn_w2_w, np.float32).T), OCH),
        "h1p": _wblocks(np.ascontiguousarray(np.asarray(L0_ffn_h1_w, np.float32).T), OCH),
        "h2p": _wblocks(np.ascontiguousarray(np.asarray(L0_ffn_h2_w, np.float32).T), OCH),
        "w3p": _wblocks(np.ascontiguousarray(np.asarray(L0_ffn_w3_w, np.float32).T), 8),
        "s1p": _pcol(L0_ffn_w1_s), "s2p": _pcol(L0_ffn_w2_s),
        "s3p": _pcol(L0_ffn_h1_s), "s4p": _pcol(L0_ffn_h2_s),
        "s5p": _pcol(L0_ffn_w3_s),
        "tqp": tqp, "d0p": d0p,
        "dqp": np.arange(QT, dtype=np.float32).reshape(1, QT),
        "onp": np.ones((1, 128), np.float32),
        "dqbp": np.tile(np.arange(QT, dtype=np.float32), (128, 1)),
        "eyp": np.concatenate([np.eye(2, dtype=np.float32), np.zeros((2, 510), np.float32)], axis=1),
    }
    xtp = [_prows(np.ascontiguousarray(x[b].T)) for b in range(2)]

    in_maps = []
    for core in range(8):
        b, hg = core // 4, core % 4
        xo = np.concatenate([x[b, 2048 * g + 512 * hg:2048 * g + 512 * hg + 512]
                             for g in range(4)], axis=0)
        m = dict(shared)
        m["xtp"] = xtp[b]
        m["xtop"] = _prows(np.ascontiguousarray(xo.T))
        m["wqp"] = _prows(np.ascontiguousarray(
            qkv_w[256 * hg:256 * hg + 256, :].T))
        m["wkp"] = _prows(np.ascontiguousarray(
            qkv_w[1024 + 256 * hg:1024 + 256 * hg + 256, :].T))
        m["wvp"] = _prows(np.ascontiguousarray(
            qkv_w[2048 + 256 * hg:2048 + 256 * hg + 256, :].T))
        bq = qkv_b[256 * hg:256 * hg + 256].reshape(2, 128)
        bk = qkv_b[1024 + 256 * hg:1024 + 256 * hg + 256].reshape(2, 128)
        m["bqkp"] = np.ascontiguousarray(np.stack(
            [bq[0], bq[1], bk[0], bk[1]], axis=1))
        m["bvp"] = np.ascontiguousarray(
            qkv_b[2048 + 256 * hg:2048 + 256 * hg + 256].reshape(1, 256))
        m["alp"] = np.ascontiguousarray(alpha[4 * hg:4 * hg + 4].reshape(1, 4))
        m["hgp"] = np.array([[hg]], np.int32)
        in_maps.append(m)

    res = run_bass_kernel_spmd(nc, in_maps, core_ids=list(range(8)))

    out = np.empty((2, 8192, 1024), np.float32)
    for core in range(8):
        b, hg = core // 4, core % 4
        o = np.asarray(res.results[core]["outp"])
        o = o.reshape(128, 8, 2048).transpose(1, 0, 2).reshape(1024, 2048)
        for g in range(4):
            out[b, 2048 * g + 512 * hg:2048 * g + 512 * hg + 512, :] = \
                o[:, 512 * g:512 * g + 512].T
    if _debug:
        return out, [dict(r) for r in res.results]
    return out


# version 08
# speedup vs baseline: 1.0673x; 1.0540x over previous
import numpy as np
import ml_dtypes
import concourse.bass as bass
import concourse.mybir as mybir
from concourse.bass_utils import run_bass_kernel_spmd
from concourse.expressions import RuntimeValue

F32 = mybir.dt.float32
BF16 = mybir.dt.bfloat16
I32 = mybir.dt.int32
AF = mybir.ActivationFunctionType
ALU = mybir.AluOpType
ds = bass.ds

D = 1024
T = 8192
IC = 8
QT = 512
NQT = T // QT
NCH = T // 128
DH = 4096
OCH = DH // 128
NLN8 = -2.0794415416798357

STAGE = 4


class Ctx:

    def __init__(self, nc):
        self.nc = nc
        self.eng = {"T": nc.tensor, "A": nc.scalar, "V": nc.vector,
                    "G": nc.gpsimd, "Y": nc.sync}
        self.sem = {k: nc.alloc_semaphore("sm" + k) for k in self.eng}
        self.pool = [nc.alloc_semaphore(f"sp{i}") for i in range(56)]
        self.cnt = {k: 0 for k in self.eng}
        self.water = {}
        nums = [s.num for s in list(self.sem.values()) + self.pool]
        assert max(nums) - min(nums) + 1 == len(nums), "sem range not contiguous"
        nc.gpsimd.sem_clear(range(min(nums), max(nums) + 1))
        nc.all_engine_barrier()

    def dsem(self, name):
        if name not in self.sem:
            self.sem[name] = self.pool.pop(0)
            self.cnt[name] = 0
        return self.sem[name]

    def wait(self, who, tok):
        if tok is None:
            return
        name, val = tok
        if name == who or val <= 0:
            return
        if self.water.get((who, name), 0) >= val:
            return
        self.water[(who, name)] = val
        self.eng[who].wait_ge(self.sem[name], val)

    def mm2(self, nc, *args, **kw):
        nc.tensor.matmul(*args, **kw)
        return self.inc("T", nc.tensor.matmul(*args, **kw))

    def v2(self, emit):
        emit()
        return self.inc("V", emit())

    def inc(self, who, ins):
        self.cnt[who] += 1
        ins.then_inc(self.sem[who], 1)
        return (who, self.cnt[who])

    def dma(self, who, name, dst, src, deps=()):
        for d in deps:
            self.wait(who, d)
        s = self.dsem(name)
        self.eng[who].dma_start(dst, src).then_inc(s, 16)
        self.cnt[name] = self.cnt.get(name, 0) + 16
        return (name, self.cnt[name])

    def join(self, engines=("T", "A", "V", "G", "Y")):
        toks = [(k, self.cnt[k]) for k in engines]
        for who in engines:
            for t in toks:
                self.wait(who, t)


class Arena:
    def __init__(self, nc, nbytes):
        self.ap = nc.alloc_sbuf_tensor("arena", [128, nbytes], mybir.dt.int8).ap()
        self.off = 0
        self.nbytes = nbytes

    def alloc(self, cols, dtype):
        esz = mybir.dt.size(dtype)
        n = cols * esz
        off = (self.off + 31) // 32 * 32
        assert off + n <= self.nbytes, f"arena overflow: {off + n} > {self.nbytes}"
        self.off = off + n
        return self.ap[:, off:off + n].bitcast(dtype)

    def mark(self):
        return self.off

    def reset(self, mark):
        self.off = mark

    def at(self, off, cols, dtype):
        n = cols * mybir.dt.size(dtype)
        assert off % 32 == 0 and off + n <= self.nbytes
        return self.ap[:, off:off + n].bitcast(dtype)


def build_nc(debug=()):
    nc = bass.Bass("TRN2", target_bir_lowering=False)
    c = Ctx(nc)

    def din(name, shape, dt=F32):
        return nc.dram_tensor(name, shape, dt, kind="ExternalInput").ap()

    d_xt = din("xtp", [128, IC * T])
    d_xo = din("xtop", [128, IC * 2048])
    d_wq = din("wqp", [128, IC * 256])
    d_wk = din("wkp", [128, IC * 256])
    d_wv = din("wvp", [128, IC * 256])
    d_bqk = din("bqkp", [128, 4])
    d_bv = din("bvp", [1, 256])
    d_al = din("alp", [1, 4])
    d_wo = din("wop", [128, IC * D])
    d_bo = din("bop", [128, 8])
    d_w1 = din("w1p", [OCH, 128, IC * 128])
    d_w2 = din("w2p", [OCH, 128, IC * 128])
    d_h1 = din("h1p", [OCH, 128, OCH * 128])
    d_h2 = din("h2p", [OCH, 128, OCH * 128])
    d_w3 = din("w3p", [8, 128, OCH * 128])
    d_s1 = din("s1p", [128, OCH])
    d_s2 = din("s2p", [128, OCH])
    d_s3 = din("s3p", [128, OCH])
    d_s4 = din("s4p", [128, OCH])
    d_s5 = din("s5p", [128, 8])
    d_tq = din("tqp", [128, NQT * NCH])
    d_d0 = din("d0p", [128, 4 * QT])
    d_dq = din("dqp", [1, QT])
    d_dqb = din("dqbp", [128, QT])
    d_on = din("onp", [1, 128])
    d_ey = din("eyp", [2, 512])
    d_hg = din("hgp", [1, 1], I32)

    d_cat = nc.dram_tensor("attn_cat", [256, T], BF16, kind="Internal")
    d_all1 = nc.dram_tensor("attn_all1", [512, T], BF16, kind="Internal")
    d_all2 = nc.dram_tensor("attn_all2", [512, T], BF16, kind="Internal")
    d_out = nc.dram_tensor("outp", [128, 8 * 2048], F32, kind="ExternalOutput").ap()

    dbg_outs = {}
    for name, shape, dt in debug:
        dbg_outs[name] = nc.dram_tensor(name, shape, dt, kind="ExternalOutput").ap()

    def sb(name, shape, dt):
        return nc.alloc_sbuf_tensor(name, shape, dt).ap()

    s_tq = sb("s_tq", [128, NQT * NCH], F32)
    s_d0 = sb("s_d0", [128, 4 * QT], F32)
    s_dq = sb("s_dq", [1, QT], F32)
    s_dqb = sb("s_dqb", [128, QT], F32)
    s_onf = sb("s_onf", [1, 128], F32)
    s_onb = sb("s_onb", [1, 128], BF16)
    s_ey = sb("s_ey", [2, 512], F32)
    s_bqk = sb("s_bqk", [128, 4], F32)
    s_bv = sb("s_bv", [1, 256], F32)
    s_al = sb("s_al", [1, 4], F32)
    s_hgi = sb("s_hgi", [1, 1], I32)
    s_bo = sb("s_bo", [128, 8], F32)
    s_s1 = sb("s_s1", [128, OCH], F32)
    s_s2 = sb("s_s2", [128, OCH], F32)
    s_s3 = sb("s_s3", [128, OCH], F32)
    s_s4 = sb("s_s4", [128, OCH], F32)
    s_s5 = sb("s_s5", [128, 8], F32)
    s_ar = sb("s_ar", [1, 512], F32)
    s_nar = sb("s_nar", [1, 4], F32)
    s_m511 = sb("s_m511", [1, 4], F32)
    s_ab = sb("s_ab", [128, 4], F32)
    s_n8ab = sb("s_n8ab", [128, 4], F32)
    s_m511b = sb("s_m511b", [128, 4], F32)
    s_ur = sb("s_ur", [1, 512], F32)
    s_nub = sb("s_nub", [128, 4], F32)
    s_k2r = sb("s_k2r", [1, 4], F32)
    s_q2r = sb("s_q2r", [1, 4], F32)
    s_k2mP = [sb("s_k2m0", [2, 1], F32), sb("s_k2m1", [2, 1], F32)]
    s_q2mP = [sb("s_q2m0", [2, 1], F32), sb("s_q2m1", [2, 1], F32)]
    s_ind2 = sb("s_ind2", [128, 2], BF16)
    s_onc = sb("s_onc", [128, 1], F32)
    s_eps = sb("s_eps", [1, 1], F32)
    s_one = sb("s_one", [1, 1], F32)
    s_nl8 = sb("s_nl8", [1, 1], F32)
    s_t14a = sb("s_t14a", [1, 4], F32)
    s_t14b = sb("s_t14b", [1, 4], F32)
    s_bsl = [sb(f"s_bsl{h}", [128, NCH], F32) for h in range(2)]

    pb = [nc.alloc_psum_tensor(f"pb{i}", [128, 512], F32).ap() for i in range(8)]

    A = Arena(nc, 182 * 1024)
    last = {}

    for dst, src in [(s_tq, d_tq), (s_d0, d_d0), (s_dq, d_dq), (s_dqb, d_dqb), (s_onf, d_on),
                     (s_ey, d_ey), (s_bqk, d_bqk), (s_bv, d_bv), (s_al, d_al),
                     (s_hgi, d_hg), (s_bo, d_bo), (s_s1, d_s1), (s_s2, d_s2),
                     (s_s3, d_s3), (s_s4, d_s4), (s_s5, d_s5)]:
        c.dma("Y", "cst", dst, src)
    tok_cst = ("cst", c.cnt["cst"])
    tok_cstg = c.dma("G", "cstg", s_onb, d_on)

    c.wait("V", tok_cst)
    c.wait("V", tok_cstg)
    nc.vector.memset(s_onc, 1.0)
    nc.vector.memset(s_eps, 1e-5)
    nc.vector.memset(s_one, 1.0)
    nc.vector.memset(s_nl8, NLN8)
    nc.vector.memset(s_ind2, 0.0)
    nc.vector.memset(s_ind2[0:64, 0:1], 1.0)
    nc.vector.memset(s_ind2[64:128, 1:2], 1.0)
    nc.vector.memset(s_ur, 0.0)
    nc.vector.memset(s_ar, 0.0)
    nc.vector.memset(s_k2mP[0], 0.0)
    nc.vector.memset(s_k2mP[1], 0.0)
    nc.vector.memset(s_q2mP[0], 0.0)
    tv0 = c.inc("V", nc.vector.memset(s_q2mP[1], 0.0))

    c.wait("A", tok_cst)
    c.wait("A", tv0)
    nc.scalar.activation(s_t14a, s_al, AF.Abs)
    nc.scalar.activation(s_t14a, s_t14a, AF.Exp, scale=-1.0)
    nc.scalar.activation(s_t14a, s_t14a, AF.Ln, bias=s_one)
    ta0 = c.inc("A", nc.scalar.activation(s_t14b, s_al, AF.Relu))
    c.wait("V", ta0)
    nc.vector.tensor_add(s_ar[0:1, 0:4], s_t14a, s_t14b)
    nc.vector.tensor_scalar_mul(s_nar, s_ar[0:1, 0:4], -1.0)
    tv1 = c.inc("V", nc.vector.tensor_scalar_mul(s_m511, s_ar[0:1, 0:4], -511.0))
    c.wait("T", tv1)
    tmm0 = c.mm2(nc, pb[0], s_onf, s_ar[0:1, :], start=True, stop=True)
    c.wait("V", tmm0)
    nc.vector.tensor_copy(s_ab, pb[0][:, 0:4])
    nc.vector.tensor_copy(s_ab, pb[0][:, 0:4])
    nc.vector.tensor_scalar_mul(s_n8ab, s_ab, -8.0)
    nc.vector.tensor_scalar_mul(s_n8ab, s_ab, -8.0)
    nc.vector.tensor_scalar_mul(s_m511b, s_ab, -511.0)
    tv2 = c.inc("V", nc.vector.tensor_scalar_mul(s_m511b, s_ab, -511.0))
    last[("pb", 0)] = tv2

    c.wait("G", tok_cst)
    rhg = nc.gpsimd.alloc_register("rhg")
    nc.gpsimd.reg_load(rhg, s_hgi)
    vhg = RuntimeValue(rhg, min_val=0, max_val=3)

    pairst = {}

    def proj_all():
        c.join()
        s_KT = [A.alloc(T, BF16) for _ in range(2)]
        s_QT = [A.alloc(T, BF16) for _ in range(2)]
        s_V = [[A.alloc(NCH * 65, BF16) for _ in range(2)] for _ in range(2)]
        pairst["KT"] = s_KT
        pairst["QT"] = s_QT
        pairst["V"] = s_V
        m1 = A.mark()
        s_xt = [A.alloc(IC * 512, BF16) for _ in range(2)]
        s_wqb = A.alloc(IC * 256, BF16)
        s_wkb = A.alloc(IC * 256, BF16)
        s_wvb = A.alloc(IC * 256, BF16)
        s_wv260 = A.alloc(IC * 260, BF16)
        s_bv260 = A.alloc(260, BF16)
        s_sq = A.alloc(512, BF16)
        s_sq2 = A.alloc(512, BF16)
        s_r21 = A.alloc(1, F32)
        s_r22 = A.alloc(1, F32)

        twq = c.dma("G", "wq0", s_wqb.rearrange("p (a o) -> p a o", o=256),
                    d_wq.rearrange("p (a o) -> p a o", o=256))
        twk = c.dma("G", "wk0", s_wkb.rearrange("p (a o) -> p a o", o=256),
                    d_wk.rearrange("p (a o) -> p a o", o=256))
        twv = c.dma("G", "wv0", s_wvb.rearrange("p (a o) -> p a o", o=256),
                    d_wv.rearrange("p (a o) -> p a o", o=256))
        c.wait("V", twv)
        c.wait("V", tok_cst)
        nc.vector.memset(s_wv260, 0.0)
        for ic in range(IC):
            for hl in range(4):
                nc.vector.tensor_copy(
                    s_wv260[:, 260 * ic + 65 * hl:260 * ic + 65 * hl + 64],
                    s_wvb[:, 256 * ic + 64 * hl:256 * ic + 64 * hl + 64])
        nc.vector.memset(s_bv260[0:1, 0:260], 0.0)
        for hl in range(4):
            nc.vector.tensor_copy(s_bv260[0:1, 65 * hl:65 * hl + 64],
                                  s_bv[0:1, 64 * hl:64 * hl + 64])
            nc.vector.memset(s_bv260[0:1, 65 * hl + 64:65 * hl + 65], 1.0)
        nc.vector.memset(s_bv260[0:1, 259:260], 1.0)
        tv260 = c.inc("V", nc.vector.memset(s_bv260[0:1, 259:260], 1.0))

        tev = {}
        tvv = None
        for tc in range(16):
            sl = tc % 2
            dep = last.get(("xt", sl))
            tx = c.dma("G", f"x{sl}", s_xt[sl].rearrange("p (a t) -> p a t", t=512),
                       d_xt.rearrange("p (a t) -> p a t", t=T)
                       [:, :, 512 * tc:512 * tc + 512],
                       deps=[dep] if dep else [])
            c.wait("T", tx)
            c.wait("T", twk)
            c.wait("T", twq)
            c.wait("T", tv260)
            for p2 in range(2):
                bK = pb[p2]
                bQ = pb[2 + p2]
                c.wait("T", tev.get(("K", p2)))
                for ic in range(IC):
                    ins = nc.tensor.matmul(
                        bK, s_wkb[:, 256 * ic + 128 * p2:256 * ic + 128 * p2 + 128],
                        s_xt[sl][:, 512 * ic:512 * ic + 512],
                        start=(ic == 0), stop=(ic == IC - 1))
                tk = c.inc("T", ins)
                c.wait("T", tev.get(("Q", p2)))
                for ic in range(IC):
                    ins = nc.tensor.matmul(
                        bQ, s_wqb[:, 256 * ic + 128 * p2:256 * ic + 128 * p2 + 128],
                        s_xt[sl][:, 512 * ic:512 * ic + 512],
                        start=(ic == 0), stop=(ic == IC - 1))
                tq_ = c.inc("T", ins)
                c.wait("V", tk)
                tev[("K", p2)] = c.inc("V", nc.vector.tensor_scalar(
                    s_KT[p2][:, 512 * tc:512 * tc + 512], bK,
                    s_bqk[:, 2 + p2:3 + p2], None, ALU.add))
                c.wait("V", tq_)
                tev[("Q", p2)] = c.inc("V", nc.vector.tensor_scalar(
                    s_QT[p2][:, 512 * tc:512 * tc + 512], bQ,
                    s_bqk[:, p2:p2 + 1], None, ALU.add))
            for ts in range(4):
                bV = pb[4 + ts % 2]
                kc = 4 * tc + ts
                c.wait("T", tev.get(("V", ts % 2)))
                for ic in range(IC):
                    nc.tensor.matmul(
                        bV[:, 0:260],
                        s_xt[sl][:, 512 * ic + 128 * ts:512 * ic + 128 * ts + 128],
                        s_wv260[:, 260 * ic:260 * ic + 260],
                        start=(ic == 0), stop=False)
                tvv = c.inc("T", nc.tensor.matmul(
                    bV[:, 0:260], s_onb, s_bv260[0:1, 0:260],
                    start=False, stop=True))
                c.wait("V", tvv)
                for p2 in range(2):
                    for h in range(2):
                        ins = nc.vector.tensor_copy(
                            s_V[p2][h][:, 65 * kc:65 * kc + 65],
                            bV[:, 130 * p2 + 65 * h:130 * p2 + 65 * h + 65])
                tev[("V", ts % 2)] = c.inc("V", ins)
            last[("xt", sl)] = tvv

        for key in (("K", 0), ("K", 1), ("Q", 0), ("Q", 1)):
            c.wait("A", tev[key])
        stat_src = []
        for p2 in range(2):
            stat_src += [(s_KT[p2], s_k2mP[p2], kc2) for kc2 in range(16)]
            stat_src += [(s_QT[p2], s_q2mP[p2], kc2) for kc2 in range(16)]
        sqs = [s_sq, s_sq2]
        NST = len(stat_src)
        tvm = None
        tsq_l = {}

        def stat_sq(i):
            srcb, _, kc2 = stat_src[i]
            dep = last.get(("stsq", i % 2))
            if dep:
                c.wait("A", dep)
            return c.inc("A", nc.scalar.activation(
                sqs[i % 2], srcb[:, 512 * kc2:512 * kc2 + 512], AF.Square))

        tsq_l[0] = stat_sq(0)
        for i in range(NST):
            bnk = pb[6] if i % 2 == 0 else pb[7]
            c.wait("T", tsq_l[i])
            c.wait("T", tv0)
            dep = last.get(("pbst6", i % 2))
            if dep:
                c.wait("T", dep)
            tm = c.mm2(nc, bnk[0:2, :], s_ind2, sqs[i % 2],
                       start=True, stop=True)
            last[("stsq", i % 2)] = tm
            if i + 1 < NST:
                tsq_l[i + 1] = stat_sq(i + 1)
            dstm = stat_src[i][1]
            c.wait("V", tm)
            if tvm is not None:
                c.wait("V", tvm)
            nc.vector.reduce_max(s_r21[0:2, 0:1], bnk[0:2, :],
                                 axis=mybir.AxisListType.X)
            nc.vector.reduce_max(s_r21[0:2, 0:1], bnk[0:2, :],
                                 axis=mybir.AxisListType.X)
            nc.vector.tensor_max(s_r22[0:2, 0:1], dstm, s_r21[0:2, 0:1])
            nc.vector.tensor_max(s_r22[0:2, 0:1], dstm, s_r21[0:2, 0:1])
            nc.vector.tensor_copy(dstm, s_r22[0:2, 0:1])
            tvm = c.inc("V", nc.vector.tensor_copy(dstm, s_r22[0:2, 0:1]))
            last[("pbst6", i % 2)] = tvm
        last[("pb", 6)] = tvm
        c.wait("T", tvm)

        for p2 in range(2):
            dep = last.get(("pb", 6))
            if dep:
                c.wait("T", dep)
            tk2 = c.mm2(nc, pb[6][0:1, 0:512], s_k2mP[p2], s_ey,
                        start=True, stop=True)
            c.wait("V", tk2)
            tvk = c.v2(lambda p2=p2: nc.vector.tensor_copy(
                s_k2r[0:1, 2 * p2:2 * p2 + 2], pb[6][0:1, 0:2]))
            c.wait("T", tvk)
            tq2 = c.mm2(nc, pb[6][0:1, 0:512], s_q2mP[p2], s_ey,
                        start=True, stop=True)
            c.wait("V", tq2)
            nc.vector.tensor_copy(s_q2r[0:1, 2 * p2:2 * p2 + 2], pb[6][0:1, 0:2])
            nc.vector.tensor_copy(s_q2r[0:1, 2 * p2:2 * p2 + 2], pb[6][0:1, 0:2])
            nc.vector.tensor_mul(
                s_t14a[0:1, 2 * p2:2 * p2 + 2], s_k2r[0:1, 2 * p2:2 * p2 + 2],
                s_q2r[0:1, 2 * p2:2 * p2 + 2])
            tvq = c.inc("V", nc.vector.tensor_mul(
                s_t14a[0:1, 2 * p2:2 * p2 + 2], s_k2r[0:1, 2 * p2:2 * p2 + 2],
                s_q2r[0:1, 2 * p2:2 * p2 + 2]))
            last[("pb", 6)] = tvq
            c.wait("A", tvq)
            nc.scalar.activation(s_t14b[0:1, 2 * p2:2 * p2 + 2],
                                 s_t14a[0:1, 2 * p2:2 * p2 + 2], AF.Ln)
            tua = c.inc("A", nc.scalar.activation(
                s_ur[0:1, 2 * p2:2 * p2 + 2], s_t14b[0:1, 2 * p2:2 * p2 + 2],
                AF.Exp, scale=0.5, bias=s_nl8))
            c.wait("T", tua)
            c.wait("T", tvq)
            tub = c.mm2(nc, pb[6], s_onf, s_ur[0:1, :], start=True, stop=True)
            c.wait("V", tub)
            tnu = c.v2(lambda p2=p2: nc.vector.tensor_scalar(
                s_nub[:, 2 * p2:2 * p2 + 2], pb[6][:, 2 * p2:2 * p2 + 2],
                -1.0, None, ALU.mult))
            last[("pb", 6)] = tnu
            pairst[("tnu", p2)] = tnu
        A.reset(m1)

    def attention_pair(p):
        c.join()
        mark = A.mark()
        s_KT = pairst["KT"][p]
        s_QT = pairst["QT"][p]
        s_V = pairst["V"][p]
        tnu = pairst[("tnu", p)]
        s_ds = [[A.alloc(512, F32) for _ in range(2)] for _ in range(2)]
        s_P = [[A.alloc(512, BF16) for _ in range(14)] for _ in range(2)]
        s_dh = [A.alloc(512, F32) for _ in range(2)]
        s_xc = [A.alloc(512, F32) for _ in range(2)]
        s_x2 = [A.alloc(512, F32) for _ in range(2)]
        s_ao = [A.alloc(512, BF16) for _ in range(2)]
        s_G = [A.alloc(QT, F32) for _ in range(2)]
        s_H = [A.alloc(QT, F32) for _ in range(2)]
        s_lin = [A.alloc(QT, F32) for _ in range(2)]
        s_lt = [A.alloc(QT, F32) for _ in range(2)]

        tgh = None
        for h in range(2):
            hl = 2 * p + h
            c.wait("V", tv2)
            tgt = c.inc("V", nc.vector.tensor_scalar(
                s_x2[h], s_dqb, s_ab[:, hl:hl + 1], None, ALU.mult))
            c.wait("A", tgt)
            nc.scalar.activation(s_G[h], s_x2[h], AF.Exp, scale=-1.0)
            tgh = c.inc("A", nc.scalar.activation(
                s_H[h], s_x2[h], AF.Exp, scale=1.0,
                bias=s_m511b[:, hl:hl + 1]))

        if STAGE < 2:
            A.reset(mark)
            return

        bS = [[pb[0], pb[1]], [pb[2], pb[3]]]
        bL = [pb[4], pb[5]]
        bR = [pb[6], pb[7]]
        c.join()

        NP = 14
        AVLAG = 12
        EPIPOS = 10
        RUNWAY = 6
        pend = None

        def emit_epi_V1(st):
            qt0 = st["qt"]
            hasL, hasR = qt0 > 0, qt0 < NQT - 1
            tvcs = []
            for h in range(2):
                c.wait("V", st["tAV"])
                c.wait("V", tgh)
                if hasL:
                    nc.vector.tensor_mul(s_xc[h][0:65, :], bL[h][0:65, :],
                                         s_G[h][0:65, :])
                    if hasR:
                        nc.vector.tensor_mul(s_x2[h][0:65, :], bR[h][0:65, :],
                                             s_H[h][0:65, :])
                        nc.vector.tensor_add(s_xc[h][0:65, :], s_xc[h][0:65, :],
                                             s_x2[h][0:65, :])
                else:
                    nc.vector.tensor_mul(s_xc[h][0:65, :], bR[h][0:65, :],
                                         s_H[h][0:65, :])
                tvcs.append(c.inc("V", nc.vector.tensor_add(
                    s_xc[h][0:65, :], s_xc[h][0:65, :], s_dh[h][0:65, :])))
            st["tvcs"] = tvcs
            last[("accR", 0)] = tvcs[-1]
            last[("accR", 1)] = tvcs[-1]

        def emit_epi_A(st):
            trs = []
            for h in range(2):
                c.wait("A", st["tvcs"][h])
                nc.scalar.activation(s_lt[h][0:1, :], s_xc[h][64:65, :], AF.Ln)
                trs.append(c.inc("A", nc.scalar.activation(
                    s_lin[h][0:1, :], s_lt[h][0:1, :], AF.Exp, scale=-1.0)))
            st["trs"] = trs

        def emit_epi_T(st):
            tbcs = []
            for h in range(2):
                c.wait("T", st["trs"][h])
                tbcs.append(c.mm2(nc, bL[h][0:64, :], s_onf[0:1, 0:64],
                                  s_lin[h][0:1, :], start=True, stop=True))
            st["tbcs"] = tbcs

        def emit_epi_V2(st):
            qt0 = st["qt"]
            aos = s_ao[qt0 % 2]
            tEp = None
            for h in range(2):
                c.wait("V", st["tbcs"][h])
                dep = last.get(("aost", qt0 % 2))
                if dep:
                    c.wait("V", dep)
                tEp = c.v2(lambda h=h, aos=aos: nc.vector.tensor_mul(
                    aos[64 * h:64 * h + 64, :], s_xc[h][0:64, :],
                    bL[h][0:64, :]))
                last[("accL", h)] = tEp
            tdma = c.dma("Y", f"cat{qt0 % 2}",
                         d_cat.ap()[128 * p:128 * p + 128,
                                    512 * qt0:512 * qt0 + 512],
                         aos, deps=[tEp])
            last[("aost", qt0 % 2)] = tdma

        for qt in range(NQT):
            tb = []
            for h in range(2):
                hl = 2 * p + h
                dep = last.get(("bsl", h))
                if dep:
                    c.wait("V", dep)
                c.wait("V", tnu)
                c.wait("V", tv2)
                tb.append(c.inc("V", nc.vector.tensor_scalar(
                    s_bsl[h], s_tq[:, 64 * qt:64 * qt + 64],
                    s_ab[:, hl:hl + 1], s_nub[:, hl:hl + 1], ALU.mult, ALU.add)))
            if pend is not None:
                emit_epi_V1(pend)

            ndiag0 = 4 * qt
            Lc = [("L", cc, None) for cc in range(0, ndiag0)]
            Rc = [("R", cc, None) for cc in range(ndiag0 + 4, NCH)]
            Dc = [("D", ndiag0 + j, j) for j in range(4)]
            if Lc:
                borrow = "R"
                chunks = Lc[:RUNWAY] + Dc + Lc[RUNWAY:] + Rc
            else:
                borrow = "L"
                chunks = Rc[:RUNWAY] + Dc + Rc[RUNWAY:]
            nch = len(chunks)
            bDA = bR if borrow == "R" else bL
            lastof = {}
            firstof = {}
            for i, (kind, cc, j) in enumerate(chunks):
                lastof[kind] = i
                firstof.setdefault(kind, i)

            def emit_S(i):
                kind, cc, j = chunks[i]
                buf = i % 2
                for h in range(2):
                    dep = last.get(("expS", h, buf))
                    if dep:
                        c.wait("T", dep)
                    ins = nc.tensor.matmul(
                        bS[h][buf],
                        s_KT[64 * h:64 * h + 64, 128 * cc:128 * cc + 128],
                        s_QT[64 * h:64 * h + 64, 512 * qt:512 * qt + 512],
                        start=True, stop=True, tile_position=(64 * h, 0))
                return c.inc("T", ins)

            def emit_mid(i, tS):
                kind, cc, j = chunks[i]
                buf = i % 2
                pslot = i % NP
                if kind == "D":
                    c.wait("V", tS)
                    for h in range(2):
                        hl = 2 * p + h
                        dep = last.get(("expD", h, j % 2))
                        if dep:
                            c.wait("V", dep)
                        ins = nc.vector.scalar_tensor_tensor(
                            s_ds[h][j % 2], s_d0[:, 512 * j:512 * j + 512],
                            s_n8ab[:, hl:hl + 1], bS[h][buf], ALU.mult, ALU.add)
                    tD = c.inc("V", ins)
                    last[("expS", 0, buf)] = tD
                    last[("expS", 1, buf)] = tD
                    c.wait("A", tD)
                    for h in range(2):
                        hl = 2 * p + h
                        dep = last.get(("avP", h, pslot))
                        if dep:
                            c.wait("A", dep)
                        ins = nc.scalar.activation(
                            s_P[h][pslot], s_ds[h][j % 2], AF.Exp, scale=0.125,
                            bias=s_nub[:, hl:hl + 1])
                    tE = c.inc("A", ins)
                    last[("expD", 0, j % 2)] = tE
                    last[("expD", 1, j % 2)] = tE
                else:
                    c.wait("A", tS)
                    c.wait("A", tb[0])
                    c.wait("A", tb[1])
                    for h in range(2):
                        dep = last.get(("avP", h, pslot))
                        if dep:
                            c.wait("A", dep)
                        ins = nc.scalar.activation(
                            s_P[h][pslot], bS[h][buf], AF.Exp, scale=0.125,
                            bias=s_bsl[h][:, cc:cc + 1])
                    tE = c.inc("A", ins)
                    last[("expS", 0, buf)] = tE
                    last[("expS", 1, buf)] = tE
                return tE

            def emit_AV(i, tE):
                kind, cc, j = chunks[i]
                pslot = i % NP
                first = (i == firstof[kind])
                lastk = (i == lastof[kind])
                bACC = bDA if kind == "D" else (bL if kind == "L" else bR)
                c.wait("T", tE)
                for h in range(2):
                    if first:
                        key = ("acc" + borrow, h) if kind == "D" else ("acc" + kind, h)
                        dep = last.get(key)
                        if dep:
                            c.wait("T", dep)
                        if kind == "R" and borrow == "R":
                            dep = last.get(("accDevac", h))
                            if dep:
                                c.wait("T", dep)
                        if kind == "L" and borrow == "L":
                            dep = last.get(("accDevac", h))
                            if dep:
                                c.wait("T", dep)
                    ins = nc.tensor.matmul(
                        bACC[h][0:65, :], s_V[h][:, 65 * cc:65 * cc + 65],
                        s_P[h][pslot], start=first, stop=lastk)
                tAV = c.inc("T", ins)
                last[("avP", 0, pslot)] = tAV
                last[("avP", 1, pslot)] = tAV
                if kind == "D" and lastk:
                    c.wait("V", tAV)
                    for h in range(2):
                        nc.vector.tensor_copy(s_dh[h][0:65, :], bDA[h][0:65, :])
                        ins = nc.vector.tensor_copy(s_dh[h][0:65, :],
                                                    bDA[h][0:65, :])
                    tDh = c.inc("V", ins)
                    last[("accDevac", 0)] = tDh
                    last[("accDevac", 1)] = tDh
                return tAV

            tS_l = {0: emit_S(0)}
            tE_l = {}
            tAV = None
            avdone = 0
            epi_T_done = pend is None
            epi_A_done = pend is None
            for i in range(nch):
                tE_l[i] = emit_mid(i, tS_l[i])
                if i + 1 < nch:
                    tS_l[i + 1] = emit_S(i + 1)
                if not epi_A_done and i >= EPIPOS - 2:
                    emit_epi_A(pend)
                    epi_A_done = True
                if not epi_T_done and i >= EPIPOS:
                    emit_epi_T(pend)
                    emit_epi_V2(pend)
                    epi_T_done = True
                navs = 0 if i < AVLAG else (2 if i >= nch - AVLAG else 1)
                for _ in range(navs):
                    if avdone <= i:
                        tAV = emit_AV(avdone, tE_l[avdone])
                        avdone += 1
            if not epi_T_done:
                if not epi_A_done:
                    emit_epi_A(pend)
                emit_epi_T(pend)
                emit_epi_V2(pend)
            while avdone < nch:
                tAV = emit_AV(avdone, tE_l[avdone])
                avdone += 1
            last[("bsl", 0)] = ("A", c.cnt["A"])
            last[("bsl", 1)] = ("A", c.cnt["A"])
            pend = {"qt": qt, "tAV": tAV}
        emit_epi_V1(pend)
        emit_epi_A(pend)
        emit_epi_T(pend)
        emit_epi_V2(pend)
        pend = None
        A.reset(mark)

    proj_all()
    attention_pair(0)
    tok_cc1 = None
    if STAGE >= 3:
        c.wait("G", ("cat0", c.cnt.get("cat0", 0)))
        c.wait("G", ("cat1", c.cnt.get("cat1", 0)))
        ccsem = c.dsem("cc")
        nc.gpsimd.collective_compute(
            "AllGather", ALU.bypass,
            replica_groups=[[0, 1, 2, 3], [4, 5, 6, 7]],
            ins=[d_cat.ap()[0:128, :].opt()], outs=[d_all1.ap().opt()],
        ).then_inc(ccsem)
        c.cnt["cc"] = c.cnt.get("cc", 0) + 1
        tok_cc1 = ("cc", c.cnt["cc"])
    attention_pair(1)

    tok_cc = None
    if STAGE >= 3:
        c.wait("G", ("cat0", c.cnt.get("cat0", 0)))
        c.wait("G", ("cat1", c.cnt.get("cat1", 0)))
        ccsem = c.dsem("cc")
        nc.gpsimd.collective_compute(
            "AllGather", ALU.bypass,
            replica_groups=[[0, 1, 2, 3], [4, 5, 6, 7]],
            ins=[d_cat.ap()[128:256, :].opt()], outs=[d_all2.ap().opt()],
        ).then_inc(ccsem)
        c.cnt["cc"] = c.cnt.get("cc", 0) + 1
        tok_cc = ("cc", c.cnt["cc"])
    if STAGE >= 4:
        c.join()
        A.reset(0)
        markB = A.mark()
        s_wo = A.alloc(IC * D, BF16)
        s_aoq = [A.alloc(IC * 512, BF16) for _ in range(2)]
        s_xoq = [A.alloc(512, F32) for _ in range(2)]
        s_u = A.alloc(IC * 512, F32)
        s_h = A.alloc(OCH * 512, BF16)
        s_h2 = A.alloc(OCH * 512, BF16)
        s_hws = [A.alloc(OCH * 128, BF16) for _ in range(3)]
        s_sqB = [A.alloc(512, F32) for _ in range(2)]
        s_mr = A.alloc(QT, F32)[0:1, :]
        s_er = A.alloc(QT, F32)[0:1, :]
        s_vr = A.alloc(QT, F32)[0:1, :]
        s_vr2 = A.alloc(QT, F32)[0:1, :]
        s_vr3 = A.alloc(QT, F32)[0:1, :]
        s_arw = A.alloc(QT, F32)[0:1, :]
        s_brw = A.alloc(QT, F32)[0:1, :]
        s_brw2 = A.alloc(QT, F32)[0:1, :]
        u2_off = (A.mark() + 31) // 32 * 32
        s_x1b = A.alloc(IC * 512, BF16)
        s_w1s = [A.alloc(IC * 128, BF16) for _ in range(3)]
        s_w2s = [A.alloc(IC * 128, BF16) for _ in range(3)]
        s_u2 = A.at(u2_off, IC * 512, F32)

        two = c.dma("G", "wo", s_wo, d_wo)
        pf = {}
        for oc in range(3):
            pf[("w1s", oc)] = c.dma("G", f"w1s{oc % 3}", s_w1s[oc % 3], d_w1[oc])
            pf[("w2s", oc)] = c.dma("G", f"w2s{oc % 3}", s_w2s[oc % 3], d_w2[oc])
            pf[("hws", oc)] = c.dma("G", f"hws{oc % 3}", s_hws[oc % 3], d_h1[oc])

        def ln_rows(bM1, bM2, tstats):
            c.wait("V", tstats)
            nc.vector.tensor_scalar(s_mr, bM1[0:1, :], 1.0 / D, None, ALU.mult)
            nc.vector.tensor_scalar(s_mr, bM1[0:1, :], 1.0 / D, None, ALU.mult)
            nc.vector.tensor_scalar(s_er, bM2[0:1, :], 1.0 / D, None, ALU.mult)
            nc.vector.tensor_scalar(s_er, bM2[0:1, :], 1.0 / D, None, ALU.mult)
            nc.vector.tensor_mul(s_vr2, s_mr, s_mr)
            nc.vector.tensor_mul(s_vr2, s_mr, s_mr)
            nc.vector.tensor_sub(s_vr, s_er, s_vr2)
            tvv_ = c.inc("V", nc.vector.tensor_sub(s_vr, s_er, s_vr2))
            c.wait("A", tvv_)
            nc.scalar.activation(s_vr3, s_vr, AF.Ln, bias=s_eps)
            tra = c.inc("A", nc.scalar.activation(s_arw, s_vr3, AF.Exp, scale=-0.5))
            c.wait("V", tra)
            nc.vector.tensor_mul(s_brw, s_mr, s_arw)
            nc.vector.tensor_scalar_mul(s_brw2, s_brw, -1.0)
            return c.inc("V", nc.vector.tensor_scalar_mul(s_brw2, s_brw, -1.0))

        for g in range(4):
            asl = g % 2
            depao = last.get(("aoq", asl))
            aodeps = [depao] if depao else []
            if tok_cc1 is not None and g == 0:
                aodeps += [tok_cc1, tok_cc]
            c.dma("G", f"aoq{asl}",
                  s_aoq[asl].rearrange("p (a t) -> p a t", t=512)[:, 0::2, :],
                  d_all1.ap().rearrange("(a p) t -> p a t", p=128)
                  [:, :, ds((vhg + 4 * g) * 512, 512)],
                  deps=aodeps)
            tao = c.dma("G", f"aoq{asl}",
                        s_aoq[asl].rearrange("p (a t) -> p a t", t=512)[:, 1::2, :],
                        d_all2.ap().rearrange("(a p) t -> p a t", p=128)
                        [:, :, ds((vhg + 4 * g) * 512, 512)],
                        deps=aodeps)
            tu = None
            for oc in range(IC):
                sl = oc % 2
                dep = last.get(("xoq", sl))
                txo = c.dma("Y", f"xoq{sl}", s_xoq[sl],
                            d_xo.rearrange("p (a t) -> p a t", t=2048)
                            [:, oc, 512 * g:512 * g + 512],
                            deps=[dep] if dep else [])
                bU = pb[oc % 2]
                c.wait("T", tao)
                c.wait("T", two)
                dep = last.get(("bU", oc % 2))
                if dep:
                    c.wait("T", dep)
                for ic in range(IC):
                    ins = nc.tensor.matmul(
                        bU, s_wo[:, 1024 * ic + 128 * oc:1024 * ic + 128 * oc + 128],
                        s_aoq[asl][:, 512 * ic:512 * ic + 512],
                        start=(ic == 0), stop=(ic == IC - 1))
                tm = c.inc("T", ins)
                c.wait("V", tm)
                c.wait("V", txo)
                tu = c.inc("V", nc.vector.scalar_tensor_tensor(
                    s_u[:, 512 * oc:512 * oc + 512], bU, s_bo[:, oc:oc + 1],
                    s_xoq[sl], ALU.add, ALU.add))
                last[("bU", oc % 2)] = tu
                last[("xoq", sl)] = tu
                dep = last.get(("sqB", oc % 2))
                if dep:
                    c.wait("A", dep)
                c.wait("A", tu)
                tsq = c.inc("A", nc.scalar.activation(
                    s_sqB[oc % 2], s_u[:, 512 * oc:512 * oc + 512], AF.Square))
                c.wait("T", tu)
                c.wait("T", tsq)
                if oc == 0:
                    dep = last.get(("pbst",))
                    if dep:
                        c.wait("T", dep)
                nc.tensor.matmul(pb[2][0:1, :], s_onc,
                                 s_u[:, 512 * oc:512 * oc + 512],
                                 start=(oc == 0), stop=(oc == IC - 1))
                tst = c.inc("T", nc.tensor.matmul(
                    pb[3][0:1, :], s_onc, s_sqB[oc % 2],
                    start=(oc == 0), stop=(oc == IC - 1)))
                last[("sqB", oc % 2)] = tst
            last[("aoq", asl)] = tm
            trows = ln_rows(pb[2], pb[3], tst)
            c.wait("T", trows)
            nc.tensor.matmul(pb[2], s_onf, s_arw, start=True, stop=True)
            nc.tensor.matmul(pb[2], s_onf, s_arw, start=True, stop=True)
            tbc = c.mm2(nc, pb[3], s_onf, s_brw2, start=True, stop=True)
            tx1 = None
            for oc in range(IC):
                c.wait("V", tbc)
                nc.vector.tensor_mul(s_sqB[oc % 2],
                                     s_u[:, 512 * oc:512 * oc + 512], pb[2])
                nc.vector.tensor_add(s_u[:, 512 * oc:512 * oc + 512],
                                     s_sqB[oc % 2], pb[3])
                nc.vector.tensor_add(s_u[:, 512 * oc:512 * oc + 512],
                                     s_sqB[oc % 2], pb[3])
                tx1 = c.inc("V", nc.vector.tensor_copy(
                    s_x1b[:, 512 * oc:512 * oc + 512],
                    s_u[:, 512 * oc:512 * oc + 512]))
            last[("pbst",)] = tx1

            if "dbg_x1" in dbg_outs and g == 0:
                c.dma("Y", "dbg", dbg_outs["dbg_x1"], s_u, deps=[tx1])
            if "dbg_ao" in dbg_outs and g == 0:
                c.dma("Y", "dbg", dbg_outs["dbg_ao"], s_aoq, deps=[tao])

            th = None
            for oc in range(OCH):
                sl = oc % 3
                if ("w1s", oc) in pf and g == 0:
                    t1 = pf[("w1s", oc)]
                    t2 = pf[("w2s", oc)]
                else:
                    dep = last.get(("w1s", sl))
                    t1 = c.dma("G", f"w1s{sl}", s_w1s[sl], d_w1[oc],
                               deps=[dep] if dep else [])
                    dep = last.get(("w2s", sl))
                    t2 = c.dma("G", f"w2s{sl}", s_w2s[sl], d_w2[oc],
                               deps=[dep] if dep else [])
                b1 = pb[4 + (oc % 2) * 2]
                b2 = pb[5 + (oc % 2) * 2]
                c.wait("T", t1)
                c.wait("T", tx1)
                dep = last.get(("b1", oc % 2))
                if dep:
                    c.wait("T", dep)
                for ic in range(IC):
                    ins = nc.tensor.matmul(
                        b1, s_w1s[sl][:, 128 * ic:128 * ic + 128],
                        s_x1b[:, 512 * ic:512 * ic + 512],
                        start=(ic == 0), stop=(ic == IC - 1))
                tm1 = c.inc("T", ins)
                c.wait("T", t2)
                for ic in range(IC):
                    ins = nc.tensor.matmul(
                        b2, s_w2s[sl][:, 128 * ic:128 * ic + 128],
                        s_x1b[:, 512 * ic:512 * ic + 512],
                        start=(ic == 0), stop=(ic == IC - 1))
                tm2 = c.inc("T", ins)
                c.wait("A", tm1)
                tsi = c.inc("A", nc.scalar.activation(
                    s_h[:, 512 * oc:512 * oc + 512], b1, AF.Silu,
                    scale=s_s1[:, oc:oc + 1]))
                c.wait("V", tm2)
                c.wait("V", tsi)
                th = c.inc("V", nc.vector.scalar_tensor_tensor(
                    s_h[:, 512 * oc:512 * oc + 512], b2, s_s2[:, oc:oc + 1],
                    s_h[:, 512 * oc:512 * oc + 512], ALU.mult, ALU.mult))
                last[("b1", oc % 2)] = th
                last[("w1s", sl)] = tm1
                last[("w2s", sl)] = tm2

            def hstage(dsrc, src_ap, dst_ap, scol, noc, wait_tok, resid=False, wtag_pf=False):
                tg_ = None
                for oc in range(noc):
                    sl = oc % 3
                    if ("hws", oc) in pf and wtag_pf:
                        tw = pf.pop(("hws", oc))
                    else:
                        dep = last.get(("hws", sl))
                        tw = c.dma("G", f"hws{sl}", s_hws[sl], dsrc[oc],
                                   deps=[dep] if dep else [])
                    bk = pb[4 + oc % 2]
                    c.wait("T", tw)
                    c.wait("T", wait_tok)
                    dep = last.get(("bh", oc % 2))
                    if dep:
                        c.wait("T", dep)
                    for icp in range(OCH):
                        ins = nc.tensor.matmul(
                            bk, s_hws[sl][:, 128 * icp:128 * icp + 128],
                            src_ap[:, 512 * icp:512 * icp + 512],
                            start=(icp == 0), stop=(icp == OCH - 1))
                    tm = c.inc("T", ins)
                    if not resid:
                        c.wait("A", tm)
                        tg_ = c.inc("A", nc.scalar.activation(
                            dst_ap[:, 512 * oc:512 * oc + 512], bk, AF.Gelu,
                            scale=scol[:, oc:oc + 1]))
                    else:
                        c.wait("V", tm)
                        tg_ = c.inc("V", nc.vector.scalar_tensor_tensor(
                            dst_ap[:, 512 * oc:512 * oc + 512], bk,
                            scol[:, oc:oc + 1],
                            s_u[:, 512 * oc:512 * oc + 512],
                            ALU.mult, ALU.add))
                        dep = last.get(("sqB", oc % 2))
                        if dep:
                            c.wait("A", dep)
                        c.wait("A", tg_)
                        tsq = c.inc("A", nc.scalar.activation(
                            s_sqB[oc % 2],
                            dst_ap[:, 512 * oc:512 * oc + 512], AF.Square))
                        c.wait("T", tg_)
                        c.wait("T", tsq)
                        if oc == 0:
                            dep2 = last.get(("pbst",))
                            if dep2:
                                c.wait("T", dep2)
                        nc.tensor.matmul(
                            pb[2][0:1, :], s_onc,
                            dst_ap[:, 512 * oc:512 * oc + 512],
                            start=(oc == 0), stop=(oc == noc - 1))
                        tst2 = c.inc("T", nc.tensor.matmul(
                            pb[3][0:1, :], s_onc, s_sqB[oc % 2],
                            start=(oc == 0), stop=(oc == noc - 1)))
                        last[("sqB", oc % 2)] = tst2
                        last[("lnst",)] = tst2
                    last[("bh", oc % 2)] = tg_
                    last[("hws", sl)] = tm
                return tg_

            tgel1 = hstage(d_h1, s_h, s_h2, s_s3, OCH, th, wtag_pf=(g == 0))
            tgel2 = hstage(d_h2, s_h2, s_h, s_s4, OCH, tgel1)
            tu2 = hstage(d_w3, s_h, s_u2, s_s5, IC, tgel2, resid=True)

            trows = ln_rows(pb[2], pb[3], last[("lnst",)])
            c.wait("T", trows)
            nc.tensor.matmul(pb[2], s_onf, s_arw, start=True, stop=True)
            nc.tensor.matmul(pb[2], s_onf, s_arw, start=True, stop=True)
            tbc = c.mm2(nc, pb[3], s_onf, s_brw2, start=True, stop=True)
            tfin = None
            for oc in range(IC):
                c.wait("V", tbc)
                nc.vector.tensor_mul(s_sqB[oc % 2],
                                     s_u2[:, 512 * oc:512 * oc + 512], pb[2])
                nc.vector.tensor_add(s_u2[:, 512 * oc:512 * oc + 512],
                                     s_sqB[oc % 2], pb[3])
                tfin = c.inc("V", nc.vector.tensor_add(
                    s_u2[:, 512 * oc:512 * oc + 512],
                    s_sqB[oc % 2], pb[3]))
            last[("pbst",)] = tfin
            tout = c.dma("Y", "out",
                         d_out.rearrange("p (a t) -> p a t", t=2048)
                         [:, :, 512 * g:512 * g + 512],
                         s_u2.rearrange("p (a t) -> p a t", t=512),
                         deps=[tfin])
            c.wait("V", tout)
            for _sl in range(3):
                last[("w1s", _sl)] = tout
                last[("w2s", _sl)] = tout
        A.reset(markB)
    else:
        c.wait("Y", tv0)
        c.dma("Y", "out", d_out[:, 0:1024], s_tq[:, 0:1024], deps=[tok_cst])

    c.wait("Y", ("out", c.cnt.get("out", 0)))
    if 2 <= STAGE < 3:
        c.wait("Y", ("cat0", c.cnt.get("cat0", 0)))
        c.wait("Y", ("cat1", c.cnt.get("cat1", 0)))
    if dbg_outs:
        c.wait("Y", ("dbg", c.cnt.get("dbg", 0)))
        c.wait("G", ("dbg", c.cnt.get("dbg", 0)))
    return nc


_CACHE = {}


def _host_tables():
    tq = np.zeros((NQT, 128, NCH), np.float32)
    dj = np.arange(128).reshape(128, 1)
    cc = np.arange(NCH).reshape(1, NCH)
    j = 128 * cc + dj
    for qt in range(NQT):
        i0 = 512 * qt
        i1 = i0 + 511
        tq[qt] = np.where(cc < 4 * qt, j - i0,
                          np.where(cc >= 4 * qt + 4, i1 - j, 0.0))
    tqp = np.ascontiguousarray(tq.transpose(1, 0, 2).reshape(128, NQT * NCH))
    d0 = np.zeros((4, 128, QT), np.float32)
    di = np.arange(QT).reshape(1, QT)
    for cp in range(4):
        d0[cp] = np.abs(128 * cp + np.arange(128).reshape(128, 1) - di)
    d0p = np.ascontiguousarray(d0.transpose(1, 0, 2).reshape(128, 4 * QT))
    return tqp, d0p


def _pcol(v, p=128):
    v = np.asarray(v, np.float32).reshape(-1)
    return np.ascontiguousarray(v.reshape(-1, p).T)


def _wblocks(wt, och):
    I, O = wt.shape
    icn = I // 128
    r = wt.reshape(icn, 128, och, 128).transpose(2, 1, 0, 3)
    return np.ascontiguousarray(r.reshape(och, 128, icn * 128))


def _prows(wt):
    I, O = wt.shape
    icn = I // 128
    return np.ascontiguousarray(
        wt.reshape(icn, 128, O).transpose(1, 0, 2).reshape(128, icn * O))


def kernel(L0_attn_x, L0_attn_qkv_w, L0_attn_qkv_b, L0_attn_out_w, L0_attn_out_b,
           L0_attn_decay_alpha, L0_ffn_w1_w, L0_ffn_w1_s, L0_ffn_w2_w, L0_ffn_w2_s,
           L0_ffn_h1_w, L0_ffn_h1_s, L0_ffn_h2_w, L0_ffn_h2_s, L0_ffn_w3_w,
           L0_ffn_w3_s, _debug=(), _stage=None):
    global STAGE
    if _stage is not None:
        STAGE = _stage
    x = np.asarray(L0_attn_x, np.float32)
    qkv_w = np.asarray(L0_attn_qkv_w, np.float32)
    qkv_b = np.asarray(L0_attn_qkv_b, np.float32)
    out_w = np.asarray(L0_attn_out_w, np.float32)
    out_b = np.asarray(L0_attn_out_b, np.float32)
    alpha = np.asarray(L0_attn_decay_alpha, np.float32)

    key = ("nc", STAGE, tuple(sorted(d[0] for d in _debug)))
    if key not in _CACHE:
        _CACHE[key] = build_nc(debug=_debug)
    nc = _CACHE[key]

    tqp, d0p = _host_tables()
    shared = {
        "wop": _prows(np.ascontiguousarray(out_w.T)),
        "bop": _pcol(out_b),
        "w1p": _wblocks(np.ascontiguousarray(np.asarray(L0_ffn_w1_w, np.float32).T), OCH),
        "w2p": _wblocks(np.ascontiguousarray(np.asarray(L0_ffn_w2_w, np.float32).T), OCH),
        "h1p": _wblocks(np.ascontiguousarray(np.asarray(L0_ffn_h1_w, np.float32).T), OCH),
        "h2p": _wblocks(np.ascontiguousarray(np.asarray(L0_ffn_h2_w, np.float32).T), OCH),
        "w3p": _wblocks(np.ascontiguousarray(np.asarray(L0_ffn_w3_w, np.float32).T), 8),
        "s1p": _pcol(L0_ffn_w1_s), "s2p": _pcol(L0_ffn_w2_s),
        "s3p": _pcol(L0_ffn_h1_s), "s4p": _pcol(L0_ffn_h2_s),
        "s5p": _pcol(L0_ffn_w3_s),
        "tqp": tqp, "d0p": d0p,
        "dqp": np.arange(QT, dtype=np.float32).reshape(1, QT),
        "onp": np.ones((1, 128), np.float32),
        "dqbp": np.tile(np.arange(QT, dtype=np.float32), (128, 1)),
        "eyp": np.concatenate([np.eye(2, dtype=np.float32), np.zeros((2, 510), np.float32)], axis=1),
    }
    xtp = [_prows(np.ascontiguousarray(x[b].T)) for b in range(2)]

    in_maps = []
    for core in range(8):
        b, hg = core // 4, core % 4
        xo = np.concatenate([x[b, 2048 * g + 512 * hg:2048 * g + 512 * hg + 512]
                             for g in range(4)], axis=0)
        m = dict(shared)
        m["xtp"] = xtp[b]
        m["xtop"] = _prows(np.ascontiguousarray(xo.T))
        m["wqp"] = _prows(np.ascontiguousarray(
            qkv_w[256 * hg:256 * hg + 256, :].T))
        m["wkp"] = _prows(np.ascontiguousarray(
            qkv_w[1024 + 256 * hg:1024 + 256 * hg + 256, :].T))
        m["wvp"] = _prows(np.ascontiguousarray(
            qkv_w[2048 + 256 * hg:2048 + 256 * hg + 256, :].T))
        bq = qkv_b[256 * hg:256 * hg + 256].reshape(2, 128)
        bk = qkv_b[1024 + 256 * hg:1024 + 256 * hg + 256].reshape(2, 128)
        m["bqkp"] = np.ascontiguousarray(np.stack(
            [bq[0], bq[1], bk[0], bk[1]], axis=1))
        m["bvp"] = np.ascontiguousarray(
            qkv_b[2048 + 256 * hg:2048 + 256 * hg + 256].reshape(1, 256))
        m["alp"] = np.ascontiguousarray(alpha[4 * hg:4 * hg + 4].reshape(1, 4))
        m["hgp"] = np.array([[hg]], np.int32)
        in_maps.append(m)

    res = run_bass_kernel_spmd(nc, in_maps, core_ids=list(range(8)))

    out = np.empty((2, 8192, 1024), np.float32)
    for core in range(8):
        b, hg = core // 4, core % 4
        o = np.asarray(res.results[core]["outp"])
        o = o.reshape(128, 8, 2048).transpose(1, 0, 2).reshape(1024, 2048)
        for g in range(4):
            out[b, 2048 * g + 512 * hg:2048 * g + 512 * hg + 512, :] = \
                o[:, 512 * g:512 * g + 512].T
    if _debug:
        return out, [dict(r) for r in res.results]
    return out


# version 09
# speedup vs baseline: 1.0966x; 1.0275x over previous
import numpy as np
import ml_dtypes
import concourse.bass as bass
import concourse.mybir as mybir
from concourse.bass_utils import run_bass_kernel_spmd
from concourse.expressions import RuntimeValue

F32 = mybir.dt.float32
BF16 = mybir.dt.bfloat16
I32 = mybir.dt.int32
AF = mybir.ActivationFunctionType
ALU = mybir.AluOpType
ds = bass.ds

D = 1024
T = 8192
IC = 8
QT = 512
NQT = T // QT
NCH = T // 128
DH = 4096
OCH = DH // 128
NLN8 = -2.0794415416798357

STAGE = 4


class Ctx:

    def __init__(self, nc):
        self.nc = nc
        self.eng = {"T": nc.tensor, "A": nc.scalar, "V": nc.vector,
                    "G": nc.gpsimd, "Y": nc.sync}
        self.sem = {k: nc.alloc_semaphore("sm" + k) for k in self.eng}
        self.pool = [nc.alloc_semaphore(f"sp{i}") for i in range(56)]
        self.cnt = {k: 0 for k in self.eng}
        self.water = {}
        nums = [s.num for s in list(self.sem.values()) + self.pool]
        assert max(nums) - min(nums) + 1 == len(nums), "sem range not contiguous"
        nc.gpsimd.sem_clear(range(min(nums), max(nums) + 1))
        nc.all_engine_barrier()

    def dsem(self, name):
        if name not in self.sem:
            self.sem[name] = self.pool.pop(0)
            self.cnt[name] = 0
        return self.sem[name]

    def wait(self, who, tok):
        if tok is None:
            return
        name, val = tok
        if name == who or val <= 0:
            return
        if self.water.get((who, name), 0) >= val:
            return
        self.water[(who, name)] = val
        self.eng[who].wait_ge(self.sem[name], val)

    def mm2(self, nc, *args, **kw):
        nc.tensor.matmul(*args, **kw)
        return self.inc("T", nc.tensor.matmul(*args, **kw))

    def v2(self, emit):
        emit()
        return self.inc("V", emit())

    def inc(self, who, ins):
        self.cnt[who] += 1
        ins.then_inc(self.sem[who], 1)
        return (who, self.cnt[who])

    def dma(self, who, name, dst, src, deps=()):
        for d in deps:
            self.wait(who, d)
        s = self.dsem(name)
        self.eng[who].dma_start(dst, src).then_inc(s, 16)
        self.cnt[name] = self.cnt.get(name, 0) + 16
        return (name, self.cnt[name])

    def join(self, engines=("T", "A", "V", "G", "Y")):
        toks = [(k, self.cnt[k]) for k in engines]
        for who in engines:
            for t in toks:
                self.wait(who, t)


class Arena:
    def __init__(self, nc, nbytes):
        self.ap = nc.alloc_sbuf_tensor("arena", [128, nbytes], mybir.dt.int8).ap()
        self.off = 0
        self.nbytes = nbytes

    def alloc(self, cols, dtype):
        esz = mybir.dt.size(dtype)
        n = cols * esz
        off = (self.off + 31) // 32 * 32
        assert off + n <= self.nbytes, f"arena overflow: {off + n} > {self.nbytes}"
        self.off = off + n
        return self.ap[:, off:off + n].bitcast(dtype)

    def mark(self):
        return self.off

    def reset(self, mark):
        self.off = mark

    def at(self, off, cols, dtype):
        n = cols * mybir.dt.size(dtype)
        assert off % 32 == 0 and off + n <= self.nbytes
        return self.ap[:, off:off + n].bitcast(dtype)


def build_nc(debug=()):
    nc = bass.Bass("TRN2", target_bir_lowering=False)
    c = Ctx(nc)

    def din(name, shape, dt=F32):
        return nc.dram_tensor(name, shape, dt, kind="ExternalInput").ap()

    d_xt = din("xtp", [128, IC * T])
    d_xo = din("xtop", [128, IC * 2048])
    d_wq = din("wqp", [128, IC * 256])
    d_wk = din("wkp", [128, IC * 256])
    d_wv = din("wvp", [128, IC * 256])
    d_bqk = din("bqkp", [128, 4])
    d_bv = din("bvp", [1, 256])
    d_al = din("alp", [1, 4])
    d_wo = din("wop", [128, IC * D])
    d_bo = din("bop", [128, 8])
    d_w1 = din("w1p", [OCH, 128, IC * 128])
    d_w2 = din("w2p", [OCH, 128, IC * 128])
    d_h1 = din("h1p", [OCH, 128, OCH * 128])
    d_h2 = din("h2p", [OCH, 128, OCH * 128])
    d_w3 = din("w3p", [8, 128, OCH * 128])
    d_s1 = din("s1p", [128, OCH])
    d_s2 = din("s2p", [128, OCH])
    d_s3 = din("s3p", [128, OCH])
    d_s4 = din("s4p", [128, OCH])
    d_s5 = din("s5p", [128, 8])
    d_tq = din("tqp", [128, NQT * NCH])
    d_d0 = din("d0p", [128, 4 * QT])
    d_dq = din("dqp", [1, QT])
    d_dqb = din("dqbp", [128, QT])
    d_on = din("onp", [1, 128])
    d_ey = din("eyp", [2, 512])
    d_hg = din("hgp", [1, 1], I32)

    d_cat = nc.dram_tensor("attn_cat", [256, T], BF16, kind="Internal")
    d_all1 = nc.dram_tensor("attn_all1", [512, T], BF16, kind="Internal")
    d_all2 = nc.dram_tensor("attn_all2", [512, T], BF16, kind="Internal")
    d_out = nc.dram_tensor("outp", [128, 8 * 2048], F32, kind="ExternalOutput").ap()

    dbg_outs = {}
    for name, shape, dt in debug:
        dbg_outs[name] = nc.dram_tensor(name, shape, dt, kind="ExternalOutput").ap()

    def sb(name, shape, dt):
        return nc.alloc_sbuf_tensor(name, shape, dt).ap()

    s_tq = sb("s_tq", [128, NQT * NCH], F32)
    s_d0 = sb("s_d0", [128, 4 * QT], F32)
    s_dq = sb("s_dq", [1, QT], F32)
    s_dqb = sb("s_dqb", [128, QT], F32)
    s_onf = sb("s_onf", [1, 128], F32)
    s_onb = sb("s_onb", [1, 128], BF16)
    s_ey = sb("s_ey", [2, 512], F32)
    s_bqk = sb("s_bqk", [128, 4], F32)
    s_bv = sb("s_bv", [1, 256], F32)
    s_al = sb("s_al", [1, 4], F32)
    s_hgi = sb("s_hgi", [1, 1], I32)
    s_bo = sb("s_bo", [128, 8], F32)
    s_s1 = sb("s_s1", [128, OCH], F32)
    s_s2 = sb("s_s2", [128, OCH], F32)
    s_s3 = sb("s_s3", [128, OCH], F32)
    s_s4 = sb("s_s4", [128, OCH], F32)
    s_s5 = sb("s_s5", [128, 8], F32)
    s_ar = sb("s_ar", [1, 512], F32)
    s_nar = sb("s_nar", [1, 4], F32)
    s_m511 = sb("s_m511", [1, 4], F32)
    s_ab = sb("s_ab", [128, 4], F32)
    s_n8ab = sb("s_n8ab", [128, 4], F32)
    s_m511b = sb("s_m511b", [128, 4], F32)
    s_ur = sb("s_ur", [1, 512], F32)
    s_nub = sb("s_nub", [128, 4], F32)
    s_k2r = sb("s_k2r", [1, 4], F32)
    s_q2r = sb("s_q2r", [1, 4], F32)
    s_k2mP = [sb("s_k2m0", [2, 1], F32), sb("s_k2m1", [2, 1], F32)]
    s_q2mP = [sb("s_q2m0", [2, 1], F32), sb("s_q2m1", [2, 1], F32)]
    s_ind2 = sb("s_ind2", [128, 2], BF16)
    s_onc = sb("s_onc", [128, 1], F32)
    s_eps = sb("s_eps", [1, 1], F32)
    s_one = sb("s_one", [1, 1], F32)
    s_nl8 = sb("s_nl8", [1, 1], F32)
    s_t14a = sb("s_t14a", [1, 4], F32)
    s_t14b = sb("s_t14b", [1, 4], F32)
    s_bsl = [sb(f"s_bsl{h}", [128, NCH], F32) for h in range(2)]

    pb = [nc.alloc_psum_tensor(f"pb{i}", [128, 512], F32).ap() for i in range(8)]

    A = Arena(nc, 182 * 1024)
    last = {}

    for dst, src in [(s_tq, d_tq), (s_d0, d_d0), (s_dq, d_dq), (s_dqb, d_dqb), (s_onf, d_on),
                     (s_ey, d_ey), (s_bqk, d_bqk), (s_bv, d_bv), (s_al, d_al),
                     (s_hgi, d_hg), (s_bo, d_bo), (s_s1, d_s1), (s_s2, d_s2),
                     (s_s3, d_s3), (s_s4, d_s4), (s_s5, d_s5)]:
        c.dma("Y", "cst", dst, src)
    tok_cst = ("cst", c.cnt["cst"])
    tok_cstg = c.dma("G", "cstg", s_onb, d_on)

    c.wait("V", tok_cst)
    c.wait("V", tok_cstg)
    nc.vector.memset(s_onc, 1.0)
    nc.vector.memset(s_eps, 1e-5)
    nc.vector.memset(s_one, 1.0)
    nc.vector.memset(s_nl8, NLN8)
    nc.vector.memset(s_ind2, 0.0)
    nc.vector.memset(s_ind2[0:64, 0:1], 1.0)
    nc.vector.memset(s_ind2[64:128, 1:2], 1.0)
    nc.vector.memset(s_ur, 0.0)
    nc.vector.memset(s_ar, 0.0)
    nc.vector.memset(s_k2mP[0], 0.0)
    nc.vector.memset(s_k2mP[1], 0.0)
    nc.vector.memset(s_q2mP[0], 0.0)
    tv0 = c.inc("V", nc.vector.memset(s_q2mP[1], 0.0))

    c.wait("A", tok_cst)
    c.wait("A", tv0)
    nc.scalar.activation(s_t14a, s_al, AF.Abs)
    nc.scalar.activation(s_t14a, s_t14a, AF.Exp, scale=-1.0)
    nc.scalar.activation(s_t14a, s_t14a, AF.Ln, bias=s_one)
    ta0 = c.inc("A", nc.scalar.activation(s_t14b, s_al, AF.Relu))
    c.wait("V", ta0)
    nc.vector.tensor_add(s_ar[0:1, 0:4], s_t14a, s_t14b)
    nc.vector.tensor_scalar_mul(s_nar, s_ar[0:1, 0:4], -1.0)
    tv1 = c.inc("V", nc.vector.tensor_scalar_mul(s_m511, s_ar[0:1, 0:4], -511.0))
    c.wait("T", tv1)
    tmm0 = c.mm2(nc, pb[0], s_onf, s_ar[0:1, :], start=True, stop=True)
    c.wait("V", tmm0)
    nc.vector.tensor_copy(s_ab, pb[0][:, 0:4])
    nc.vector.tensor_copy(s_ab, pb[0][:, 0:4])
    nc.vector.tensor_scalar_mul(s_n8ab, s_ab, -8.0)
    nc.vector.tensor_scalar_mul(s_n8ab, s_ab, -8.0)
    nc.vector.tensor_scalar_mul(s_m511b, s_ab, -511.0)
    tv2 = c.inc("V", nc.vector.tensor_scalar_mul(s_m511b, s_ab, -511.0))
    last[("pb", 0)] = tv2

    c.wait("G", tok_cst)
    rhg = nc.gpsimd.alloc_register("rhg")
    nc.gpsimd.reg_load(rhg, s_hgi)
    vhg = RuntimeValue(rhg, min_val=0, max_val=3)

    pairst = {}

    def proj_all():
        c.join()
        s_KT = [A.alloc(T, BF16) for _ in range(2)]
        s_QT = [A.alloc(T, BF16) for _ in range(2)]
        s_V = [[A.alloc(NCH * 65, BF16) for _ in range(2)] for _ in range(2)]
        pairst["KT"] = s_KT
        pairst["QT"] = s_QT
        pairst["V"] = s_V
        m1 = A.mark()
        s_xt = [A.alloc(IC * 512, BF16) for _ in range(2)]
        s_wqb = A.alloc(IC * 256, BF16)
        s_wkb = A.alloc(IC * 256, BF16)
        s_wvb = A.alloc(IC * 256, BF16)
        s_wv260 = A.alloc(IC * 260, BF16)
        s_bv260 = A.alloc(260, BF16)
        s_sq = A.alloc(512, BF16)
        s_sq2 = A.alloc(512, BF16)
        s_r21 = A.alloc(1, F32)
        s_r22 = A.alloc(1, F32)

        twq = c.dma("G", "wq0", s_wqb.rearrange("p (a o) -> p a o", o=256),
                    d_wq.rearrange("p (a o) -> p a o", o=256))
        twk = c.dma("G", "wk0", s_wkb.rearrange("p (a o) -> p a o", o=256),
                    d_wk.rearrange("p (a o) -> p a o", o=256))
        twv = c.dma("G", "wv0", s_wvb.rearrange("p (a o) -> p a o", o=256),
                    d_wv.rearrange("p (a o) -> p a o", o=256))
        c.wait("V", twv)
        c.wait("V", tok_cst)
        nc.vector.memset(s_wv260, 0.0)
        for ic in range(IC):
            for hl in range(4):
                nc.vector.tensor_copy(
                    s_wv260[:, 260 * ic + 65 * hl:260 * ic + 65 * hl + 64],
                    s_wvb[:, 256 * ic + 64 * hl:256 * ic + 64 * hl + 64])
        nc.vector.memset(s_bv260[0:1, 0:260], 0.0)
        for hl in range(4):
            nc.vector.tensor_copy(s_bv260[0:1, 65 * hl:65 * hl + 64],
                                  s_bv[0:1, 64 * hl:64 * hl + 64])
            nc.vector.memset(s_bv260[0:1, 65 * hl + 64:65 * hl + 65], 1.0)
        nc.vector.memset(s_bv260[0:1, 259:260], 1.0)
        tv260 = c.inc("V", nc.vector.memset(s_bv260[0:1, 259:260], 1.0))

        tev = {}
        tvv = None
        for tc in range(16):
            sl = tc % 2
            dep = last.get(("xt", sl))
            tx = c.dma("G", f"x{sl}", s_xt[sl].rearrange("p (a t) -> p a t", t=512),
                       d_xt.rearrange("p (a t) -> p a t", t=T)
                       [:, :, 512 * tc:512 * tc + 512],
                       deps=[dep] if dep else [])
            c.wait("T", tx)
            c.wait("T", twk)
            c.wait("T", twq)
            c.wait("T", tv260)
            for p2 in range(2):
                bK = pb[p2]
                bQ = pb[2 + p2]
                c.wait("T", tev.get(("K", p2)))
                for ic in range(IC):
                    ins = nc.tensor.matmul(
                        bK, s_wkb[:, 256 * ic + 128 * p2:256 * ic + 128 * p2 + 128],
                        s_xt[sl][:, 512 * ic:512 * ic + 512],
                        start=(ic == 0), stop=(ic == IC - 1))
                tk = c.inc("T", ins)
                c.wait("T", tev.get(("Q", p2)))
                for ic in range(IC):
                    ins = nc.tensor.matmul(
                        bQ, s_wqb[:, 256 * ic + 128 * p2:256 * ic + 128 * p2 + 128],
                        s_xt[sl][:, 512 * ic:512 * ic + 512],
                        start=(ic == 0), stop=(ic == IC - 1))
                tq_ = c.inc("T", ins)
                c.wait("V", tk)
                tev[("K", p2)] = c.inc("V", nc.vector.tensor_scalar(
                    s_KT[p2][:, 512 * tc:512 * tc + 512], bK,
                    s_bqk[:, 2 + p2:3 + p2], None, ALU.add))
                c.wait("V", tq_)
                tev[("Q", p2)] = c.inc("V", nc.vector.tensor_scalar(
                    s_QT[p2][:, 512 * tc:512 * tc + 512], bQ,
                    s_bqk[:, p2:p2 + 1], None, ALU.add))
            for ts in range(4):
                bV = pb[4 + ts % 2]
                kc = 4 * tc + ts
                c.wait("T", tev.get(("V", ts % 2)))
                for ic in range(IC):
                    nc.tensor.matmul(
                        bV[:, 0:260],
                        s_xt[sl][:, 512 * ic + 128 * ts:512 * ic + 128 * ts + 128],
                        s_wv260[:, 260 * ic:260 * ic + 260],
                        start=(ic == 0), stop=False)
                tvv = c.inc("T", nc.tensor.matmul(
                    bV[:, 0:260], s_onb, s_bv260[0:1, 0:260],
                    start=False, stop=True))
                c.wait("V", tvv)
                for p2 in range(2):
                    for h in range(2):
                        ins = nc.vector.tensor_copy(
                            s_V[p2][h][:, 65 * kc:65 * kc + 65],
                            bV[:, 130 * p2 + 65 * h:130 * p2 + 65 * h + 65])
                tev[("V", ts % 2)] = c.inc("V", ins)
            last[("xt", sl)] = tvv

        for key in (("K", 0), ("K", 1), ("Q", 0), ("Q", 1)):
            c.wait("A", tev[key])
        stat_src = []
        for p2 in range(2):
            stat_src += [(s_KT[p2], s_k2mP[p2], kc2) for kc2 in range(16)]
            stat_src += [(s_QT[p2], s_q2mP[p2], kc2) for kc2 in range(16)]
        sqs = [s_sq, s_sq2]
        NST = len(stat_src)
        tvm = None
        tsq_l = {}

        def stat_sq(i):
            srcb, _, kc2 = stat_src[i]
            dep = last.get(("stsq", i % 2))
            if dep:
                c.wait("A", dep)
            return c.inc("A", nc.scalar.activation(
                sqs[i % 2], srcb[:, 512 * kc2:512 * kc2 + 512], AF.Square))

        tsq_l[0] = stat_sq(0)
        for i in range(NST):
            bnk = pb[6] if i % 2 == 0 else pb[7]
            c.wait("T", tsq_l[i])
            c.wait("T", tv0)
            dep = last.get(("pbst6", i % 2))
            if dep:
                c.wait("T", dep)
            tm = c.mm2(nc, bnk[0:2, :], s_ind2, sqs[i % 2],
                       start=True, stop=True)
            last[("stsq", i % 2)] = tm
            if i + 1 < NST:
                tsq_l[i + 1] = stat_sq(i + 1)
            dstm = stat_src[i][1]
            c.wait("V", tm)
            if tvm is not None:
                c.wait("V", tvm)
            nc.vector.reduce_max(s_r21[0:2, 0:1], bnk[0:2, :],
                                 axis=mybir.AxisListType.X)
            nc.vector.reduce_max(s_r21[0:2, 0:1], bnk[0:2, :],
                                 axis=mybir.AxisListType.X)
            nc.vector.tensor_max(s_r22[0:2, 0:1], dstm, s_r21[0:2, 0:1])
            nc.vector.tensor_max(s_r22[0:2, 0:1], dstm, s_r21[0:2, 0:1])
            nc.vector.tensor_copy(dstm, s_r22[0:2, 0:1])
            tvm = c.inc("V", nc.vector.tensor_copy(dstm, s_r22[0:2, 0:1]))
            last[("pbst6", i % 2)] = tvm
        last[("pb", 6)] = tvm
        c.wait("T", tvm)

        for p2 in range(2):
            dep = last.get(("pb", 6))
            if dep:
                c.wait("T", dep)
            tk2 = c.mm2(nc, pb[6][0:1, 0:512], s_k2mP[p2], s_ey,
                        start=True, stop=True)
            c.wait("V", tk2)
            tvk = c.v2(lambda p2=p2: nc.vector.tensor_copy(
                s_k2r[0:1, 2 * p2:2 * p2 + 2], pb[6][0:1, 0:2]))
            c.wait("T", tvk)
            tq2 = c.mm2(nc, pb[6][0:1, 0:512], s_q2mP[p2], s_ey,
                        start=True, stop=True)
            c.wait("V", tq2)
            nc.vector.tensor_copy(s_q2r[0:1, 2 * p2:2 * p2 + 2], pb[6][0:1, 0:2])
            nc.vector.tensor_copy(s_q2r[0:1, 2 * p2:2 * p2 + 2], pb[6][0:1, 0:2])
            nc.vector.tensor_mul(
                s_t14a[0:1, 2 * p2:2 * p2 + 2], s_k2r[0:1, 2 * p2:2 * p2 + 2],
                s_q2r[0:1, 2 * p2:2 * p2 + 2])
            tvq = c.inc("V", nc.vector.tensor_mul(
                s_t14a[0:1, 2 * p2:2 * p2 + 2], s_k2r[0:1, 2 * p2:2 * p2 + 2],
                s_q2r[0:1, 2 * p2:2 * p2 + 2]))
            last[("pb", 6)] = tvq
            c.wait("A", tvq)
            nc.scalar.activation(s_t14b[0:1, 2 * p2:2 * p2 + 2],
                                 s_t14a[0:1, 2 * p2:2 * p2 + 2], AF.Ln)
            tua = c.inc("A", nc.scalar.activation(
                s_ur[0:1, 2 * p2:2 * p2 + 2], s_t14b[0:1, 2 * p2:2 * p2 + 2],
                AF.Exp, scale=0.5, bias=s_nl8))
            c.wait("T", tua)
            c.wait("T", tvq)
            tub = c.mm2(nc, pb[6], s_onf, s_ur[0:1, :], start=True, stop=True)
            c.wait("V", tub)
            tnu = c.v2(lambda p2=p2: nc.vector.tensor_scalar(
                s_nub[:, 2 * p2:2 * p2 + 2], pb[6][:, 2 * p2:2 * p2 + 2],
                -1.0, None, ALU.mult))
            last[("pb", 6)] = tnu
            pairst[("tnu", p2)] = tnu
        A.reset(m1)

    def attention_pair(p):
        c.join()
        mark = A.mark()
        s_KT = pairst["KT"][p]
        s_QT = pairst["QT"][p]
        s_V = pairst["V"][p]
        tnu = pairst[("tnu", p)]
        s_ds = [[A.alloc(512, F32) for _ in range(2)] for _ in range(2)]
        s_P = [[A.alloc(512, BF16) for _ in range(14)] for _ in range(2)]
        s_dh = [A.alloc(512, F32) for _ in range(2)]
        s_xc = [A.alloc(512, F32) for _ in range(2)]
        s_x2 = [A.alloc(512, F32) for _ in range(2)]
        s_ao = [A.alloc(512, BF16) for _ in range(2)]
        s_G = [A.alloc(QT, F32) for _ in range(2)]
        s_H = [A.alloc(QT, F32) for _ in range(2)]
        s_lin = [A.alloc(QT, F32) for _ in range(2)]
        s_lt = [A.alloc(QT, F32) for _ in range(2)]

        tgh = None
        for h in range(2):
            hl = 2 * p + h
            c.wait("V", tv2)
            tgt = c.inc("V", nc.vector.tensor_scalar(
                s_x2[h], s_dqb, s_ab[:, hl:hl + 1], None, ALU.mult))
            c.wait("A", tgt)
            nc.scalar.activation(s_G[h], s_x2[h], AF.Exp, scale=-1.0)
            tgh = c.inc("A", nc.scalar.activation(
                s_H[h], s_x2[h], AF.Exp, scale=1.0,
                bias=s_m511b[:, hl:hl + 1]))

        if STAGE < 2:
            A.reset(mark)
            return

        bS = [[pb[0], pb[1]], [pb[2], pb[3]]]
        bL = [pb[4], pb[5]]
        bR = [pb[6], pb[7]]
        c.join()

        NP = 14
        AVLAG = 12
        EPIPOS = 10
        RUNWAY = 6
        pend = None

        def emit_epi_V1(st):
            qt0 = st["qt"]
            hasL, hasR = qt0 > 0, qt0 < NQT - 1
            tvcs = []
            for h in range(2):
                c.wait("V", st["tAV"])
                c.wait("V", tgh)
                if hasL:
                    nc.vector.tensor_mul(s_xc[h][0:65, :], bL[h][0:65, :],
                                         s_G[h][0:65, :])
                    if hasR:
                        nc.vector.tensor_mul(s_x2[h][0:65, :], bR[h][0:65, :],
                                             s_H[h][0:65, :])
                        nc.vector.tensor_add(s_xc[h][0:65, :], s_xc[h][0:65, :],
                                             s_x2[h][0:65, :])
                else:
                    nc.vector.tensor_mul(s_xc[h][0:65, :], bR[h][0:65, :],
                                         s_H[h][0:65, :])
                tvcs.append(c.inc("V", nc.vector.tensor_add(
                    s_xc[h][0:65, :], s_xc[h][0:65, :], s_dh[h][0:65, :])))
            st["tvcs"] = tvcs
            last[("accR", 0)] = tvcs[-1]
            last[("accR", 1)] = tvcs[-1]

        def emit_epi_A(st):
            trs = []
            for h in range(2):
                c.wait("A", st["tvcs"][h])
                nc.scalar.activation(s_lt[h][0:1, :], s_xc[h][64:65, :], AF.Ln)
                trs.append(c.inc("A", nc.scalar.activation(
                    s_lin[h][0:1, :], s_lt[h][0:1, :], AF.Exp, scale=-1.0)))
            st["trs"] = trs

        def emit_epi_T(st):
            tbcs = []
            for h in range(2):
                c.wait("T", st["trs"][h])
                tbcs.append(c.mm2(nc, bL[h][0:64, :], s_onf[0:1, 0:64],
                                  s_lin[h][0:1, :], start=True, stop=True))
            st["tbcs"] = tbcs

        def emit_epi_V2(st):
            qt0 = st["qt"]
            aos = s_ao[qt0 % 2]
            tEp = None
            for h in range(2):
                c.wait("V", st["tbcs"][h])
                dep = last.get(("aost", qt0 % 2))
                if dep:
                    c.wait("V", dep)
                tEp = c.v2(lambda h=h, aos=aos: nc.vector.tensor_mul(
                    aos[64 * h:64 * h + 64, :], s_xc[h][0:64, :],
                    bL[h][0:64, :]))
                last[("accL", h)] = tEp
            tdma = c.dma("Y", f"cat{qt0 % 2}",
                         d_cat.ap()[128 * p:128 * p + 128,
                                    512 * qt0:512 * qt0 + 512],
                         aos, deps=[tEp])
            last[("aost", qt0 % 2)] = tdma

        for qt in range(NQT):
            tb = []
            for h in range(2):
                hl = 2 * p + h
                dep = last.get(("bsl", h))
                if dep:
                    c.wait("V", dep)
                c.wait("V", tnu)
                c.wait("V", tv2)
                tb.append(c.inc("V", nc.vector.tensor_scalar(
                    s_bsl[h], s_tq[:, 64 * qt:64 * qt + 64],
                    s_ab[:, hl:hl + 1], s_nub[:, hl:hl + 1], ALU.mult, ALU.add)))
            if pend is not None:
                emit_epi_V1(pend)

            ndiag0 = 4 * qt
            Lc = [("L", cc, None) for cc in range(0, ndiag0)]
            Rc = [("R", cc, None) for cc in range(ndiag0 + 4, NCH)]
            Dc = [("D", ndiag0 + j, j) for j in range(4)]
            if Lc:
                borrow = "R"
                chunks = Lc[:RUNWAY] + Dc + Lc[RUNWAY:] + Rc
            else:
                borrow = "L"
                chunks = Rc[:RUNWAY] + Dc + Rc[RUNWAY:]
            nch = len(chunks)
            bDA = bR if borrow == "R" else bL
            lastof = {}
            firstof = {}
            for i, (kind, cc, j) in enumerate(chunks):
                lastof[kind] = i
                firstof.setdefault(kind, i)

            def emit_S(i):
                kind, cc, j = chunks[i]
                buf = i % 2
                for h in range(2):
                    dep = last.get(("expS", h, buf))
                    if dep:
                        c.wait("T", dep)
                    ins = nc.tensor.matmul(
                        bS[h][buf],
                        s_KT[64 * h:64 * h + 64, 128 * cc:128 * cc + 128],
                        s_QT[64 * h:64 * h + 64, 512 * qt:512 * qt + 512],
                        start=True, stop=True, tile_position=(64 * h, 0))
                return c.inc("T", ins)

            def emit_mid(i, tS):
                kind, cc, j = chunks[i]
                buf = i % 2
                pslot = i % NP
                if kind == "D":
                    c.wait("V", tS)
                    for h in range(2):
                        hl = 2 * p + h
                        dep = last.get(("expD", h, j % 2))
                        if dep:
                            c.wait("V", dep)
                        ins = nc.vector.scalar_tensor_tensor(
                            s_ds[h][j % 2], s_d0[:, 512 * j:512 * j + 512],
                            s_n8ab[:, hl:hl + 1], bS[h][buf], ALU.mult, ALU.add)
                    tD = c.inc("V", ins)
                    last[("expS", 0, buf)] = tD
                    last[("expS", 1, buf)] = tD
                    c.wait("A", tD)
                    for h in range(2):
                        hl = 2 * p + h
                        dep = last.get(("avP", h, pslot))
                        if dep:
                            c.wait("A", dep)
                        ins = nc.scalar.activation(
                            s_P[h][pslot], s_ds[h][j % 2], AF.Exp, scale=0.125,
                            bias=s_nub[:, hl:hl + 1])
                    tE = c.inc("A", ins)
                    last[("expD", 0, j % 2)] = tE
                    last[("expD", 1, j % 2)] = tE
                else:
                    c.wait("A", tS)
                    c.wait("A", tb[0])
                    c.wait("A", tb[1])
                    for h in range(2):
                        dep = last.get(("avP", h, pslot))
                        if dep:
                            c.wait("A", dep)
                        ins = nc.scalar.activation(
                            s_P[h][pslot], bS[h][buf], AF.Exp, scale=0.125,
                            bias=s_bsl[h][:, cc:cc + 1])
                    tE = c.inc("A", ins)
                    last[("expS", 0, buf)] = tE
                    last[("expS", 1, buf)] = tE
                return tE

            def emit_AV(i, tE):
                kind, cc, j = chunks[i]
                pslot = i % NP
                first = (i == firstof[kind])
                lastk = (i == lastof[kind])
                bACC = bDA if kind == "D" else (bL if kind == "L" else bR)
                c.wait("T", tE)
                for h in range(2):
                    if first:
                        key = ("acc" + borrow, h) if kind == "D" else ("acc" + kind, h)
                        dep = last.get(key)
                        if dep:
                            c.wait("T", dep)
                        if kind == "R" and borrow == "R":
                            dep = last.get(("accDevac", h))
                            if dep:
                                c.wait("T", dep)
                        if kind == "L" and borrow == "L":
                            dep = last.get(("accDevac", h))
                            if dep:
                                c.wait("T", dep)
                    ins = nc.tensor.matmul(
                        bACC[h][0:65, :], s_V[h][:, 65 * cc:65 * cc + 65],
                        s_P[h][pslot], start=first, stop=lastk)
                tAV = c.inc("T", ins)
                last[("avP", 0, pslot)] = tAV
                last[("avP", 1, pslot)] = tAV
                if kind == "D" and lastk:
                    c.wait("V", tAV)
                    for h in range(2):
                        nc.vector.tensor_copy(s_dh[h][0:65, :], bDA[h][0:65, :])
                        ins = nc.vector.tensor_copy(s_dh[h][0:65, :],
                                                    bDA[h][0:65, :])
                    tDh = c.inc("V", ins)
                    last[("accDevac", 0)] = tDh
                    last[("accDevac", 1)] = tDh
                return tAV

            tS_l = {0: emit_S(0)}
            tE_l = {}
            tAV = None
            avdone = 0
            epi_T_done = pend is None
            epi_A_done = pend is None
            for i in range(nch):
                tE_l[i] = emit_mid(i, tS_l[i])
                if i + 1 < nch:
                    tS_l[i + 1] = emit_S(i + 1)
                if not epi_A_done and i >= EPIPOS - 2:
                    emit_epi_A(pend)
                    epi_A_done = True
                if not epi_T_done and i >= EPIPOS:
                    emit_epi_T(pend)
                    emit_epi_V2(pend)
                    epi_T_done = True
                navs = 0 if i < AVLAG else (2 if i >= nch - AVLAG else 1)
                for _ in range(navs):
                    if avdone <= i:
                        tAV = emit_AV(avdone, tE_l[avdone])
                        avdone += 1
            if not epi_T_done:
                if not epi_A_done:
                    emit_epi_A(pend)
                emit_epi_T(pend)
                emit_epi_V2(pend)
            while avdone < nch:
                tAV = emit_AV(avdone, tE_l[avdone])
                avdone += 1
            last[("bsl", 0)] = ("A", c.cnt["A"])
            last[("bsl", 1)] = ("A", c.cnt["A"])
            pend = {"qt": qt, "tAV": tAV}
        emit_epi_V1(pend)
        emit_epi_A(pend)
        emit_epi_T(pend)
        emit_epi_V2(pend)
        pend = None
        A.reset(mark)

    proj_all()
    attention_pair(0)
    tok_cc1 = None
    if STAGE >= 3:
        c.wait("G", ("cat0", c.cnt.get("cat0", 0)))
        c.wait("G", ("cat1", c.cnt.get("cat1", 0)))
        ccsem = c.dsem("cc")
        nc.gpsimd.collective_compute(
            "AllGather", ALU.bypass,
            replica_groups=[[0, 1, 2, 3], [4, 5, 6, 7]],
            ins=[d_cat.ap()[0:128, :].opt()], outs=[d_all1.ap().opt()],
        ).then_inc(ccsem)
        c.cnt["cc"] = c.cnt.get("cc", 0) + 1
        tok_cc1 = ("cc", c.cnt["cc"])
    attention_pair(1)

    tok_cc = None
    if STAGE >= 3:
        c.wait("G", ("cat0", c.cnt.get("cat0", 0)))
        c.wait("G", ("cat1", c.cnt.get("cat1", 0)))
        ccsem = c.dsem("cc")
        nc.gpsimd.collective_compute(
            "AllGather", ALU.bypass,
            replica_groups=[[0, 1, 2, 3], [4, 5, 6, 7]],
            ins=[d_cat.ap()[128:256, :].opt()], outs=[d_all2.ap().opt()],
        ).then_inc(ccsem)
        c.cnt["cc"] = c.cnt.get("cc", 0) + 1
        tok_cc = ("cc", c.cnt["cc"])
    if STAGE >= 4:
        c.join()
        A.reset(0)
        markB = A.mark()
        s_wo = A.alloc(IC * D, BF16)
        s_aoq = [A.alloc(IC * 512, BF16) for _ in range(2)]
        s_xoq = [A.alloc(512, F32) for _ in range(2)]
        s_u = A.alloc(IC * 512, F32)
        s_h = A.alloc(OCH * 512, BF16)
        s_h2 = A.alloc(OCH * 512, BF16)
        s_hws = [A.alloc(OCH * 128, BF16) for _ in range(3)]
        s_sqB = [A.alloc(512, F32) for _ in range(2)]
        s_mr = A.alloc(QT, F32)[0:1, :]
        s_er = A.alloc(QT, F32)[0:1, :]
        s_vr = A.alloc(QT, F32)[0:1, :]
        s_vr2 = A.alloc(QT, F32)[0:1, :]
        s_vr3 = A.alloc(QT, F32)[0:1, :]
        s_arw = A.alloc(QT, F32)[0:1, :]
        s_brw = A.alloc(QT, F32)[0:1, :]
        s_brw2 = A.alloc(QT, F32)[0:1, :]
        u2_off = (A.mark() + 31) // 32 * 32
        s_x1b = A.alloc(IC * 512, BF16)
        s_w1s = [A.alloc(IC * 128, BF16) for _ in range(3)]
        s_w2s = [A.alloc(IC * 128, BF16) for _ in range(3)]
        s_u2 = A.at(u2_off, IC * 512, F32)

        two = c.dma("G", "wo", s_wo, d_wo)
        pf = {}
        for oc in range(3):
            pf[("w1s", oc)] = c.dma("G", f"w1s{oc % 3}", s_w1s[oc % 3], d_w1[oc])
            pf[("w2s", oc)] = c.dma("G", f"w2s{oc % 3}", s_w2s[oc % 3], d_w2[oc])
            pf[("hws", oc)] = c.dma("G", f"hws{oc % 3}", s_hws[oc % 3], d_h1[oc])

        def ln_rows(bM1, bM2, tstats):
            c.wait("V", tstats)
            nc.vector.tensor_scalar(s_mr, bM1[0:1, :], 1.0 / D, None, ALU.mult)
            nc.vector.tensor_scalar(s_mr, bM1[0:1, :], 1.0 / D, None, ALU.mult)
            nc.vector.tensor_scalar(s_er, bM2[0:1, :], 1.0 / D, None, ALU.mult)
            nc.vector.tensor_scalar(s_er, bM2[0:1, :], 1.0 / D, None, ALU.mult)
            nc.vector.tensor_mul(s_vr2, s_mr, s_mr)
            nc.vector.tensor_mul(s_vr2, s_mr, s_mr)
            nc.vector.tensor_sub(s_vr, s_er, s_vr2)
            tvv_ = c.inc("V", nc.vector.tensor_sub(s_vr, s_er, s_vr2))
            c.wait("A", tvv_)
            nc.scalar.activation(s_vr3, s_vr, AF.Ln, bias=s_eps)
            tra = c.inc("A", nc.scalar.activation(s_arw, s_vr3, AF.Exp, scale=-0.5))
            c.wait("V", tra)
            nc.vector.tensor_mul(s_brw, s_mr, s_arw)
            nc.vector.tensor_scalar_mul(s_brw2, s_brw, -1.0)
            return c.inc("V", nc.vector.tensor_scalar_mul(s_brw2, s_brw, -1.0))

        for g in range(4):
            asl = g % 2
            depao = last.get(("aoq", asl))
            aodeps = [depao] if depao else []
            if tok_cc1 is not None and g == 0:
                aodeps += [tok_cc1, tok_cc]
            c.dma("G", f"aoq{asl}",
                  s_aoq[asl].rearrange("p (a t) -> p a t", t=512)[:, 0::2, :],
                  d_all1.ap().rearrange("(a p) t -> p a t", p=128)
                  [:, :, ds((vhg + 4 * g) * 512, 512)],
                  deps=aodeps)
            tao = c.dma("G", f"aoq{asl}",
                        s_aoq[asl].rearrange("p (a t) -> p a t", t=512)[:, 1::2, :],
                        d_all2.ap().rearrange("(a p) t -> p a t", p=128)
                        [:, :, ds((vhg + 4 * g) * 512, 512)],
                        deps=aodeps)
            tu = None
            tus = {}
            ubanks = [pb[0], pb[1], pb[4], pb[5]]
            for oc in range(IC):
                sl = oc % 2
                dep = last.get(("xoq", sl))
                txo = c.dma("Y", f"xoq{sl}", s_xoq[sl],
                            d_xo.rearrange("p (a t) -> p a t", t=2048)
                            [:, oc, 512 * g:512 * g + 512],
                            deps=[dep] if dep else [])
                bU = ubanks[oc % 4]
                c.wait("T", tao)
                c.wait("T", two)
                dep = last.get(("bU", oc % 4))
                if dep:
                    c.wait("T", dep)
                for ic in range(IC):
                    ins = nc.tensor.matmul(
                        bU, s_wo[:, 1024 * ic + 128 * oc:1024 * ic + 128 * oc + 128],
                        s_aoq[asl][:, 512 * ic:512 * ic + 512],
                        start=(ic == 0), stop=(ic == IC - 1))
                tm = c.inc("T", ins)
                c.wait("V", tm)
                c.wait("V", txo)
                tu = c.inc("V", nc.vector.scalar_tensor_tensor(
                    s_u[:, 512 * oc:512 * oc + 512], bU, s_bo[:, oc:oc + 1],
                    s_xoq[sl], ALU.add, ALU.add))
                tus[oc] = tu
                last[("bU", oc % 4)] = tu
                last[("xoq", sl)] = tu
            last[("aoq", asl)] = tm
            tst = None
            for oc in range(IC):
                dep = last.get(("sqB", oc % 2))
                if dep:
                    c.wait("A", dep)
                c.wait("A", tus[oc])
                tsq = c.inc("A", nc.scalar.activation(
                    s_sqB[oc % 2], s_u[:, 512 * oc:512 * oc + 512], AF.Square))
                c.wait("T", tus[oc])
                c.wait("T", tsq)
                if oc == 0:
                    dep = last.get(("pbst",))
                    if dep:
                        c.wait("T", dep)
                nc.tensor.matmul(pb[2][0:1, :], s_onc,
                                 s_u[:, 512 * oc:512 * oc + 512],
                                 start=(oc == 0), stop=(oc == IC - 1))
                tst = c.inc("T", nc.tensor.matmul(
                    pb[3][0:1, :], s_onc, s_sqB[oc % 2],
                    start=(oc == 0), stop=(oc == IC - 1)))
                last[("sqB", oc % 2)] = tst
            trows = ln_rows(pb[2], pb[3], tst)
            c.wait("T", trows)
            nc.tensor.matmul(pb[2], s_onf, s_arw, start=True, stop=True)
            nc.tensor.matmul(pb[2], s_onf, s_arw, start=True, stop=True)
            tbc = c.mm2(nc, pb[3], s_onf, s_brw2, start=True, stop=True)
            tx1 = None
            for oc in range(IC):
                c.wait("V", tbc)
                nc.vector.tensor_mul(s_sqB[oc % 2],
                                     s_u[:, 512 * oc:512 * oc + 512], pb[2])
                nc.vector.tensor_add(s_u[:, 512 * oc:512 * oc + 512],
                                     s_sqB[oc % 2], pb[3])
                tx1 = c.inc("V", nc.vector.tensor_copy(
                    s_x1b[:, 512 * oc:512 * oc + 512],
                    s_u[:, 512 * oc:512 * oc + 512]))
            last[("pbst",)] = tx1

            if "dbg_x1" in dbg_outs and g == 0:
                c.dma("Y", "dbg", dbg_outs["dbg_x1"], s_u, deps=[tx1])
            if "dbg_ao" in dbg_outs and g == 0:
                c.dma("Y", "dbg", dbg_outs["dbg_ao"], s_aoq, deps=[tao])

            th = None
            for oc in range(OCH):
                sl = oc % 3
                if ("w1s", oc) in pf and g == 0:
                    t1 = pf[("w1s", oc)]
                    t2 = pf[("w2s", oc)]
                else:
                    dep = last.get(("w1s", sl))
                    t1 = c.dma("G", f"w1s{sl}", s_w1s[sl], d_w1[oc],
                               deps=[dep] if dep else [])
                    dep = last.get(("w2s", sl))
                    t2 = c.dma("G", f"w2s{sl}", s_w2s[sl], d_w2[oc],
                               deps=[dep] if dep else [])
                b1 = pb[4 + (oc % 2) * 2]
                b2 = pb[5 + (oc % 2) * 2]
                c.wait("T", t1)
                c.wait("T", tx1)
                dep = last.get(("b1", oc % 2))
                if dep:
                    c.wait("T", dep)
                if oc < 2:
                    dep = last.get(("bU", 2 + oc % 2))
                    if dep:
                        c.wait("T", dep)
                for ic in range(IC):
                    ins = nc.tensor.matmul(
                        b1, s_w1s[sl][:, 128 * ic:128 * ic + 128],
                        s_x1b[:, 512 * ic:512 * ic + 512],
                        start=(ic == 0), stop=(ic == IC - 1))
                tm1 = c.inc("T", ins)
                c.wait("T", t2)
                for ic in range(IC):
                    ins = nc.tensor.matmul(
                        b2, s_w2s[sl][:, 128 * ic:128 * ic + 128],
                        s_x1b[:, 512 * ic:512 * ic + 512],
                        start=(ic == 0), stop=(ic == IC - 1))
                tm2 = c.inc("T", ins)
                c.wait("A", tm1)
                tsi = c.inc("A", nc.scalar.activation(
                    s_h[:, 512 * oc:512 * oc + 512], b1, AF.Silu,
                    scale=s_s1[:, oc:oc + 1]))
                c.wait("V", tm2)
                c.wait("V", tsi)
                th = c.inc("V", nc.vector.scalar_tensor_tensor(
                    s_h[:, 512 * oc:512 * oc + 512], b2, s_s2[:, oc:oc + 1],
                    s_h[:, 512 * oc:512 * oc + 512], ALU.mult, ALU.mult))
                last[("b1", oc % 2)] = th
                last[("w1s", sl)] = tm1
                last[("w2s", sl)] = tm2

            def hstage(dsrc, src_ap, dst_ap, scol, noc, wait_tok, resid=False, wtag_pf=False):
                tg_ = None
                for oc in range(noc):
                    sl = oc % 3
                    if ("hws", oc) in pf and wtag_pf:
                        tw = pf.pop(("hws", oc))
                    else:
                        dep = last.get(("hws", sl))
                        tw = c.dma("G", f"hws{sl}", s_hws[sl], dsrc[oc],
                                   deps=[dep] if dep else [])
                    bk = pb[4 + oc % 2]
                    c.wait("T", tw)
                    c.wait("T", wait_tok)
                    dep = last.get(("bh", oc % 2))
                    if dep:
                        c.wait("T", dep)
                    for icp in range(OCH):
                        ins = nc.tensor.matmul(
                            bk, s_hws[sl][:, 128 * icp:128 * icp + 128],
                            src_ap[:, 512 * icp:512 * icp + 512],
                            start=(icp == 0), stop=(icp == OCH - 1))
                    tm = c.inc("T", ins)
                    if not resid:
                        c.wait("A", tm)
                        tg_ = c.inc("A", nc.scalar.activation(
                            dst_ap[:, 512 * oc:512 * oc + 512], bk, AF.Gelu,
                            scale=scol[:, oc:oc + 1]))
                    else:
                        c.wait("V", tm)
                        tg_ = c.inc("V", nc.vector.scalar_tensor_tensor(
                            dst_ap[:, 512 * oc:512 * oc + 512], bk,
                            scol[:, oc:oc + 1],
                            s_u[:, 512 * oc:512 * oc + 512],
                            ALU.mult, ALU.add))
                        dep = last.get(("sqB", oc % 2))
                        if dep:
                            c.wait("A", dep)
                        c.wait("A", tg_)
                        tsq = c.inc("A", nc.scalar.activation(
                            s_sqB[oc % 2],
                            dst_ap[:, 512 * oc:512 * oc + 512], AF.Square))
                        c.wait("T", tg_)
                        c.wait("T", tsq)
                        if oc == 0:
                            dep2 = last.get(("pbst",))
                            if dep2:
                                c.wait("T", dep2)
                        nc.tensor.matmul(
                            pb[2][0:1, :], s_onc,
                            dst_ap[:, 512 * oc:512 * oc + 512],
                            start=(oc == 0), stop=(oc == noc - 1))
                        tst2 = c.inc("T", nc.tensor.matmul(
                            pb[3][0:1, :], s_onc, s_sqB[oc % 2],
                            start=(oc == 0), stop=(oc == noc - 1)))
                        last[("sqB", oc % 2)] = tst2
                        last[("lnst",)] = tst2
                    last[("bh", oc % 2)] = tg_
                    last[("hws", sl)] = tm
                return tg_

            tgel1 = hstage(d_h1, s_h, s_h2, s_s3, OCH, th, wtag_pf=(g == 0))
            tgel2 = hstage(d_h2, s_h2, s_h, s_s4, OCH, tgel1)
            tu2 = hstage(d_w3, s_h, s_u2, s_s5, IC, tgel2, resid=True)

            trows = ln_rows(pb[2], pb[3], last[("lnst",)])
            c.wait("T", trows)
            nc.tensor.matmul(pb[2], s_onf, s_arw, start=True, stop=True)
            nc.tensor.matmul(pb[2], s_onf, s_arw, start=True, stop=True)
            tbc = c.mm2(nc, pb[3], s_onf, s_brw2, start=True, stop=True)
            tfin = None
            for oc in range(IC):
                c.wait("V", tbc)
                nc.vector.tensor_mul(s_sqB[oc % 2],
                                     s_u2[:, 512 * oc:512 * oc + 512], pb[2])
                tfin = c.inc("V", nc.vector.tensor_add(
                    s_u2[:, 512 * oc:512 * oc + 512],
                    s_sqB[oc % 2], pb[3]))
            last[("pbst",)] = tfin
            tout = c.dma("Y", "out",
                         d_out.rearrange("p (a t) -> p a t", t=2048)
                         [:, :, 512 * g:512 * g + 512],
                         s_u2.rearrange("p (a t) -> p a t", t=512),
                         deps=[tfin])
            c.wait("V", tout)
            for _sl in range(3):
                last[("w1s", _sl)] = tout
                last[("w2s", _sl)] = tout
        A.reset(markB)
    else:
        c.wait("Y", tv0)
        c.dma("Y", "out", d_out[:, 0:1024], s_tq[:, 0:1024], deps=[tok_cst])

    c.wait("Y", ("out", c.cnt.get("out", 0)))
    if 2 <= STAGE < 3:
        c.wait("Y", ("cat0", c.cnt.get("cat0", 0)))
        c.wait("Y", ("cat1", c.cnt.get("cat1", 0)))
    if dbg_outs:
        c.wait("Y", ("dbg", c.cnt.get("dbg", 0)))
        c.wait("G", ("dbg", c.cnt.get("dbg", 0)))
    return nc


_CACHE = {}


def _host_tables():
    tq = np.zeros((NQT, 128, NCH), np.float32)
    dj = np.arange(128).reshape(128, 1)
    cc = np.arange(NCH).reshape(1, NCH)
    j = 128 * cc + dj
    for qt in range(NQT):
        i0 = 512 * qt
        i1 = i0 + 511
        tq[qt] = np.where(cc < 4 * qt, j - i0,
                          np.where(cc >= 4 * qt + 4, i1 - j, 0.0))
    tqp = np.ascontiguousarray(tq.transpose(1, 0, 2).reshape(128, NQT * NCH))
    d0 = np.zeros((4, 128, QT), np.float32)
    di = np.arange(QT).reshape(1, QT)
    for cp in range(4):
        d0[cp] = np.abs(128 * cp + np.arange(128).reshape(128, 1) - di)
    d0p = np.ascontiguousarray(d0.transpose(1, 0, 2).reshape(128, 4 * QT))
    return tqp, d0p


def _pcol(v, p=128):
    v = np.asarray(v, np.float32).reshape(-1)
    return np.ascontiguousarray(v.reshape(-1, p).T)


def _wblocks(wt, och):
    I, O = wt.shape
    icn = I // 128
    r = wt.reshape(icn, 128, och, 128).transpose(2, 1, 0, 3)
    return np.ascontiguousarray(r.reshape(och, 128, icn * 128))


def _prows(wt):
    I, O = wt.shape
    icn = I // 128
    return np.ascontiguousarray(
        wt.reshape(icn, 128, O).transpose(1, 0, 2).reshape(128, icn * O))


def kernel(L0_attn_x, L0_attn_qkv_w, L0_attn_qkv_b, L0_attn_out_w, L0_attn_out_b,
           L0_attn_decay_alpha, L0_ffn_w1_w, L0_ffn_w1_s, L0_ffn_w2_w, L0_ffn_w2_s,
           L0_ffn_h1_w, L0_ffn_h1_s, L0_ffn_h2_w, L0_ffn_h2_s, L0_ffn_w3_w,
           L0_ffn_w3_s, _debug=(), _stage=None):
    global STAGE
    if _stage is not None:
        STAGE = _stage
    x = np.asarray(L0_attn_x, np.float32)
    qkv_w = np.asarray(L0_attn_qkv_w, np.float32)
    qkv_b = np.asarray(L0_attn_qkv_b, np.float32)
    out_w = np.asarray(L0_attn_out_w, np.float32)
    out_b = np.asarray(L0_attn_out_b, np.float32)
    alpha = np.asarray(L0_attn_decay_alpha, np.float32)

    key = ("nc", STAGE, tuple(sorted(d[0] for d in _debug)))
    if key not in _CACHE:
        _CACHE[key] = build_nc(debug=_debug)
    nc = _CACHE[key]

    tqp, d0p = _host_tables()
    shared = {
        "wop": _prows(np.ascontiguousarray(out_w.T)),
        "bop": _pcol(out_b),
        "w1p": _wblocks(np.ascontiguousarray(np.asarray(L0_ffn_w1_w, np.float32).T), OCH),
        "w2p": _wblocks(np.ascontiguousarray(np.asarray(L0_ffn_w2_w, np.float32).T), OCH),
        "h1p": _wblocks(np.ascontiguousarray(np.asarray(L0_ffn_h1_w, np.float32).T), OCH),
        "h2p": _wblocks(np.ascontiguousarray(np.asarray(L0_ffn_h2_w, np.float32).T), OCH),
        "w3p": _wblocks(np.ascontiguousarray(np.asarray(L0_ffn_w3_w, np.float32).T), 8),
        "s1p": _pcol(L0_ffn_w1_s), "s2p": _pcol(L0_ffn_w2_s),
        "s3p": _pcol(L0_ffn_h1_s), "s4p": _pcol(L0_ffn_h2_s),
        "s5p": _pcol(L0_ffn_w3_s),
        "tqp": tqp, "d0p": d0p,
        "dqp": np.arange(QT, dtype=np.float32).reshape(1, QT),
        "onp": np.ones((1, 128), np.float32),
        "dqbp": np.tile(np.arange(QT, dtype=np.float32), (128, 1)),
        "eyp": np.concatenate([np.eye(2, dtype=np.float32), np.zeros((2, 510), np.float32)], axis=1),
    }
    xtp = [_prows(np.ascontiguousarray(x[b].T)) for b in range(2)]

    in_maps = []
    for core in range(8):
        b, hg = core // 4, core % 4
        xo = np.concatenate([x[b, 2048 * g + 512 * hg:2048 * g + 512 * hg + 512]
                             for g in range(4)], axis=0)
        m = dict(shared)
        m["xtp"] = xtp[b]
        m["xtop"] = _prows(np.ascontiguousarray(xo.T))
        m["wqp"] = _prows(np.ascontiguousarray(
            qkv_w[256 * hg:256 * hg + 256, :].T))
        m["wkp"] = _prows(np.ascontiguousarray(
            qkv_w[1024 + 256 * hg:1024 + 256 * hg + 256, :].T))
        m["wvp"] = _prows(np.ascontiguousarray(
            qkv_w[2048 + 256 * hg:2048 + 256 * hg + 256, :].T))
        bq = qkv_b[256 * hg:256 * hg + 256].reshape(2, 128)
        bk = qkv_b[1024 + 256 * hg:1024 + 256 * hg + 256].reshape(2, 128)
        m["bqkp"] = np.ascontiguousarray(np.stack(
            [bq[0], bq[1], bk[0], bk[1]], axis=1))
        m["bvp"] = np.ascontiguousarray(
            qkv_b[2048 + 256 * hg:2048 + 256 * hg + 256].reshape(1, 256))
        m["alp"] = np.ascontiguousarray(alpha[4 * hg:4 * hg + 4].reshape(1, 4))
        m["hgp"] = np.array([[hg]], np.int32)
        in_maps.append(m)

    res = run_bass_kernel_spmd(nc, in_maps, core_ids=list(range(8)))

    out = np.empty((2, 8192, 1024), np.float32)
    for core in range(8):
        b, hg = core // 4, core % 4
        o = np.asarray(res.results[core]["outp"])
        o = o.reshape(128, 8, 2048).transpose(1, 0, 2).reshape(1024, 2048)
        for g in range(4):
            out[b, 2048 * g + 512 * hg:2048 * g + 512 * hg + 512, :] = \
                o[:, 512 * g:512 * g + 512].T
    if _debug:
        return out, [dict(r) for r in res.results]
    return out


# version 10
# speedup vs baseline: 1.1022x; 1.0051x over previous
import numpy as np
import ml_dtypes
import concourse.bass as bass
import concourse.mybir as mybir
from concourse.bass_utils import run_bass_kernel_spmd
from concourse.expressions import RuntimeValue

F32 = mybir.dt.float32
BF16 = mybir.dt.bfloat16
I32 = mybir.dt.int32
AF = mybir.ActivationFunctionType
ALU = mybir.AluOpType
ds = bass.ds

D = 1024
T = 8192
IC = 8
QT = 512
NQT = T // QT
NCH = T // 128
DH = 4096
OCH = DH // 128
NLN8 = -2.0794415416798357

STAGE = 4


class Ctx:

    def __init__(self, nc):
        self.nc = nc
        self.eng = {"T": nc.tensor, "A": nc.scalar, "V": nc.vector,
                    "G": nc.gpsimd, "Y": nc.sync}
        self.sem = {k: nc.alloc_semaphore("sm" + k) for k in self.eng}
        self.pool = [nc.alloc_semaphore(f"sp{i}") for i in range(56)]
        self.cnt = {k: 0 for k in self.eng}
        self.water = {}
        nums = [s.num for s in list(self.sem.values()) + self.pool]
        assert max(nums) - min(nums) + 1 == len(nums), "sem range not contiguous"
        nc.gpsimd.sem_clear(range(min(nums), max(nums) + 1))
        nc.all_engine_barrier()

    def dsem(self, name):
        if name not in self.sem:
            self.sem[name] = self.pool.pop(0)
            self.cnt[name] = 0
        return self.sem[name]

    def wait(self, who, tok):
        if tok is None:
            return
        name, val = tok
        if name == who or val <= 0:
            return
        if self.water.get((who, name), 0) >= val:
            return
        self.water[(who, name)] = val
        self.eng[who].wait_ge(self.sem[name], val)

    def mm2(self, nc, *args, **kw):
        nc.tensor.matmul(*args, **kw)
        return self.inc("T", nc.tensor.matmul(*args, **kw))

    def v2(self, emit):
        emit()
        return self.inc("V", emit())

    def inc(self, who, ins):
        self.cnt[who] += 1
        ins.then_inc(self.sem[who], 1)
        return (who, self.cnt[who])

    def dma(self, who, name, dst, src, deps=()):
        for d in deps:
            self.wait(who, d)
        s = self.dsem(name)
        self.eng[who].dma_start(dst, src).then_inc(s, 16)
        self.cnt[name] = self.cnt.get(name, 0) + 16
        return (name, self.cnt[name])

    def join(self, engines=("T", "A", "V", "G", "Y")):
        toks = [(k, self.cnt[k]) for k in engines]
        for who in engines:
            for t in toks:
                self.wait(who, t)


class Arena:
    def __init__(self, nc, nbytes):
        self.ap = nc.alloc_sbuf_tensor("arena", [128, nbytes], mybir.dt.int8).ap()
        self.off = 0
        self.nbytes = nbytes

    def alloc(self, cols, dtype):
        esz = mybir.dt.size(dtype)
        n = cols * esz
        off = (self.off + 31) // 32 * 32
        assert off + n <= self.nbytes, f"arena overflow: {off + n} > {self.nbytes}"
        self.off = off + n
        return self.ap[:, off:off + n].bitcast(dtype)

    def mark(self):
        return self.off

    def reset(self, mark):
        self.off = mark

    def at(self, off, cols, dtype):
        n = cols * mybir.dt.size(dtype)
        assert off % 32 == 0 and off + n <= self.nbytes
        return self.ap[:, off:off + n].bitcast(dtype)


def build_nc(debug=()):
    nc = bass.Bass("TRN2", target_bir_lowering=False)
    c = Ctx(nc)

    def din(name, shape, dt=F32):
        return nc.dram_tensor(name, shape, dt, kind="ExternalInput").ap()

    d_xt = din("xtp", [128, IC * T])
    d_xo = din("xtop", [128, IC * 2048])
    d_wq = din("wqp", [128, IC * 256])
    d_wk = din("wkp", [128, IC * 256])
    d_wv = din("wvp", [128, IC * 256])
    d_bqk = din("bqkp", [128, 4])
    d_bv = din("bvp", [1, 256])
    d_al = din("alp", [1, 4])
    d_wo = din("wop", [128, IC * D])
    d_bo = din("bop", [128, 8])
    d_w1 = din("w1p", [OCH, 128, IC * 128])
    d_w2 = din("w2p", [OCH, 128, IC * 128])
    d_h1 = din("h1p", [OCH, 128, OCH * 128])
    d_h2 = din("h2p", [OCH, 128, OCH * 128])
    d_w3 = din("w3p", [8, 128, OCH * 128])
    d_s1 = din("s1p", [128, OCH])
    d_s2 = din("s2p", [128, OCH])
    d_s3 = din("s3p", [128, OCH])
    d_s4 = din("s4p", [128, OCH])
    d_s5 = din("s5p", [128, 8])
    d_tq = din("tqp", [128, NQT * NCH])
    d_d0 = din("d0p", [128, 4 * QT])
    d_dq = din("dqp", [1, QT])
    d_dqb = din("dqbp", [128, QT])
    d_on = din("onp", [1, 128])
    d_ey = din("eyp", [2, 512])
    d_hg = din("hgp", [1, 1], I32)

    d_cat = nc.dram_tensor("attn_cat", [4 * 256, 2048], BF16, kind="Internal")
    d_allq = [nc.dram_tensor(f"attn_all{g}", [1024, 2048], BF16, kind="Internal")
              for g in range(4)]
    d_out = nc.dram_tensor("outp", [128, 8 * 2048], F32, kind="ExternalOutput").ap()

    dbg_outs = {}
    for name, shape, dt in debug:
        dbg_outs[name] = nc.dram_tensor(name, shape, dt, kind="ExternalOutput").ap()

    def sb(name, shape, dt):
        return nc.alloc_sbuf_tensor(name, shape, dt).ap()

    s_tq = sb("s_tq", [128, NQT * NCH], F32)
    s_d0 = sb("s_d0", [128, 4 * QT], F32)
    s_dq = sb("s_dq", [1, QT], F32)
    s_dqb = sb("s_dqb", [128, QT], F32)
    s_onf = sb("s_onf", [1, 128], F32)
    s_onb = sb("s_onb", [1, 128], BF16)
    s_ey = sb("s_ey", [2, 512], F32)
    s_bqk = sb("s_bqk", [128, 4], F32)
    s_bv = sb("s_bv", [1, 256], F32)
    s_al = sb("s_al", [1, 4], F32)
    s_hgi = sb("s_hgi", [1, 1], I32)
    s_bo = sb("s_bo", [128, 8], F32)
    s_s1 = sb("s_s1", [128, OCH], F32)
    s_s2 = sb("s_s2", [128, OCH], F32)
    s_s3 = sb("s_s3", [128, OCH], F32)
    s_s4 = sb("s_s4", [128, OCH], F32)
    s_s5 = sb("s_s5", [128, 8], F32)
    s_ar = sb("s_ar", [1, 512], F32)
    s_nar = sb("s_nar", [1, 4], F32)
    s_m511 = sb("s_m511", [1, 4], F32)
    s_ab = sb("s_ab", [128, 4], F32)
    s_n8ab = sb("s_n8ab", [128, 4], F32)
    s_m511b = sb("s_m511b", [128, 4], F32)
    s_ur = sb("s_ur", [1, 512], F32)
    s_nub = sb("s_nub", [128, 4], F32)
    s_k2r = sb("s_k2r", [1, 4], F32)
    s_q2r = sb("s_q2r", [1, 4], F32)
    s_k2mP = [sb("s_k2m0", [2, 1], F32), sb("s_k2m1", [2, 1], F32)]
    s_q2mP = [sb("s_q2m0", [2, 1], F32), sb("s_q2m1", [2, 1], F32)]
    s_ind2 = sb("s_ind2", [128, 2], BF16)
    s_onc = sb("s_onc", [128, 1], F32)
    s_eps = sb("s_eps", [1, 1], F32)
    s_one = sb("s_one", [1, 1], F32)
    s_nl8 = sb("s_nl8", [1, 1], F32)
    s_t14a = sb("s_t14a", [1, 4], F32)
    s_t14b = sb("s_t14b", [1, 4], F32)
    s_bsl = [sb(f"s_bsl{h}", [128, NCH], F32) for h in range(2)]

    pb = [nc.alloc_psum_tensor(f"pb{i}", [128, 512], F32).ap() for i in range(8)]

    A = Arena(nc, 182 * 1024)
    last = {}

    for dst, src in [(s_tq, d_tq), (s_d0, d_d0), (s_dq, d_dq), (s_dqb, d_dqb), (s_onf, d_on),
                     (s_ey, d_ey), (s_bqk, d_bqk), (s_bv, d_bv), (s_al, d_al),
                     (s_hgi, d_hg), (s_bo, d_bo), (s_s1, d_s1), (s_s2, d_s2),
                     (s_s3, d_s3), (s_s4, d_s4), (s_s5, d_s5)]:
        c.dma("Y", "cst", dst, src)
    tok_cst = ("cst", c.cnt["cst"])
    tok_cstg = c.dma("G", "cstg", s_onb, d_on)

    c.wait("V", tok_cst)
    c.wait("V", tok_cstg)
    nc.vector.memset(s_onc, 1.0)
    nc.vector.memset(s_eps, 1e-5)
    nc.vector.memset(s_one, 1.0)
    nc.vector.memset(s_nl8, NLN8)
    nc.vector.memset(s_ind2, 0.0)
    nc.vector.memset(s_ind2[0:64, 0:1], 1.0)
    nc.vector.memset(s_ind2[64:128, 1:2], 1.0)
    nc.vector.memset(s_ur, 0.0)
    nc.vector.memset(s_ar, 0.0)
    nc.vector.memset(s_k2mP[0], 0.0)
    nc.vector.memset(s_k2mP[1], 0.0)
    nc.vector.memset(s_q2mP[0], 0.0)
    tv0 = c.inc("V", nc.vector.memset(s_q2mP[1], 0.0))

    c.wait("A", tok_cst)
    c.wait("A", tv0)
    nc.scalar.activation(s_t14a, s_al, AF.Abs)
    nc.scalar.activation(s_t14a, s_t14a, AF.Exp, scale=-1.0)
    nc.scalar.activation(s_t14a, s_t14a, AF.Ln, bias=s_one)
    ta0 = c.inc("A", nc.scalar.activation(s_t14b, s_al, AF.Relu))
    c.wait("V", ta0)
    nc.vector.tensor_add(s_ar[0:1, 0:4], s_t14a, s_t14b)
    nc.vector.tensor_scalar_mul(s_nar, s_ar[0:1, 0:4], -1.0)
    tv1 = c.inc("V", nc.vector.tensor_scalar_mul(s_m511, s_ar[0:1, 0:4], -511.0))
    c.wait("T", tv1)
    tmm0 = c.mm2(nc, pb[0], s_onf, s_ar[0:1, :], start=True, stop=True)
    c.wait("V", tmm0)
    nc.vector.tensor_copy(s_ab, pb[0][:, 0:4])
    nc.vector.tensor_copy(s_ab, pb[0][:, 0:4])
    nc.vector.tensor_scalar_mul(s_n8ab, s_ab, -8.0)
    nc.vector.tensor_scalar_mul(s_n8ab, s_ab, -8.0)
    nc.vector.tensor_scalar_mul(s_m511b, s_ab, -511.0)
    tv2 = c.inc("V", nc.vector.tensor_scalar_mul(s_m511b, s_ab, -511.0))
    last[("pb", 0)] = tv2

    c.wait("G", tok_cst)
    rhg = nc.gpsimd.alloc_register("rhg")
    nc.gpsimd.reg_load(rhg, s_hgi)
    vhg = RuntimeValue(rhg, min_val=0, max_val=3)

    pairst = {}
    cc_toks = {}

    def emit_collective(gq):
        c.wait("G", ("cat0", c.cnt.get("cat0", 0)))
        c.wait("G", ("cat1", c.cnt.get("cat1", 0)))
        ccsem = c.dsem("cc")
        nc.gpsimd.collective_compute(
            "AllGather", ALU.bypass,
            replica_groups=[[0, 1, 2, 3], [4, 5, 6, 7]],
            ins=[d_cat.ap()[256 * gq:256 * gq + 256, :].opt()],
            outs=[d_allq[gq].ap().opt()],
        ).then_inc(ccsem)
        c.cnt["cc"] = c.cnt.get("cc", 0) + 1
        cc_toks[gq] = ("cc", c.cnt["cc"])

    def proj_all():
        c.join()
        s_KT = [A.alloc(T, BF16) for _ in range(2)]
        s_QT = [A.alloc(T, BF16) for _ in range(2)]
        s_V = [[A.alloc(NCH * 65, BF16) for _ in range(2)] for _ in range(2)]
        pairst["KT"] = s_KT
        pairst["QT"] = s_QT
        pairst["V"] = s_V
        m1 = A.mark()
        s_xt = [A.alloc(IC * 512, BF16) for _ in range(2)]
        s_wqb = A.alloc(IC * 256, BF16)
        s_wkb = A.alloc(IC * 256, BF16)
        s_wvb = A.alloc(IC * 256, BF16)
        s_wv260 = A.alloc(IC * 260, BF16)
        s_bv260 = A.alloc(260, BF16)
        s_sq = A.alloc(512, BF16)
        s_sq2 = A.alloc(512, BF16)
        s_r21 = A.alloc(1, F32)
        s_r22 = A.alloc(1, F32)

        twq = c.dma("G", "wq0", s_wqb.rearrange("p (a o) -> p a o", o=256),
                    d_wq.rearrange("p (a o) -> p a o", o=256))
        twk = c.dma("G", "wk0", s_wkb.rearrange("p (a o) -> p a o", o=256),
                    d_wk.rearrange("p (a o) -> p a o", o=256))
        twv = c.dma("G", "wv0", s_wvb.rearrange("p (a o) -> p a o", o=256),
                    d_wv.rearrange("p (a o) -> p a o", o=256))
        c.wait("V", twv)
        c.wait("V", tok_cst)
        nc.vector.memset(s_wv260, 0.0)
        for ic in range(IC):
            for hl in range(4):
                nc.vector.tensor_copy(
                    s_wv260[:, 260 * ic + 65 * hl:260 * ic + 65 * hl + 64],
                    s_wvb[:, 256 * ic + 64 * hl:256 * ic + 64 * hl + 64])
        nc.vector.memset(s_bv260[0:1, 0:260], 0.0)
        for hl in range(4):
            nc.vector.tensor_copy(s_bv260[0:1, 65 * hl:65 * hl + 64],
                                  s_bv[0:1, 64 * hl:64 * hl + 64])
            nc.vector.memset(s_bv260[0:1, 65 * hl + 64:65 * hl + 65], 1.0)
        nc.vector.memset(s_bv260[0:1, 259:260], 1.0)
        tv260 = c.inc("V", nc.vector.memset(s_bv260[0:1, 259:260], 1.0))

        tev = {}
        tvv = None
        for tc in range(16):
            sl = tc % 2
            dep = last.get(("xt", sl))
            tx = c.dma("G", f"x{sl}", s_xt[sl].rearrange("p (a t) -> p a t", t=512),
                       d_xt.rearrange("p (a t) -> p a t", t=T)
                       [:, :, 512 * tc:512 * tc + 512],
                       deps=[dep] if dep else [])
            c.wait("T", tx)
            c.wait("T", twk)
            c.wait("T", twq)
            c.wait("T", tv260)
            for p2 in range(2):
                bK = pb[p2]
                bQ = pb[2 + p2]
                c.wait("T", tev.get(("K", p2)))
                for ic in range(IC):
                    ins = nc.tensor.matmul(
                        bK, s_wkb[:, 256 * ic + 128 * p2:256 * ic + 128 * p2 + 128],
                        s_xt[sl][:, 512 * ic:512 * ic + 512],
                        start=(ic == 0), stop=(ic == IC - 1))
                tk = c.inc("T", ins)
                c.wait("T", tev.get(("Q", p2)))
                for ic in range(IC):
                    ins = nc.tensor.matmul(
                        bQ, s_wqb[:, 256 * ic + 128 * p2:256 * ic + 128 * p2 + 128],
                        s_xt[sl][:, 512 * ic:512 * ic + 512],
                        start=(ic == 0), stop=(ic == IC - 1))
                tq_ = c.inc("T", ins)
                c.wait("V", tk)
                tev[("K", p2)] = c.inc("V", nc.vector.tensor_scalar(
                    s_KT[p2][:, 512 * tc:512 * tc + 512], bK,
                    s_bqk[:, 2 + p2:3 + p2], None, ALU.add))
                c.wait("V", tq_)
                tev[("Q", p2)] = c.inc("V", nc.vector.tensor_scalar(
                    s_QT[p2][:, 512 * tc:512 * tc + 512], bQ,
                    s_bqk[:, p2:p2 + 1], None, ALU.add))
            for ts in range(4):
                bV = pb[4 + ts % 2]
                kc = 4 * tc + ts
                c.wait("T", tev.get(("V", ts % 2)))
                for ic in range(IC):
                    nc.tensor.matmul(
                        bV[:, 0:260],
                        s_xt[sl][:, 512 * ic + 128 * ts:512 * ic + 128 * ts + 128],
                        s_wv260[:, 260 * ic:260 * ic + 260],
                        start=(ic == 0), stop=False)
                tvv = c.inc("T", nc.tensor.matmul(
                    bV[:, 0:260], s_onb, s_bv260[0:1, 0:260],
                    start=False, stop=True))
                c.wait("V", tvv)
                for p2 in range(2):
                    for h in range(2):
                        ins = nc.vector.tensor_copy(
                            s_V[p2][h][:, 65 * kc:65 * kc + 65],
                            bV[:, 130 * p2 + 65 * h:130 * p2 + 65 * h + 65])
                tev[("V", ts % 2)] = c.inc("V", ins)
            last[("xt", sl)] = tvv

        for key in (("K", 0), ("K", 1), ("Q", 0), ("Q", 1)):
            c.wait("A", tev[key])
        stat_src = []
        for p2 in range(2):
            stat_src += [(s_KT[p2], s_k2mP[p2], kc2) for kc2 in range(16)]
            stat_src += [(s_QT[p2], s_q2mP[p2], kc2) for kc2 in range(16)]
        sqs = [s_sq, s_sq2]
        NST = len(stat_src)
        tvm = None
        tsq_l = {}

        def stat_sq(i):
            srcb, _, kc2 = stat_src[i]
            dep = last.get(("stsq", i % 2))
            if dep:
                c.wait("A", dep)
            return c.inc("A", nc.scalar.activation(
                sqs[i % 2], srcb[:, 512 * kc2:512 * kc2 + 512], AF.Square))

        tsq_l[0] = stat_sq(0)
        for i in range(NST):
            bnk = pb[6] if i % 2 == 0 else pb[7]
            c.wait("T", tsq_l[i])
            c.wait("T", tv0)
            dep = last.get(("pbst6", i % 2))
            if dep:
                c.wait("T", dep)
            tm = c.mm2(nc, bnk[0:2, :], s_ind2, sqs[i % 2],
                       start=True, stop=True)
            last[("stsq", i % 2)] = tm
            if i + 1 < NST:
                tsq_l[i + 1] = stat_sq(i + 1)
            dstm = stat_src[i][1]
            c.wait("V", tm)
            if tvm is not None:
                c.wait("V", tvm)
            nc.vector.reduce_max(s_r21[0:2, 0:1], bnk[0:2, :],
                                 axis=mybir.AxisListType.X)
            nc.vector.reduce_max(s_r21[0:2, 0:1], bnk[0:2, :],
                                 axis=mybir.AxisListType.X)
            nc.vector.tensor_max(s_r22[0:2, 0:1], dstm, s_r21[0:2, 0:1])
            nc.vector.tensor_max(s_r22[0:2, 0:1], dstm, s_r21[0:2, 0:1])
            nc.vector.tensor_copy(dstm, s_r22[0:2, 0:1])
            tvm = c.inc("V", nc.vector.tensor_copy(dstm, s_r22[0:2, 0:1]))
            last[("pbst6", i % 2)] = tvm
        last[("pb", 6)] = tvm
        c.wait("T", tvm)

        for p2 in range(2):
            dep = last.get(("pb", 6))
            if dep:
                c.wait("T", dep)
            tk2 = c.mm2(nc, pb[6][0:1, 0:512], s_k2mP[p2], s_ey,
                        start=True, stop=True)
            c.wait("V", tk2)
            tvk = c.v2(lambda p2=p2: nc.vector.tensor_copy(
                s_k2r[0:1, 2 * p2:2 * p2 + 2], pb[6][0:1, 0:2]))
            c.wait("T", tvk)
            tq2 = c.mm2(nc, pb[6][0:1, 0:512], s_q2mP[p2], s_ey,
                        start=True, stop=True)
            c.wait("V", tq2)
            nc.vector.tensor_copy(s_q2r[0:1, 2 * p2:2 * p2 + 2], pb[6][0:1, 0:2])
            nc.vector.tensor_copy(s_q2r[0:1, 2 * p2:2 * p2 + 2], pb[6][0:1, 0:2])
            nc.vector.tensor_mul(
                s_t14a[0:1, 2 * p2:2 * p2 + 2], s_k2r[0:1, 2 * p2:2 * p2 + 2],
                s_q2r[0:1, 2 * p2:2 * p2 + 2])
            tvq = c.inc("V", nc.vector.tensor_mul(
                s_t14a[0:1, 2 * p2:2 * p2 + 2], s_k2r[0:1, 2 * p2:2 * p2 + 2],
                s_q2r[0:1, 2 * p2:2 * p2 + 2]))
            last[("pb", 6)] = tvq
            c.wait("A", tvq)
            nc.scalar.activation(s_t14b[0:1, 2 * p2:2 * p2 + 2],
                                 s_t14a[0:1, 2 * p2:2 * p2 + 2], AF.Ln)
            tua = c.inc("A", nc.scalar.activation(
                s_ur[0:1, 2 * p2:2 * p2 + 2], s_t14b[0:1, 2 * p2:2 * p2 + 2],
                AF.Exp, scale=0.5, bias=s_nl8))
            c.wait("T", tua)
            c.wait("T", tvq)
            tub = c.mm2(nc, pb[6], s_onf, s_ur[0:1, :], start=True, stop=True)
            c.wait("V", tub)
            tnu = c.v2(lambda p2=p2: nc.vector.tensor_scalar(
                s_nub[:, 2 * p2:2 * p2 + 2], pb[6][:, 2 * p2:2 * p2 + 2],
                -1.0, None, ALU.mult))
            last[("pb", 6)] = tnu
            pairst[("tnu", p2)] = tnu
        A.reset(m1)

    def attention_pair(p):
        c.join()
        mark = A.mark()
        s_KT = pairst["KT"][p]
        s_QT = pairst["QT"][p]
        s_V = pairst["V"][p]
        tnu = pairst[("tnu", p)]
        s_ds = [[A.alloc(512, F32) for _ in range(2)] for _ in range(2)]
        s_P = [[A.alloc(512, BF16) for _ in range(14)] for _ in range(2)]
        s_dh = [A.alloc(512, F32) for _ in range(2)]
        s_xc = [A.alloc(512, F32) for _ in range(2)]
        s_x2 = [A.alloc(512, F32) for _ in range(2)]
        s_ao = [A.alloc(512, BF16) for _ in range(2)]
        s_G = [A.alloc(QT, F32) for _ in range(2)]
        s_H = [A.alloc(QT, F32) for _ in range(2)]
        s_lin = [A.alloc(QT, F32) for _ in range(2)]
        s_lt = [A.alloc(QT, F32) for _ in range(2)]

        tgh = None
        for h in range(2):
            hl = 2 * p + h
            c.wait("V", tv2)
            tgt = c.inc("V", nc.vector.tensor_scalar(
                s_x2[h], s_dqb, s_ab[:, hl:hl + 1], None, ALU.mult))
            c.wait("A", tgt)
            nc.scalar.activation(s_G[h], s_x2[h], AF.Exp, scale=-1.0)
            tgh = c.inc("A", nc.scalar.activation(
                s_H[h], s_x2[h], AF.Exp, scale=1.0,
                bias=s_m511b[:, hl:hl + 1]))

        if STAGE < 2:
            A.reset(mark)
            return

        bS = [[pb[0], pb[1]], [pb[2], pb[3]]]
        bL = [pb[4], pb[5]]
        bR = [pb[6], pb[7]]
        c.join()

        NP = 14
        AVLAG = 12
        EPIPOS = 10
        RUNWAY = 6
        pend = None

        def emit_epi_V1(st):
            qt0 = st["qt"]
            hasL, hasR = qt0 > 0, qt0 < NQT - 1
            tvcs = []
            for h in range(2):
                c.wait("V", st["tAV"])
                c.wait("V", tgh)
                if hasL:
                    nc.vector.tensor_mul(s_xc[h][0:65, :], bL[h][0:65, :],
                                         s_G[h][0:65, :])
                    if hasR:
                        nc.vector.tensor_mul(s_x2[h][0:65, :], bR[h][0:65, :],
                                             s_H[h][0:65, :])
                        nc.vector.tensor_add(s_xc[h][0:65, :], s_xc[h][0:65, :],
                                             s_x2[h][0:65, :])
                else:
                    nc.vector.tensor_mul(s_xc[h][0:65, :], bR[h][0:65, :],
                                         s_H[h][0:65, :])
                tvcs.append(c.inc("V", nc.vector.tensor_add(
                    s_xc[h][0:65, :], s_xc[h][0:65, :], s_dh[h][0:65, :])))
            st["tvcs"] = tvcs
            last[("accR", 0)] = tvcs[-1]
            last[("accR", 1)] = tvcs[-1]

        def emit_epi_A(st):
            trs = []
            for h in range(2):
                c.wait("A", st["tvcs"][h])
                nc.scalar.activation(s_lt[h][0:1, :], s_xc[h][64:65, :], AF.Ln)
                trs.append(c.inc("A", nc.scalar.activation(
                    s_lin[h][0:1, :], s_lt[h][0:1, :], AF.Exp, scale=-1.0)))
            st["trs"] = trs

        def emit_epi_T(st):
            tbcs = []
            for h in range(2):
                c.wait("T", st["trs"][h])
                tbcs.append(c.mm2(nc, bL[h][0:64, :], s_onf[0:1, 0:64],
                                  s_lin[h][0:1, :], start=True, stop=True))
            st["tbcs"] = tbcs

        def emit_epi_V2(st):
            qt0 = st["qt"]
            aos = s_ao[qt0 % 2]
            tEp = None
            for h in range(2):
                c.wait("V", st["tbcs"][h])
                dep = last.get(("aost", qt0 % 2))
                if dep:
                    c.wait("V", dep)
                tEp = c.v2(lambda h=h, aos=aos: nc.vector.tensor_mul(
                    aos[64 * h:64 * h + 64, :], s_xc[h][0:64, :],
                    bL[h][0:64, :]))
                last[("accL", h)] = tEp
            gq = qt0 // 4
            tdma = c.dma("Y", f"cat{qt0 % 2}",
                         d_cat.ap()[256 * gq + 128 * p:256 * gq + 128 * p + 128,
                                    512 * (qt0 % 4):512 * (qt0 % 4) + 512],
                         aos, deps=[tEp])
            last[("aost", qt0 % 2)] = tdma
            if STAGE >= 3 and p == 1 and qt0 % 4 == 3:
                emit_collective(gq)

        for qt in range(NQT):
            tb = []
            for h in range(2):
                hl = 2 * p + h
                dep = last.get(("bsl", h))
                if dep:
                    c.wait("V", dep)
                c.wait("V", tnu)
                c.wait("V", tv2)
                tb.append(c.inc("V", nc.vector.tensor_scalar(
                    s_bsl[h], s_tq[:, 64 * qt:64 * qt + 64],
                    s_ab[:, hl:hl + 1], s_nub[:, hl:hl + 1], ALU.mult, ALU.add)))
            if pend is not None:
                emit_epi_V1(pend)

            ndiag0 = 4 * qt
            Lc = [("L", cc, None) for cc in range(0, ndiag0)]
            Rc = [("R", cc, None) for cc in range(ndiag0 + 4, NCH)]
            Dc = [("D", ndiag0 + j, j) for j in range(4)]
            if Lc:
                borrow = "R"
                chunks = Lc[:RUNWAY] + Dc + Lc[RUNWAY:] + Rc
            else:
                borrow = "L"
                chunks = Rc[:RUNWAY] + Dc + Rc[RUNWAY:]
            nch = len(chunks)
            bDA = bR if borrow == "R" else bL
            lastof = {}
            firstof = {}
            for i, (kind, cc, j) in enumerate(chunks):
                lastof[kind] = i
                firstof.setdefault(kind, i)

            def emit_S(i):
                kind, cc, j = chunks[i]
                buf = i % 2
                for h in range(2):
                    dep = last.get(("expS", h, buf))
                    if dep:
                        c.wait("T", dep)
                    ins = nc.tensor.matmul(
                        bS[h][buf],
                        s_KT[64 * h:64 * h + 64, 128 * cc:128 * cc + 128],
                        s_QT[64 * h:64 * h + 64, 512 * qt:512 * qt + 512],
                        start=True, stop=True, tile_position=(64 * h, 0))
                return c.inc("T", ins)

            def emit_mid(i, tS):
                kind, cc, j = chunks[i]
                buf = i % 2
                pslot = i % NP
                if kind == "D":
                    c.wait("V", tS)
                    for h in range(2):
                        hl = 2 * p + h
                        dep = last.get(("expD", h, j % 2))
                        if dep:
                            c.wait("V", dep)
                        ins = nc.vector.scalar_tensor_tensor(
                            s_ds[h][j % 2], s_d0[:, 512 * j:512 * j + 512],
                            s_n8ab[:, hl:hl + 1], bS[h][buf], ALU.mult, ALU.add)
                    tD = c.inc("V", ins)
                    last[("expS", 0, buf)] = tD
                    last[("expS", 1, buf)] = tD
                    c.wait("A", tD)
                    for h in range(2):
                        hl = 2 * p + h
                        dep = last.get(("avP", h, pslot))
                        if dep:
                            c.wait("A", dep)
                        ins = nc.scalar.activation(
                            s_P[h][pslot], s_ds[h][j % 2], AF.Exp, scale=0.125,
                            bias=s_nub[:, hl:hl + 1])
                    tE = c.inc("A", ins)
                    last[("expD", 0, j % 2)] = tE
                    last[("expD", 1, j % 2)] = tE
                else:
                    c.wait("A", tS)
                    c.wait("A", tb[0])
                    c.wait("A", tb[1])
                    for h in range(2):
                        dep = last.get(("avP", h, pslot))
                        if dep:
                            c.wait("A", dep)
                        ins = nc.scalar.activation(
                            s_P[h][pslot], bS[h][buf], AF.Exp, scale=0.125,
                            bias=s_bsl[h][:, cc:cc + 1])
                    tE = c.inc("A", ins)
                    last[("expS", 0, buf)] = tE
                    last[("expS", 1, buf)] = tE
                return tE

            def emit_AV(i, tE):
                kind, cc, j = chunks[i]
                pslot = i % NP
                first = (i == firstof[kind])
                lastk = (i == lastof[kind])
                bACC = bDA if kind == "D" else (bL if kind == "L" else bR)
                c.wait("T", tE)
                for h in range(2):
                    if first:
                        key = ("acc" + borrow, h) if kind == "D" else ("acc" + kind, h)
                        dep = last.get(key)
                        if dep:
                            c.wait("T", dep)
                        if kind == "R" and borrow == "R":
                            dep = last.get(("accDevac", h))
                            if dep:
                                c.wait("T", dep)
                        if kind == "L" and borrow == "L":
                            dep = last.get(("accDevac", h))
                            if dep:
                                c.wait("T", dep)
                    ins = nc.tensor.matmul(
                        bACC[h][0:65, :], s_V[h][:, 65 * cc:65 * cc + 65],
                        s_P[h][pslot], start=first, stop=lastk)
                tAV = c.inc("T", ins)
                last[("avP", 0, pslot)] = tAV
                last[("avP", 1, pslot)] = tAV
                if kind == "D" and lastk:
                    c.wait("V", tAV)
                    for h in range(2):
                        nc.vector.tensor_copy(s_dh[h][0:65, :], bDA[h][0:65, :])
                        ins = nc.vector.tensor_copy(s_dh[h][0:65, :],
                                                    bDA[h][0:65, :])
                    tDh = c.inc("V", ins)
                    last[("accDevac", 0)] = tDh
                    last[("accDevac", 1)] = tDh
                return tAV

            tS_l = {0: emit_S(0)}
            tE_l = {}
            tAV = None
            avdone = 0
            epi_T_done = pend is None
            epi_A_done = pend is None
            for i in range(nch):
                tE_l[i] = emit_mid(i, tS_l[i])
                if i + 1 < nch:
                    tS_l[i + 1] = emit_S(i + 1)
                if not epi_A_done and i >= EPIPOS - 2:
                    emit_epi_A(pend)
                    epi_A_done = True
                if not epi_T_done and i >= EPIPOS:
                    emit_epi_T(pend)
                    emit_epi_V2(pend)
                    epi_T_done = True
                navs = 0 if i < AVLAG else (2 if i >= nch - AVLAG else 1)
                for _ in range(navs):
                    if avdone <= i:
                        tAV = emit_AV(avdone, tE_l[avdone])
                        avdone += 1
            if not epi_T_done:
                if not epi_A_done:
                    emit_epi_A(pend)
                emit_epi_T(pend)
                emit_epi_V2(pend)
            while avdone < nch:
                tAV = emit_AV(avdone, tE_l[avdone])
                avdone += 1
            last[("bsl", 0)] = ("A", c.cnt["A"])
            last[("bsl", 1)] = ("A", c.cnt["A"])
            pend = {"qt": qt, "tAV": tAV}
        emit_epi_V1(pend)
        emit_epi_A(pend)
        emit_epi_T(pend)
        emit_epi_V2(pend)
        pend = None
        A.reset(mark)

    proj_all()
    attention_pair(0)
    attention_pair(1)
    if STAGE >= 4:
        c.join()
        A.reset(0)
        markB = A.mark()
        s_wo = A.alloc(IC * D, BF16)
        s_aoq = [A.alloc(IC * 512, BF16) for _ in range(2)]
        s_xoq = [A.alloc(512, F32) for _ in range(2)]
        s_u = A.alloc(IC * 512, F32)
        s_h = A.alloc(OCH * 512, BF16)
        s_h2 = A.alloc(OCH * 512, BF16)
        s_hws = [A.alloc(OCH * 128, BF16) for _ in range(3)]
        s_sqB = [A.alloc(512, F32) for _ in range(2)]
        s_mr = A.alloc(QT, F32)[0:1, :]
        s_er = A.alloc(QT, F32)[0:1, :]
        s_vr = A.alloc(QT, F32)[0:1, :]
        s_vr2 = A.alloc(QT, F32)[0:1, :]
        s_vr3 = A.alloc(QT, F32)[0:1, :]
        s_arw = A.alloc(QT, F32)[0:1, :]
        s_brw = A.alloc(QT, F32)[0:1, :]
        s_brw2 = A.alloc(QT, F32)[0:1, :]
        u2_off = (A.mark() + 31) // 32 * 32
        s_x1b = A.alloc(IC * 512, BF16)
        s_w1s = [A.alloc(IC * 128, BF16) for _ in range(3)]
        s_w2s = [A.alloc(IC * 128, BF16) for _ in range(3)]
        s_u2 = A.at(u2_off, IC * 512, F32)

        two = c.dma("G", "wo", s_wo, d_wo)
        pf = {}
        for oc in range(3):
            pf[("w1s", oc)] = c.dma("G", f"w1s{oc % 3}", s_w1s[oc % 3], d_w1[oc])
            pf[("w2s", oc)] = c.dma("G", f"w2s{oc % 3}", s_w2s[oc % 3], d_w2[oc])
            pf[("hws", oc)] = c.dma("G", f"hws{oc % 3}", s_hws[oc % 3], d_h1[oc])

        def ln_rows(bM1, bM2, tstats):
            c.wait("V", tstats)
            nc.vector.tensor_scalar(s_mr, bM1[0:1, :], 1.0 / D, None, ALU.mult)
            nc.vector.tensor_scalar(s_mr, bM1[0:1, :], 1.0 / D, None, ALU.mult)
            nc.vector.tensor_scalar(s_er, bM2[0:1, :], 1.0 / D, None, ALU.mult)
            nc.vector.tensor_scalar(s_er, bM2[0:1, :], 1.0 / D, None, ALU.mult)
            nc.vector.tensor_mul(s_vr2, s_mr, s_mr)
            nc.vector.tensor_mul(s_vr2, s_mr, s_mr)
            nc.vector.tensor_sub(s_vr, s_er, s_vr2)
            tvv_ = c.inc("V", nc.vector.tensor_sub(s_vr, s_er, s_vr2))
            c.wait("A", tvv_)
            nc.scalar.activation(s_vr3, s_vr, AF.Ln, bias=s_eps)
            tra = c.inc("A", nc.scalar.activation(s_arw, s_vr3, AF.Exp, scale=-0.5))
            c.wait("V", tra)
            nc.vector.tensor_mul(s_brw, s_mr, s_arw)
            nc.vector.tensor_scalar_mul(s_brw2, s_brw, -1.0)
            return c.inc("V", nc.vector.tensor_scalar_mul(s_brw2, s_brw, -1.0))

        for g in range(4):
            asl = g % 2
            depao = last.get(("aoq", asl))
            aodeps = [depao] if depao else []
            if cc_toks:
                aodeps.append(cc_toks[g])
            tao = c.dma("G", f"aoq{asl}",
                        s_aoq[asl].rearrange("p (a t) -> p a t", t=512),
                        d_allq[g].ap().rearrange("(a p) t -> p a t", p=128)
                        [:, :, ds(vhg * 512, 512)],
                        deps=aodeps)
            tu = None
            tus = {}
            ubanks = [pb[0], pb[1], pb[4], pb[5]]
            for oc in range(IC):
                sl = oc % 2
                dep = last.get(("xoq", sl))
                txo = c.dma("Y", f"xoq{sl}", s_xoq[sl],
                            d_xo.rearrange("p (a t) -> p a t", t=2048)
                            [:, oc, 512 * g:512 * g + 512],
                            deps=[dep] if dep else [])
                bU = ubanks[oc % 4]
                c.wait("T", tao)
                c.wait("T", two)
                dep = last.get(("bU", oc % 4))
                if dep:
                    c.wait("T", dep)
                for ic in range(IC):
                    ins = nc.tensor.matmul(
                        bU, s_wo[:, 1024 * ic + 128 * oc:1024 * ic + 128 * oc + 128],
                        s_aoq[asl][:, 512 * ic:512 * ic + 512],
                        start=(ic == 0), stop=(ic == IC - 1))
                tm = c.inc("T", ins)
                c.wait("V", tm)
                c.wait("V", txo)
                tu = c.inc("V", nc.vector.scalar_tensor_tensor(
                    s_u[:, 512 * oc:512 * oc + 512], bU, s_bo[:, oc:oc + 1],
                    s_xoq[sl], ALU.add, ALU.add))
                tus[oc] = tu
                last[("bU", oc % 4)] = tu
                last[("xoq", sl)] = tu
            last[("aoq", asl)] = tm
            tst = None
            for oc in range(IC):
                dep = last.get(("sqB", oc % 2))
                if dep:
                    c.wait("A", dep)
                c.wait("A", tus[oc])
                tsq = c.inc("A", nc.scalar.activation(
                    s_sqB[oc % 2], s_u[:, 512 * oc:512 * oc + 512], AF.Square))
                c.wait("T", tus[oc])
                c.wait("T", tsq)
                if oc == 0:
                    dep = last.get(("pbst",))
                    if dep:
                        c.wait("T", dep)
                nc.tensor.matmul(pb[2][0:1, :], s_onc,
                                 s_u[:, 512 * oc:512 * oc + 512],
                                 start=(oc == 0), stop=(oc == IC - 1))
                tst = c.inc("T", nc.tensor.matmul(
                    pb[3][0:1, :], s_onc, s_sqB[oc % 2],
                    start=(oc == 0), stop=(oc == IC - 1)))
                last[("sqB", oc % 2)] = tst
            trows = ln_rows(pb[2], pb[3], tst)
            c.wait("T", trows)
            nc.tensor.matmul(pb[2], s_onf, s_arw, start=True, stop=True)
            nc.tensor.matmul(pb[2], s_onf, s_arw, start=True, stop=True)
            tbc = c.mm2(nc, pb[3], s_onf, s_brw2, start=True, stop=True)
            tx1 = None
            for oc in range(IC):
                c.wait("V", tbc)
                nc.vector.tensor_mul(s_sqB[oc % 2],
                                     s_u[:, 512 * oc:512 * oc + 512], pb[2])
                nc.vector.tensor_add(s_u[:, 512 * oc:512 * oc + 512],
                                     s_sqB[oc % 2], pb[3])
                tx1 = c.inc("V", nc.vector.tensor_copy(
                    s_x1b[:, 512 * oc:512 * oc + 512],
                    s_u[:, 512 * oc:512 * oc + 512]))
            last[("pbst",)] = tx1

            if "dbg_x1" in dbg_outs and g == 0:
                c.dma("Y", "dbg", dbg_outs["dbg_x1"], s_u, deps=[tx1])
            if "dbg_ao" in dbg_outs and g == 0:
                c.dma("Y", "dbg", dbg_outs["dbg_ao"], s_aoq, deps=[tao])

            th = None
            for oc in range(OCH):
                sl = oc % 3
                if ("w1s", oc) in pf and g == 0:
                    t1 = pf[("w1s", oc)]
                    t2 = pf[("w2s", oc)]
                else:
                    dep = last.get(("w1s", sl))
                    t1 = c.dma("G", f"w1s{sl}", s_w1s[sl], d_w1[oc],
                               deps=[dep] if dep else [])
                    dep = last.get(("w2s", sl))
                    t2 = c.dma("G", f"w2s{sl}", s_w2s[sl], d_w2[oc],
                               deps=[dep] if dep else [])
                b1 = pb[4 + (oc % 2) * 2]
                b2 = pb[5 + (oc % 2) * 2]
                c.wait("T", t1)
                c.wait("T", tx1)
                dep = last.get(("b1", oc % 2))
                if dep:
                    c.wait("T", dep)
                if oc < 2:
                    dep = last.get(("bU", 2 + oc % 2))
                    if dep:
                        c.wait("T", dep)
                for ic in range(IC):
                    ins = nc.tensor.matmul(
                        b1, s_w1s[sl][:, 128 * ic:128 * ic + 128],
                        s_x1b[:, 512 * ic:512 * ic + 512],
                        start=(ic == 0), stop=(ic == IC - 1))
                tm1 = c.inc("T", ins)
                c.wait("T", t2)
                for ic in range(IC):
                    ins = nc.tensor.matmul(
                        b2, s_w2s[sl][:, 128 * ic:128 * ic + 128],
                        s_x1b[:, 512 * ic:512 * ic + 512],
                        start=(ic == 0), stop=(ic == IC - 1))
                tm2 = c.inc("T", ins)
                c.wait("A", tm1)
                tsi = c.inc("A", nc.scalar.activation(
                    s_h[:, 512 * oc:512 * oc + 512], b1, AF.Silu,
                    scale=s_s1[:, oc:oc + 1]))
                c.wait("V", tm2)
                c.wait("V", tsi)
                th = c.inc("V", nc.vector.scalar_tensor_tensor(
                    s_h[:, 512 * oc:512 * oc + 512], b2, s_s2[:, oc:oc + 1],
                    s_h[:, 512 * oc:512 * oc + 512], ALU.mult, ALU.mult))
                last[("b1", oc % 2)] = th
                last[("w1s", sl)] = tm1
                last[("w2s", sl)] = tm2

            def hstage(dsrc, src_ap, dst_ap, scol, noc, wait_tok, resid=False, wtag_pf=False):
                tg_ = None
                for oc in range(noc):
                    sl = oc % 3
                    if ("hws", oc) in pf and wtag_pf:
                        tw = pf.pop(("hws", oc))
                    else:
                        dep = last.get(("hws", sl))
                        tw = c.dma("G", f"hws{sl}", s_hws[sl], dsrc[oc],
                                   deps=[dep] if dep else [])
                    bk = pb[4 + oc % 2]
                    c.wait("T", tw)
                    c.wait("T", wait_tok)
                    dep = last.get(("bh", oc % 2))
                    if dep:
                        c.wait("T", dep)
                    for icp in range(OCH):
                        ins = nc.tensor.matmul(
                            bk, s_hws[sl][:, 128 * icp:128 * icp + 128],
                            src_ap[:, 512 * icp:512 * icp + 512],
                            start=(icp == 0), stop=(icp == OCH - 1))
                    tm = c.inc("T", ins)
                    if not resid:
                        c.wait("A", tm)
                        tg_ = c.inc("A", nc.scalar.activation(
                            dst_ap[:, 512 * oc:512 * oc + 512], bk, AF.Gelu,
                            scale=scol[:, oc:oc + 1]))
                    else:
                        c.wait("V", tm)
                        tg_ = c.inc("V", nc.vector.scalar_tensor_tensor(
                            dst_ap[:, 512 * oc:512 * oc + 512], bk,
                            scol[:, oc:oc + 1],
                            s_u[:, 512 * oc:512 * oc + 512],
                            ALU.mult, ALU.add))
                        dep = last.get(("sqB", oc % 2))
                        if dep:
                            c.wait("A", dep)
                        c.wait("A", tg_)
                        tsq = c.inc("A", nc.scalar.activation(
                            s_sqB[oc % 2],
                            dst_ap[:, 512 * oc:512 * oc + 512], AF.Square))
                        c.wait("T", tg_)
                        c.wait("T", tsq)
                        if oc == 0:
                            dep2 = last.get(("pbst",))
                            if dep2:
                                c.wait("T", dep2)
                        nc.tensor.matmul(
                            pb[2][0:1, :], s_onc,
                            dst_ap[:, 512 * oc:512 * oc + 512],
                            start=(oc == 0), stop=(oc == noc - 1))
                        tst2 = c.inc("T", nc.tensor.matmul(
                            pb[3][0:1, :], s_onc, s_sqB[oc % 2],
                            start=(oc == 0), stop=(oc == noc - 1)))
                        last[("sqB", oc % 2)] = tst2
                        last[("lnst",)] = tst2
                    last[("bh", oc % 2)] = tg_
                    last[("hws", sl)] = tm
                return tg_

            tgel1 = hstage(d_h1, s_h, s_h2, s_s3, OCH, th, wtag_pf=(g == 0))
            tgel2 = hstage(d_h2, s_h2, s_h, s_s4, OCH, tgel1)
            tu2 = hstage(d_w3, s_h, s_u2, s_s5, IC, tgel2, resid=True)

            trows = ln_rows(pb[2], pb[3], last[("lnst",)])
            c.wait("T", trows)
            nc.tensor.matmul(pb[2], s_onf, s_arw, start=True, stop=True)
            nc.tensor.matmul(pb[2], s_onf, s_arw, start=True, stop=True)
            tbc = c.mm2(nc, pb[3], s_onf, s_brw2, start=True, stop=True)
            tfin = None
            for oc in range(IC):
                c.wait("V", tbc)
                nc.vector.tensor_mul(s_sqB[oc % 2],
                                     s_u2[:, 512 * oc:512 * oc + 512], pb[2])
                tfin = c.inc("V", nc.vector.tensor_add(
                    s_u2[:, 512 * oc:512 * oc + 512],
                    s_sqB[oc % 2], pb[3]))
            last[("pbst",)] = tfin
            tout = c.dma("Y", "out",
                         d_out.rearrange("p (a t) -> p a t", t=2048)
                         [:, :, 512 * g:512 * g + 512],
                         s_u2.rearrange("p (a t) -> p a t", t=512),
                         deps=[tfin])
            c.wait("V", tout)
            for _sl in range(3):
                last[("w1s", _sl)] = tout
                last[("w2s", _sl)] = tout
        A.reset(markB)
    else:
        c.wait("Y", tv0)
        c.dma("Y", "out", d_out[:, 0:1024], s_tq[:, 0:1024], deps=[tok_cst])

    c.wait("Y", ("out", c.cnt.get("out", 0)))
    if 2 <= STAGE < 3:
        c.wait("Y", ("cat0", c.cnt.get("cat0", 0)))
        c.wait("Y", ("cat1", c.cnt.get("cat1", 0)))
    if dbg_outs:
        c.wait("Y", ("dbg", c.cnt.get("dbg", 0)))
        c.wait("G", ("dbg", c.cnt.get("dbg", 0)))
    return nc


_CACHE = {}


def _host_tables():
    tq = np.zeros((NQT, 128, NCH), np.float32)
    dj = np.arange(128).reshape(128, 1)
    cc = np.arange(NCH).reshape(1, NCH)
    j = 128 * cc + dj
    for qt in range(NQT):
        i0 = 512 * qt
        i1 = i0 + 511
        tq[qt] = np.where(cc < 4 * qt, j - i0,
                          np.where(cc >= 4 * qt + 4, i1 - j, 0.0))
    tqp = np.ascontiguousarray(tq.transpose(1, 0, 2).reshape(128, NQT * NCH))
    d0 = np.zeros((4, 128, QT), np.float32)
    di = np.arange(QT).reshape(1, QT)
    for cp in range(4):
        d0[cp] = np.abs(128 * cp + np.arange(128).reshape(128, 1) - di)
    d0p = np.ascontiguousarray(d0.transpose(1, 0, 2).reshape(128, 4 * QT))
    return tqp, d0p


def _pcol(v, p=128):
    v = np.asarray(v, np.float32).reshape(-1)
    return np.ascontiguousarray(v.reshape(-1, p).T)


def _wblocks(wt, och):
    I, O = wt.shape
    icn = I // 128
    r = wt.reshape(icn, 128, och, 128).transpose(2, 1, 0, 3)
    return np.ascontiguousarray(r.reshape(och, 128, icn * 128))


def _prows(wt):
    I, O = wt.shape
    icn = I // 128
    return np.ascontiguousarray(
        wt.reshape(icn, 128, O).transpose(1, 0, 2).reshape(128, icn * O))


def kernel(L0_attn_x, L0_attn_qkv_w, L0_attn_qkv_b, L0_attn_out_w, L0_attn_out_b,
           L0_attn_decay_alpha, L0_ffn_w1_w, L0_ffn_w1_s, L0_ffn_w2_w, L0_ffn_w2_s,
           L0_ffn_h1_w, L0_ffn_h1_s, L0_ffn_h2_w, L0_ffn_h2_s, L0_ffn_w3_w,
           L0_ffn_w3_s, _debug=(), _stage=None):
    global STAGE
    if _stage is not None:
        STAGE = _stage
    x = np.asarray(L0_attn_x, np.float32)
    qkv_w = np.asarray(L0_attn_qkv_w, np.float32)
    qkv_b = np.asarray(L0_attn_qkv_b, np.float32)
    out_w = np.asarray(L0_attn_out_w, np.float32)
    out_b = np.asarray(L0_attn_out_b, np.float32)
    alpha = np.asarray(L0_attn_decay_alpha, np.float32)

    key = ("nc", STAGE, tuple(sorted(d[0] for d in _debug)))
    if key not in _CACHE:
        _CACHE[key] = build_nc(debug=_debug)
    nc = _CACHE[key]

    tqp, d0p = _host_tables()
    shared = {
        "wop": _prows(np.ascontiguousarray(out_w.T)),
        "bop": _pcol(out_b),
        "w1p": _wblocks(np.ascontiguousarray(np.asarray(L0_ffn_w1_w, np.float32).T), OCH),
        "w2p": _wblocks(np.ascontiguousarray(np.asarray(L0_ffn_w2_w, np.float32).T), OCH),
        "h1p": _wblocks(np.ascontiguousarray(np.asarray(L0_ffn_h1_w, np.float32).T), OCH),
        "h2p": _wblocks(np.ascontiguousarray(np.asarray(L0_ffn_h2_w, np.float32).T), OCH),
        "w3p": _wblocks(np.ascontiguousarray(np.asarray(L0_ffn_w3_w, np.float32).T), 8),
        "s1p": _pcol(L0_ffn_w1_s), "s2p": _pcol(L0_ffn_w2_s),
        "s3p": _pcol(L0_ffn_h1_s), "s4p": _pcol(L0_ffn_h2_s),
        "s5p": _pcol(L0_ffn_w3_s),
        "tqp": tqp, "d0p": d0p,
        "dqp": np.arange(QT, dtype=np.float32).reshape(1, QT),
        "onp": np.ones((1, 128), np.float32),
        "dqbp": np.tile(np.arange(QT, dtype=np.float32), (128, 1)),
        "eyp": np.concatenate([np.eye(2, dtype=np.float32), np.zeros((2, 510), np.float32)], axis=1),
    }
    xtp = [_prows(np.ascontiguousarray(x[b].T)) for b in range(2)]

    in_maps = []
    for core in range(8):
        b, hg = core // 4, core % 4
        xo = np.concatenate([x[b, 2048 * g + 512 * hg:2048 * g + 512 * hg + 512]
                             for g in range(4)], axis=0)
        m = dict(shared)
        m["xtp"] = xtp[b]
        m["xtop"] = _prows(np.ascontiguousarray(xo.T))
        m["wqp"] = _prows(np.ascontiguousarray(
            qkv_w[256 * hg:256 * hg + 256, :].T))
        m["wkp"] = _prows(np.ascontiguousarray(
            qkv_w[1024 + 256 * hg:1024 + 256 * hg + 256, :].T))
        m["wvp"] = _prows(np.ascontiguousarray(
            qkv_w[2048 + 256 * hg:2048 + 256 * hg + 256, :].T))
        bq = qkv_b[256 * hg:256 * hg + 256].reshape(2, 128)
        bk = qkv_b[1024 + 256 * hg:1024 + 256 * hg + 256].reshape(2, 128)
        m["bqkp"] = np.ascontiguousarray(np.stack(
            [bq[0], bq[1], bk[0], bk[1]], axis=1))
        m["bvp"] = np.ascontiguousarray(
            qkv_b[2048 + 256 * hg:2048 + 256 * hg + 256].reshape(1, 256))
        m["alp"] = np.ascontiguousarray(alpha[4 * hg:4 * hg + 4].reshape(1, 4))
        m["hgp"] = np.array([[hg]], np.int32)
        in_maps.append(m)

    res = run_bass_kernel_spmd(nc, in_maps, core_ids=list(range(8)))

    out = np.empty((2, 8192, 1024), np.float32)
    for core in range(8):
        b, hg = core // 4, core % 4
        o = np.asarray(res.results[core]["outp"])
        o = o.reshape(128, 8, 2048).transpose(1, 0, 2).reshape(1024, 2048)
        for g in range(4):
            out[b, 2048 * g + 512 * hg:2048 * g + 512 * hg + 512, :] = \
                o[:, 512 * g:512 * g + 512].T
    if _debug:
        return out, [dict(r) for r in res.results]
    return out
